# version 06
# baseline (speedup 1.0000x reference)
import math
from contextlib import ExitStack

import numpy as np
import ml_dtypes

import concourse.bass as bass
import concourse.mybir as mybir
from concourse.bass_utils import run_bass_kernel_spmd

F32 = mybir.dt.float32
BF16 = mybir.dt.bfloat16
I32 = mybir.dt.int32
U8 = mybir.dt.uint8
AF = mybir.ActivationFunctionType
ALU = mybir.AluOpType
AX = mybir.AxisListType

S = 4096
H = 2048
I = 8192
NT = S // 128
NH = H // 128
NI = I // 128
GAMMA = 0.96875
SCALE_BASE = 512
KVC = 128
HEADS = 16
GROUPS = 4

BF = ml_dtypes.bfloat16


def _hadamard(n):
    h = np.array([[1.0]])
    while h.shape[0] < n:
        h = np.block([[h, h], [h, -h]])
    return h


def build_consts():
    c = {}
    c["ident"] = np.eye(128, dtype=np.float32)
    c["h128"] = _hadamard(128).astype(np.float32)
    c["h16x8"] = np.kron(np.eye(8), _hadamard(16)).astype(np.float32)
    c["h64x2"] = np.kron(np.eye(2), _hadamard(64)).astype(np.float32)
    c["utmask"] = np.triu(np.ones((128, 128), dtype=np.float32))
    d = H
    half = d // 2
    hi = np.arange(half, dtype=np.float64)
    n = np.arange(S, dtype=np.float64)
    scale_vec = (2.0 * hi + 0.4 * d) / (1.4 * d)
    power = n / SCALE_BASE
    scale = scale_vec[None, :] ** power[:, None]
    inv_freq = 1.0 / 10000.0 ** (hi / half)
    sinus = n[:, None] * inv_freq[None, :]
    sin, cos = np.sin(sinus), np.cos(sinus)
    dq = GAMMA ** (n % 128)
    dk = GAMMA ** (-(n % 128))
    c["qcos"] = (cos * scale * dq[:, None]).T.astype(np.float32)
    c["qsin"] = (sin * scale * dq[:, None]).T.astype(np.float32)
    c["kcos"] = (cos / scale * dk[:, None]).T.astype(np.float32)
    c["ksin"] = (sin / scale * dk[:, None]).T.astype(np.float32)
    c["cdec"] = (GAMMA ** (128.0 * np.arange(NT, dtype=np.float64))).astype(np.float32)
    return c


def _swiz(wt, ntile):
    R, C = wt.shape
    assert R == ntile * 128
    return np.ascontiguousarray(
        wt.reshape(ntile, 128, C).transpose(1, 0, 2).reshape(R, C))


def _perm_eo(w):
    return np.ascontiguousarray(np.concatenate([w[:, 0::2], w[:, 1::2]], axis=1))


def host_prep(inputs):
    p = {}
    asc = np.ascontiguousarray
    p["wqp"] = _perm_eo(inputs["L0_attn_W_Q"].astype(np.float32, copy=False))
    p["wkp"] = _perm_eo(inputs["L0_attn_W_K"].astype(np.float32, copy=False))
    p["wv"] = asc(inputs["L0_attn_W_V"].astype(np.float32, copy=False))
    p["wupts"] = _swiz(asc(inputs["L0_ffn_W_up"].T), 16)
    p["wgatets"] = _swiz(asc(inputs["L0_ffn_W_gate"].T), 16)
    p["wdnts"] = _swiz(asc(inputs["L0_ffn_W_down"].T), 64)
    col = lambda v, nt: asc(v.reshape(nt, 128).T.astype(np.float32))
    p["su_ug"] = col(inputs["L0_ffn_SU_ug"], 16)
    p["su_dp"] = col(inputs["L0_ffn_SU_dp"], 64)
    p["wsc_up"] = col(inputs["L0_ffn_Wscale_ug"][:I], 64)
    p["wsc_gate"] = col(inputs["L0_ffn_Wscale_ug"][I:], 64)
    p["wsc_dp"] = asc(inputs["L0_ffn_Wscale_dp"].reshape(1, H).astype(np.float32))
    p["wq1"] = asc(inputs["L1_attn_w_q"].astype(np.float32, copy=False))
    p["wkv1"] = asc(inputs["L1_attn_w_kv"].astype(np.float32, copy=False))
    p["wdense1"] = asc(inputs["L1_attn_w_dense"].astype(np.float32, copy=False))
    def lowbyte(a):
        a = asc(a)
        return asc(a.view(np.uint8).reshape(a.shape[0], a.shape[1], 4)[:, :, 0])
    p["upt1"] = lowbyte(inputs["L1_ffn_up_w"].T)
    p["gatet1"] = lowbyte(inputs["L1_ffn_gate_w"].T)
    p["dnt1"] = lowbyte(inputs["L1_ffn_down_w"].T)
    p["sc_up1"] = col(inputs["L1_ffn_up_scale"], 64)
    p["sc_gate1"] = col(inputs["L1_ffn_gate_scale"], 64)
    p["sc_dn1"] = asc(inputs["L1_ffn_down_scale"].reshape(1, H).astype(np.float32))
    return p



class Prog:

    def __init__(self, nc):
        self.nc = nc
        self.ctx = ExitStack()
        self._n = 0
        self.dram = {}

    def name(self, p):
        self._n += 1
        return f"{p}{self._n}"

    def sbuf(self, shape, dt, p="sb", ctx=None):
        return (ctx or self.ctx).enter_context(self.nc.sbuf_tensor(self.name(p), shape, dt))

    def psum(self, shape, dt, p="ps", ctx=None):
        return (ctx or self.ctx).enter_context(self.nc.psum_tensor(self.name(p), shape, dt))

    def sem(self, p="sem", ctx=None):
        return (ctx or self.ctx).enter_context(self.nc.semaphore(self.name(p)))

    _CHAIN_OPS = ("tensor_copy", "tensor_tensor", "tensor_add", "tensor_sub",
                  "tensor_mul", "tensor_max", "tensor_scalar", "tensor_single_scalar",
                  "scalar_tensor_tensor", "tensor_reduce", "reduce_sum", "reduce_max",
                  "reciprocal", "memset", "select", "activation", "copy", "add",
                  "mul", "square", "sqrt", "sign", "iota")

    def chained(self, eng, cname):
        prog = self
        c = self.ctr(cname)

        class _W:
            def __getattr__(self, k):
                f = getattr(eng, k)
                if k in prog._CHAIN_OPS:
                    def g(*a, **kw):
                        eng.wait_ge(c.h, c.b)
                        ins = f(*a, **kw)
                        ins.then_inc(c.h, 1)
                        c.b += 1
                        return ins
                    return g
                return f
        w = _W()
        w.c = c
        return w

    def dmaN(self, eng, ctrh, dst, src):
        n0 = self.nc.n_instructions()
        ins = eng.dma_start(dst, src)
        parts = self.nc.n_instructions() - n0
        ins.then_inc(ctrh, 16)
        return 16 * parts

    def ctr(self, name):
        if not hasattr(self, "_ctrs"):
            self._ctrs = {}
        if name not in self._ctrs:
            h = self.ctx.enter_context(self.nc.semaphore("c_" + name))
            c = type("Ctr", (), {})()
            c.h, c.b = h, 0
            self._ctrs[name] = c
        return self._ctrs[name]

    def dram_in(self, name, shape, dt):
        self.dram[name] = self.nc.dram_tensor(name, list(shape), dt, kind="ExternalInput").ap()
        return self.dram[name]

    def dram_out(self, name, shape, dt):
        self.dram[name] = self.nc.dram_tensor(name, list(shape), dt, kind="ExternalOutput").ap()
        return self.dram[name]

    def dram_tmp(self, name, shape, dt):
        self.dram[name] = self.nc.dram_tensor(name, list(shape), dt, kind="Internal").ap()
        return self.dram[name]

    def setup_consts(self):
        nc = self.nc
        d = self.dram
        self.identb = self.sbuf([128, 128], BF16, "identb")
        self.h128b = self.sbuf([128, 128], BF16, "h128b")
        self.h16x8b = self.sbuf([128, 128], BF16, "h16x8b")
        self.h64x2b = self.sbuf([128, 128], BF16, "h64x2b")
        self.utmaskb = self.sbuf([128, 128], BF16, "utmaskb")
        self.onescol = self.sbuf([128, 1], BF16, "onescol")
        self.onesrow = self.sbuf([1, 128], F32, "onesrow")
        self.epscol = self.sbuf([128, 1], F32, "epscol")
        self.m128col = self.sbuf([128, 1], F32, "m128col")
        names = ["ident", "h128", "h16x8", "h64x2", "utmask"]
        dsts = [self.identb, self.h128b, self.h16x8b, self.h64x2b, self.utmaskb]
        pc = ExitStack()
        tmp = self.sbuf([128, 5 * 128], F32, "ctmp", ctx=pc)
        ld = self.ctr("ld0")
        with nc.Block() as blk:
            @blk.sync
            def _(sync):
                for i, n in enumerate(names):
                    sync.dma_start(tmp[:, 128 * i:128 * (i + 1)], d[n][:]).then_inc(ld.h, 16)
                sync.wait_ge(ld.h, ld.b + 80)

            @blk.vector
            def _(v):
                v = self.chained(v, "vch")
                v.wait_ge(ld.h, ld.b + 80)
                for i, dst in enumerate(dsts):
                    v.tensor_copy(dst[:], tmp[:, 128 * i:128 * (i + 1)])
                v.memset(self.onescol[:], 1.0)
                v.memset(self.onesrow[:], 1.0)
                v.memset(self.epscol[:], 1e-5)
                v.memset(self.m128col[:], -128.0)
        self.nc.const_aps.register(F32, -128.0, self.m128col[:])
        ld.b += 80
        pc.close()

    def ph_ln(self, x_name):
        nc = self.nc
        x = self.dram[x_name]
        xnT = self.dram["xnT"]
        pc = ExitStack()
        xb = [self.sbuf([128, H], F32, ctx=pc) for _ in range(2)]
        sq = [self.sbuf([128, H], F32, ctx=pc) for _ in range(2)]
        xnb = [self.sbuf([128, H], BF16, ctx=pc) for _ in range(2)]
        stat = [self.sbuf([128, 8], F32, ctx=pc) for _ in range(2)]
        stg = [self.sbuf([128, 2048], BF16, ctx=pc) for _ in range(2)]
        ptb = [self.psum([128, 1024], BF16, ctx=pc) for _ in range(4)]
        ld = [self.ctr("ld0"), self.ctr("ld1")]
        st = [self.ctr("st0"), self.ctr("st1")]
        tp = self.ctr("tp")
        vc, sc = self.ctr("vch"), self.ctr("sch")
        vcb, scb, tpb = vc.b, sc.b, tp.b
        with nc.Block() as blk:
            @blk.sync
            def _(sync):
                for i in range(NT):
                    if i >= 2:
                        sync.wait_ge(vc.h, vcb + 8 * (i - 1))
                    sync.dma_start(xb[i % 2][:], x[128 * i:128 * (i + 1), :]).then_inc(ld[i % 2].h, 16)
                sync.wait_ge(ld[0].h, ld[0].b + 16 * (NT // 2))
                sync.wait_ge(ld[1].h, ld[1].b + 16 * (NT // 2))

            @blk.vector
            def _(v):
                v = self.chained(v, "vch")
                for i in range(NT):
                    b = i % 2
                    v.wait_ge(ld[b].h, ld[b].b + 16 * (i // 2 + 1))
                    v.tensor_tensor(sq[b][:], xb[b][:], xb[b][:], ALU.mult)
                    v.reduce_sum(stat[b][:, 0:1], xb[b][:], axis=AX.X)
                    v.reduce_sum(stat[b][:, 1:2], sq[b][:], axis=AX.X)
                    v.tensor_scalar(stat[b][:, 2:3], stat[b][:, 0:1], -1.0 / H, None, ALU.mult)
                    v.tensor_tensor(stat[b][:, 3:4], stat[b][:, 2:3], stat[b][:, 2:3], ALU.mult)
                    v.scalar_tensor_tensor(stat[b][:, 4:5], stat[b][:, 1:2], 1.0 / H,
                                           stat[b][:, 3:4], ALU.mult, ALU.subtract)
                    v.wait_ge(sc.h, scb + 3 * i + 1)
                    v.reciprocal(stat[b][:, 6:7], stat[b][:, 5:6])
                    if i >= 2:
                        v.wait_ge(tp.h, tpb + 2 * (i - 1))
                    v.tensor_scalar(xnb[b][:], xb[b][:], stat[b][:, 2:3], stat[b][:, 6:7],
                                    ALU.add, ALU.mult)

            @blk.scalar
            def _(s):
                s = self.chained(s, "sch")
                for i in range(NT):
                    b = i % 2
                    s.wait_ge(vc.h, vcb + 8 * i + 6)
                    s.activation(stat[b][:, 5:6], stat[b][:, 4:5], AF.Sqrt, bias=self.epscol[:])
                    s.wait_ge(tp.h, tpb + 2 * i + 1)
                    if i >= 2:
                        s.wait_ge(st[b].h, st[b].b + 256 * (i // 2))
                    s.copy(stg[b][:, 0:1024], ptb[(2 * i) % 4][:])
                    s.wait_ge(tp.h, tpb + 2 * i + 2)
                    s.copy(stg[b][:, 1024:2048], ptb[(2 * i + 1) % 4][:])

            @blk.tensor
            def _(t):
                for i in range(NT):
                    b = i % 2
                    t.wait_ge(vc.h, vcb + 8 * (i + 1))
                    if i >= 2:
                        t.wait_ge(sc.h, scb + 3 * (i - 1))
                    for half in range(2):
                        p = ptb[(2 * i + half) % 4]
                        for c in range(8):
                            cc = 8 * half + c
                            ins = t.transpose(p[:, 128 * c:128 * (c + 1)],
                                              xnb[b][:, 128 * cc:128 * (cc + 1)], self.identb[:])
                        ins.then_inc(tp.h, 1)

            @blk.gpsimd
            def _(g):
                for i in range(NT):
                    b = i % 2
                    g.wait_ge(sc.h, scb + 3 * (i + 1))
                    if i >= 2:
                        g.wait_ge(st[b].h, st[b].b + 256 * (i // 2))
                    for c in range(16):
                        g.dma_start(xnT[128 * c:128 * (c + 1), 128 * i:128 * (i + 1)],
                                    stg[b][:, 128 * c:128 * (c + 1)]).then_inc(st[b].h, 16)
                g.wait_ge(st[0].h, st[0].b + 256 * (NT // 2))
                g.wait_ge(st[1].h, st[1].b + 256 * (NT // 2))
        ld[0].b += 16 * (NT // 2)
        ld[1].b += 16 * (NT // 2)
        st[0].b += 256 * (NT // 2)
        st[1].b += 256 * (NT // 2)
        tp.b += 2 * NT
        assert vc.b == vcb + 8 * NT, (vc.b, vcb)
        assert sc.b == scb + 3 * NT, (sc.b, scb)
        pc.close()


    def ph_projN(self, w_name, out_name, E, src_name="xnT", col_off=0,
                 res_name=None, w_cols=None):
        nc = self.nc
        d = self.dram
        w, outd, srcd = d[w_name], d[out_name], d[src_name]
        C = srcd.shape[0]
        NC = C // 128
        NE = E // 512
        w_cols = w_cols if w_cols is not None else E
        pc = ExitStack()
        wsb = self.sbuf([128, NC * E], BF16, ctx=pc)
        wtmp = [self.sbuf([128, E], F32, ctx=pc) for _ in range(2)]
        lb = [self.sbuf([128, 16 * 128], BF16, ctx=pc) for _ in range(2)]
        ob = [self.sbuf([128, E], BF16 if res_name is None else F32, ctx=pc) for _ in range(2)]
        rb = [self.sbuf([128, E], F32, ctx=pc) for _ in range(2)] if res_name else None
        pj = [self.psum([128, 512], F32, ctx=pc) for _ in range(8)]
        ld = [self.ctr("ld0"), self.ctr("ld1")]
        xl = [self.ctr("ld2"), self.ctr("ld3")]
        rl = [self.ctr("ld4"), self.ctr("ld5")]
        st = [self.ctr("st0"), self.ctr("st1")]
        mm = self.ctr("mm")
        vc = self.ctr("vch")
        vcb, mmb = vc.b, mm.b
        xtot = [0, 0]
        xrec = {}

        with nc.Block() as blk:
            @blk.sync
            def _(sync):
                for c in range(NC):
                    if c >= 2:
                        sync.wait_ge(vc.h, vcb + c - 1)
                    sync.dma_start(wtmp[c % 2][:, 0:w_cols],
                                   w[128 * c:128 * (c + 1), col_off:col_off + w_cols]).then_inc(ld[c % 2].h, 16)
                src3 = srcd.rearrange("(c p) n -> p c n", p=128)
                for i in range(NT):
                    if i >= 2:
                        sync.wait_ge(mm.h, mmb + (i - 1) * NE)
                    xtot[i % 2] += self.dmaN(sync, xl[i % 2].h,
                                             lb[i % 2][:].rearrange("p (c n) -> p c n", n=128),
                                             src3[:, :, 128 * i:128 * (i + 1)])
                    xrec[i] = xtot[i % 2]
                sync.wait_ge(xl[0].h, xl[0].b + xtot[0])
                sync.wait_ge(xl[1].h, xl[1].b + xtot[1])
                sync.wait_ge(ld[0].h, ld[0].b + 16 * ((NC + 1) // 2))
                sync.wait_ge(ld[1].h, ld[1].b + 16 * (NC // 2))

            @blk.scalar
            def _(s):
                if res_name is None:
                    return
                res = d[res_name]
                for i in range(NT):
                    if i >= 2:
                        s.wait_ge(vc.h, vcb + NC + NE * (i - 1))
                    s.dma_start(rb[i % 2][:], res[128 * i:128 * (i + 1), :]).then_inc(rl[i % 2].h, 16)
                s.wait_ge(rl[0].h, rl[0].b + 16 * (NT // 2))
                s.wait_ge(rl[1].h, rl[1].b + 16 * (NT // 2))

            @blk.vector
            def _(v):
                v = self.chained(v, "vch")
                for c in range(NC):
                    v.wait_ge(ld[c % 2].h, ld[c % 2].b + 16 * (c // 2 + 1))
                    v.tensor_copy(wsb[:, E * c + 0:E * c + w_cols], wtmp[c % 2][:, 0:w_cols])
                for i in range(NT):
                    b = i % 2
                    if res_name is not None:
                        v.wait_ge(rl[b].h, rl[b].b + 16 * (i // 2 + 1))
                    if i >= 2:
                        v.wait_ge(st[b].h, st[b].b + 16 * (i // 2))
                    for hc in range(NE):
                        k = i * NE + hc
                        v.wait_ge(mm.h, mmb + k + 1)
                        if res_name is None:
                            v.tensor_copy(ob[b][:, 512 * hc:512 * (hc + 1)], pj[k % 8][:])
                        else:
                            v.tensor_add(ob[b][:, 512 * hc:512 * (hc + 1)], pj[k % 8][:],
                                         rb[b][:, 512 * hc:512 * (hc + 1)])

            @blk.gpsimd
            def _(g):
                for i in range(NT):
                    b = i % 2
                    g.wait_ge(vc.h, vcb + NC + NE * (i + 1))
                    if i >= 2:
                        g.wait_ge(st[b].h, st[b].b + 16 * (i // 2))
                    g.dma_start(outd[128 * i:128 * (i + 1), :], ob[b][:]).then_inc(st[b].h, 16)
                g.wait_ge(st[0].h, st[0].b + 16 * (NT // 2))
                g.wait_ge(st[1].h, st[1].b + 16 * (NT // 2))

            @blk.tensor
            def _(t):
                t.wait_ge(vc.h, vcb + NC)
                for i in range(NT):
                    t.wait_ge(xl[i % 2].h, xl[i % 2].b + xrec[i])
                    if i >= 1:
                        t.wait_ge(vc.h, vcb + NC + NE * i)
                    for c in range(NC):
                        for hc in range(NE):
                            k = i * NE + hc
                            ins = t.matmul(pj[k % 8][:],
                                           lb[i % 2][:, 128 * c:128 * (c + 1)],
                                           wsb[:, E * c + 512 * hc:E * c + 512 * (hc + 1)],
                                           start=(c == 0), stop=(c == NC - 1))
                            if c == NC - 1:
                                ins.then_inc(mm.h, 1)

        ld[0].b += 16 * ((NC + 1) // 2)
        ld[1].b += 16 * (NC // 2)
        xl[0].b += xtot[0]
        xl[1].b += xtot[1]
        if res_name is not None:
            rl[0].b += 16 * (NT // 2)
            rl[1].b += 16 * (NT // 2)
        st[0].b += 16 * (NT // 2)
        st[1].b += 16 * (NT // 2)
        mm.b += NT * NE
        assert vc.b == vcb + NC + NE * NT
        pc.close()

    def ph_scores0(self, cdec):
        nc = self.nc
        d = self.dram
        qT, kT, psp = d["qT"], d["kT"], d["psp"]
        NST = S // 512
        pc = ExitStack()
        qts = [self.sbuf([128, 512 * NH], BF16, ctx=pc) for _ in range(2)]
        ktb = [self.sbuf([128, 16 * 128], BF16, ctx=pc) for _ in range(2)]
        ob = [self.sbuf([128, 512], BF16, ctx=pc) for _ in range(2)]
        pj = [self.psum([128, 512], F32, ctx=pc) for _ in range(2)]
        ql = [self.ctr("ld0"), self.ctr("ld1")]
        kl = [self.ctr("ld2"), self.ctr("ld3")]
        st = [self.ctr("st0"), self.ctr("st1")]
        mm = self.ctr("mm")
        vc = self.ctr("vch")
        vcb, mmb = vc.b, mm.b
        pairs = []
        for s_ in range(NST):
            for j in range(4 * s_ + 4):
                pairs.append((s_, j))
        qtot, qrec = [0, 0], {}
        ktot, krec = [0, 0], {}

        with nc.Block() as blk:
            @blk.sync
            def _(sync):
                qT3 = qT.rearrange("(c p) n -> p c n", p=128)
                for s_ in range(NST):
                    if s_ >= 2:
                        done = sum(1 for (ss2, _) in pairs if ss2 <= s_ - 1)
                        sync.wait_ge(mm.h, mmb + done)
                    qtot[s_ % 2] += self.dmaN(sync, ql[s_ % 2].h,
                                              qts[s_ % 2][:].rearrange("p (c n) -> p c n", n=512),
                                              qT3[:, :, 512 * s_:512 * (s_ + 1)])
                    qrec[s_] = qtot[s_ % 2]
                sync.wait_ge(ql[0].h, ql[0].b + qtot[0])
                sync.wait_ge(ql[1].h, ql[1].b + qtot[1])

            @blk.scalar
            def _(s):
                kT3 = kT.rearrange("(c p) n -> p c n", p=128)
                for idx, (s_, j) in enumerate(pairs):
                    if idx >= 2:
                        s.wait_ge(mm.h, mmb + idx - 1)
                    ktot[idx % 2] += self.dmaN(s, kl[idx % 2].h,
                                               ktb[idx % 2][:].rearrange("p (c n) -> p c n", n=128),
                                               kT3[:, :, 128 * j:128 * (j + 1)])
                    krec[idx] = ktot[idx % 2]
                s.wait_ge(kl[0].h, kl[0].b + ktot[0])
                s.wait_ge(kl[1].h, kl[1].b + ktot[1])

            @blk.vector
            def _(v):
                v = self.chained(v, "vch")
                for idx, (s_, j) in enumerate(pairs):
                    v.wait_ge(mm.h, mmb + idx + 1)
                    b = idx % 2
                    if idx >= 2:
                        v.wait_ge(st[b].h, st[b].b + 16 * (idx // 2))
                    for qc in range(4):
                        it = 4 * s_ + qc
                        dst = ob[b][:, 128 * qc:128 * (qc + 1)]
                        ps = pj[b][:, 128 * qc:128 * (qc + 1)]
                        if j > it:
                            v.memset(dst, 0.0)
                        elif j == it:
                            v.tensor_tensor(dst, ps, self.utmaskb[:], ALU.mult)
                        else:
                            v.tensor_scalar(dst, ps, float(cdec[it - j]), None, ALU.mult)

            @blk.gpsimd
            def _(g):
                for idx, (s_, j) in enumerate(pairs):
                    b = idx % 2
                    g.wait_ge(vc.h, vcb + 4 * (idx + 1))
                    if idx >= 2:
                        g.wait_ge(st[b].h, st[b].b + 16 * (idx // 2))
                    g.dma_start(psp[128 * j:128 * (j + 1), 512 * s_:512 * (s_ + 1)],
                                ob[b][:]).then_inc(st[b].h, 16)
                g.wait_ge(st[0].h, st[0].b + 16 * ((len(pairs) + 1) // 2))
                g.wait_ge(st[1].h, st[1].b + 16 * (len(pairs) // 2))

            @blk.tensor
            def _(t):
                for idx, (s_, j) in enumerate(pairs):
                    b = idx % 2
                    t.wait_ge(ql[s_ % 2].h, ql[s_ % 2].b + qrec[s_])
                    t.wait_ge(kl[b].h, kl[b].b + krec[idx])
                    if idx >= 2:
                        t.wait_ge(vc.h, vcb + 4 * (idx - 1))
                    for c in range(NH):
                        ins = t.matmul(pj[b][:],
                                       ktb[b][:, 128 * c:128 * (c + 1)],
                                       qts[s_ % 2][:, 512 * c:512 * (c + 1)],
                                       start=(c == 0), stop=(c == NH - 1))
                    ins.then_inc(mm.h, 1)

        npair = len(pairs)
        ql[0].b += qtot[0]
        ql[1].b += qtot[1]
        kl[0].b += ktot[0]
        kl[1].b += ktot[1]
        st[0].b += 16 * ((npair + 1) // 2)
        st[1].b += 16 * (npair // 2)
        mm.b += npair
        assert vc.b == vcb + 4 * npair
        pc.close()

    def ph_pv0(self):
        nc = self.nc
        d = self.dram
        psp, v0, x0, r1 = d["psp"], d["v0"], d["x0"], d["r1"]
        pc = ExitStack()
        vsb = self.sbuf([128, NT * 2048], BF16, ctx=pc)
        pb = [self.sbuf([128, NT * 128], BF16, ctx=pc) for _ in range(2)]
        xb = [self.sbuf([128, 2048], F32, ctx=pc) for _ in range(2)]
        ob = [self.sbuf([128, 2048], F32, ctx=pc) for _ in range(2)]
        pj = [self.psum([128, 512], F32, ctx=pc) for _ in range(8)]
        vl = self.ctr("ld0")
        plq = [self.ctr("ld2"), self.ctr("ld3")]
        rl = [self.ctr("ld4"), self.ctr("ld5")]
        st = [self.ctr("st0"), self.ctr("st1")]
        mm = self.ctr("mm")
        vc = self.ctr("vch")
        vcb, mmb = vc.b, mm.b
        steps = [(i, j) for i in range(NT) for j in range(i + 1)]
        ptot, prec = [0, 0], {}

        with nc.Block() as blk:
            @blk.sync
            def _(sync):
                for t_ in range(NT):
                    sync.dma_start(vsb[:, 2048 * t_:2048 * (t_ + 1)],
                                   v0[128 * t_:128 * (t_ + 1), :]).then_inc(vl.h, 16)
                psp3 = psp.rearrange("(j p) n -> p j n", p=128)
                for i in range(NT):
                    if i >= 2:
                        sync.wait_ge(mm.h, mmb + steps.index((i - 1, i - 1)))
                    ptot[i % 2] += self.dmaN(sync, plq[i % 2].h,
                                             pb[i % 2][:].rearrange("p (j n) -> p j n", n=128)[:, 0:i + 1, :],
                                             psp3[:, 0:i + 1, 128 * i:128 * (i + 1)])
                    prec[i] = ptot[i % 2]
                sync.wait_ge(plq[0].h, plq[0].b + ptot[0])
                sync.wait_ge(plq[1].h, plq[1].b + ptot[1])

            @blk.scalar
            def _(s):
                for i in range(NT):
                    if i >= 2:
                        s.wait_ge(vc.h, vcb + 4 * (i - 1))
                    s.dma_start(xb[i % 2][:], x0[128 * i:128 * (i + 1), :]).then_inc(rl[i % 2].h, 16)
                s.wait_ge(rl[0].h, rl[0].b + 16 * (NT // 2))
                s.wait_ge(rl[1].h, rl[1].b + 16 * (NT // 2))

            @blk.vector
            def _(v):
                v = self.chained(v, "vch")
                for i in range(NT):
                    b = i % 2
                    v.wait_ge(mm.h, mmb + sum(1 for (i2, _) in steps if i2 <= i))
                    v.wait_ge(rl[b].h, rl[b].b + 16 * (i // 2 + 1))
                    if i >= 2:
                        v.wait_ge(st[b].h, st[b].b + 16 * (i // 2))
                    for hc in range(4):
                        v.tensor_add(ob[b][:, 512 * hc:512 * (hc + 1)],
                                     pj[(i % 2) * 4 + hc][:], xb[b][:, 512 * hc:512 * (hc + 1)])

            @blk.gpsimd
            def _(g):
                for i in range(NT):
                    b = i % 2
                    g.wait_ge(vc.h, vcb + 4 * (i + 1))
                    if i >= 2:
                        g.wait_ge(st[b].h, st[b].b + 16 * (i // 2))
                    g.dma_start(r1[128 * i:128 * (i + 1), :], ob[b][:]).then_inc(st[b].h, 16)
                g.wait_ge(st[0].h, st[0].b + 16 * (NT // 2))
                g.wait_ge(st[1].h, st[1].b + 16 * (NT // 2))

            @blk.tensor
            def _(t):
                t.wait_ge(vl.h, vl.b + 16 * NT)
                for idx, (i, j) in enumerate(steps):
                    if j == 0:
                        t.wait_ge(plq[i % 2].h, plq[i % 2].b + prec[i])
                        if i >= 2:
                            t.wait_ge(vc.h, vcb + 4 * (i - 1))
                    for hc in range(4):
                        ins = t.matmul(pj[(i % 2) * 4 + hc][:],
                                       pb[i % 2][:, 128 * j:128 * (j + 1)],
                                       vsb[:, 2048 * j + 512 * hc:2048 * j + 512 * (hc + 1)],
                                       start=(j == 0), stop=(j == i))
                    ins.then_inc(mm.h, 1)

        nsteps = len(steps)
        vl.b += 16 * NT
        plq[0].b += ptot[0]
        plq[1].b += ptot[1]
        rl[0].b += 16 * (NT // 2)
        rl[1].b += 16 * (NT // 2)
        st[0].b += 16 * (NT // 2)
        st[1].b += 16 * (NT // 2)
        mm.b += nsteps
        assert vc.b == vcb + 4 * NT
        pc.close()

    def ph_projT(self, w_name, out_name, ne_tiles, xpos=None, scale=1.0, col_off=0):
        nc = self.nc
        d = self.dram
        w, outd, xnT = d[w_name], d[out_name], d["xnT"]
        E = ne_tiles * 128
        NST = S // 512
        pc = ExitStack()
        GS = 4 if NST % 4 == 0 else (2 if NST % 2 == 0 else 1)
        NXB = 2 if xpos is not None else GS
        wsb = self.sbuf([128, NH * E], BF16, ctx=pc)
        wtmp = [self.sbuf([128, E], F32, ctx=pc) for _ in range(2)]
        xts = [self.sbuf([128, 512 * NH], BF16, ctx=pc) for _ in range(NXB)]
        ob = [self.sbuf([128, 512], BF16, ctx=pc) for _ in range(4)]
        pj = [self.psum([128, 512], F32, ctx=pc) for _ in range(8)]
        ld = [self.ctr("ld0"), self.ctr("ld1")]
        xl = [self.ctr("ld2"), self.ctr("ld3"), self.ctr("ld4"), self.ctr("ld5")][:NXB]
        st = [self.ctr("st0"), self.ctr("st1"), self.ctr("st2"), self.ctr("st3")]
        mm = self.ctr("mm")
        vc, sc = self.ctr("vch"), self.ctr("sch")
        vcb, scb, mmb = vc.b, sc.b, mm.b
        if xpos is not None:
            qraw = self.sbuf([128, 512 * 16], F32, ctx=pc)
            tcos = [self.sbuf([128, 512], F32, ctx=pc) for _ in range(2)]
            tsin = [self.sbuf([128, 512], F32, ctx=pc) for _ in range(2)]
            tmp1 = self.sbuf([128, 512], F32, ctx=pc)
            tmp2 = self.sbuf([128, 512], F32, ctx=pc)
            tl = [self.ctr("tld0"), self.ctr("tld1")]
        npairs = 8 * NST
        nk = NST * ne_tiles
        xtot = [0] * NXB
        xrec = {}

        with nc.Block() as blk:
            @blk.sync
            def _(sync):
                for c in range(NH):
                    if c >= 2:
                        sync.wait_ge(vc.h, vcb + c - 1)
                    sync.dma_start(wtmp[c % 2][:],
                                   w[128 * c:128 * (c + 1), col_off:col_off + E]).then_inc(ld[c % 2].h, 16)
                xnT3 = xnT.rearrange("(c p) n -> p c n", p=128)
                for s_ in range(NST):
                    sl_ = s_ % NXB
                    if xpos is not None and s_ >= 2:
                        sync.wait_ge(mm.h, mmb + (s_ - 1) * ne_tiles)
                    if xpos is None and s_ >= GS:
                        sync.wait_ge(mm.h, mmb + (s_ // GS) * ne_tiles * GS)
                    xtot[sl_] += self.dmaN(sync, xl[sl_].h,
                                           xts[sl_][:].rearrange("p (c n) -> p c n", n=512),
                                           xnT3[:, :, 512 * s_:512 * (s_ + 1)])
                    xrec[s_] = xtot[sl_]
                    if xpos is not None:
                        qct, qst = d[xpos[0]], d[xpos[1]]
                        for p in range(8):
                            j = s_ * 8 + p
                            if j >= 2:
                                sync.wait_ge(vc.h, vcb + NH + 6 * (j - 2) + 5)
                            sync.dma_start(tcos[j % 2][:], qct[128 * p:128 * (p + 1), 512 * s_:512 * (s_ + 1)]).then_inc(tl[j % 2].h, 16)
                            sync.dma_start(tsin[j % 2][:], qst[128 * p:128 * (p + 1), 512 * s_:512 * (s_ + 1)]).then_inc(tl[j % 2].h, 16)
                for q in range(NXB):
                    sync.wait_ge(xl[q].h, xl[q].b + xtot[q])
                sync.wait_ge(ld[0].h, ld[0].b + 16 * ((NH + 1) // 2))
                sync.wait_ge(ld[1].h, ld[1].b + 16 * (NH // 2))

            @blk.vector
            def _(v):
                v = self.chained(v, "vch")
                for c in range(NH):
                    v.wait_ge(ld[c % 2].h, ld[c % 2].b + 16 * (c // 2 + 1))
                    v.tensor_copy(wsb[:, E * c:E * (c + 1)], wtmp[c % 2][:])
                if xpos is None:
                    for sg in range(NST // GS):
                        for et in range(ne_tiles):
                            for s4 in range(GS):
                                k = (sg * ne_tiles + et) * GS + s4
                                v.wait_ge(mm.h, mmb + k + 1)
                                if k >= 4:
                                    v.wait_ge(st[k % 4].h, st[k % 4].b + 16 * (k // 4))
                                v.tensor_scalar(ob[k % 4][:], pj[(et % 2) * GS + s4][:],
                                                scale, None, ALU.mult)
                else:
                    for s_ in range(NST):
                        for p in range(8):
                            j = s_ * 8 + p
                            v.wait_ge(sc.h, scb + 16 * s_ + 8 + p + 1)
                            v.wait_ge(tl[j % 2].h, tl[j % 2].b + 32 * (j // 2 + 1))
                            qe = qraw[:, 512 * p:512 * (p + 1)]
                            qo = qraw[:, 512 * (8 + p):512 * (9 + p)]
                            co, si = tcos[j % 2][:], tsin[j % 2][:]
                            if j >= 2:
                                v.wait_ge(st[j % 2].h, st[j % 2].b + 32 * (j // 2))
                            v.tensor_tensor(tmp1[:], qe, co, ALU.mult)
                            v.tensor_tensor(tmp2[:], qo, si, ALU.mult)
                            v.tensor_tensor(ob[(2 * j) % 4][:], tmp1[:], tmp2[:], ALU.subtract)
                            v.tensor_tensor(tmp1[:], qo, co, ALU.mult)
                            v.tensor_tensor(tmp2[:], qe, si, ALU.mult)
                            v.tensor_tensor(ob[(2 * j + 1) % 4][:], tmp1[:], tmp2[:], ALU.add)

            @blk.scalar
            def _(s):
                if xpos is None:
                    return
                s = self.chained(s, "sch")
                for s_ in range(NST):
                    if s_ >= 1:
                        s.wait_ge(vc.h, vcb + NH + 6 * 8 * s_)
                    for et in range(ne_tiles):
                        k = s_ * ne_tiles + et
                        s.wait_ge(mm.h, mmb + k + 1)
                        s.copy(qraw[:, 512 * et:512 * (et + 1)], pj[k % 4][:])

            @blk.gpsimd
            def _(g):
                if xpos is None:
                    for sg in range(NST // GS):
                        for et in range(ne_tiles):
                            for s4 in range(GS):
                                s_ = GS * sg + s4
                                k = (sg * ne_tiles + et) * GS + s4
                                g.wait_ge(vc.h, vcb + NH + k + 1)
                                if k >= 4:
                                    g.wait_ge(st[k % 4].h, st[k % 4].b + 16 * (k // 4))
                                g.dma_start(outd[128 * et:128 * (et + 1), 512 * s_:512 * (s_ + 1)],
                                            ob[k % 4][:]).then_inc(st[k % 4].h, 16)
                    nkk = nk * GS // GS
                    for q in range(4):
                        g.wait_ge(st[q].h, st[q].b + 16 * ((nkk - q + 3) // 4))
                else:
                    for s_ in range(NST):
                        for p in range(8):
                            j = s_ * 8 + p
                            g.wait_ge(vc.h, vcb + NH + 6 * (j + 1))
                            if j >= 2:
                                g.wait_ge(st[j % 2].h, st[j % 2].b + 32 * (j // 2))
                            g.dma_start(outd[128 * p:128 * (p + 1), 512 * s_:512 * (s_ + 1)],
                                        ob[(2 * j) % 4][:]).then_inc(st[j % 2].h, 16)
                            g.dma_start(outd[1024 + 128 * p:1024 + 128 * (p + 1), 512 * s_:512 * (s_ + 1)],
                                        ob[(2 * j + 1) % 4][:]).then_inc(st[j % 2].h, 16)
                    g.wait_ge(st[0].h, st[0].b + 32 * ((npairs + 1) // 2))
                    g.wait_ge(st[1].h, st[1].b + 32 * (npairs // 2))

            @blk.tensor
            def _(t):
                t.wait_ge(vc.h, vcb + NH)
                if xpos is not None:
                    for s_ in range(NST):
                        t.wait_ge(xl[s_ % NXB].h, xl[s_ % NXB].b + xrec[s_])
                        for et in range(ne_tiles):
                            k = s_ * ne_tiles + et
                            if k >= 4:
                                t.wait_ge(sc.h, scb + k - 3)
                            for c in range(NH):
                                ins = t.matmul(pj[k % 4][:],
                                               wsb[:, E * c + 128 * et:E * c + 128 * (et + 1)],
                                               xts[s_ % NXB][:, 512 * c:512 * (c + 1)],
                                               start=(c == 0), stop=(c == NH - 1))
                            ins.then_inc(mm.h, 1)
                else:
                    for sg in range(NST // GS):
                        for s4 in range(GS):
                            s_ = GS * sg + s4
                            t.wait_ge(xl[s_ % NXB].h, xl[s_ % NXB].b + xrec[s_])
                        for et in range(ne_tiles):
                            ke = sg * ne_tiles + et
                            if ke >= 2:
                                t.wait_ge(vc.h, vcb + NH + (ke - 2) * GS + GS)
                            for c in range(NH):
                                for s4 in range(GS):
                                    s_ = GS * sg + s4
                                    ins = t.matmul(pj[(et % 2) * GS + s4][:],
                                                   wsb[:, E * c + 128 * et:E * c + 128 * (et + 1)],
                                                   xts[s_ % NXB][:, 512 * c:512 * (c + 1)],
                                                   start=(c == 0), stop=(c == NH - 1))
                                    if c == NH - 1:
                                        ins.then_inc(mm.h, 1)

        ld[0].b += 16 * ((NH + 1) // 2)
        ld[1].b += 16 * (NH // 2)
        for q in range(NXB):
            xl[q].b += xtot[q]
        mm.b += nk
        if xpos is None:
            assert vc.b == vcb + NH + nk
            for q in range(4):
                st[q].b += 16 * ((nk - q + 3) // 4)
        else:
            assert vc.b == vcb + NH + 6 * npairs, (vc.b, vcb)
            assert sc.b == scb + nk
            st[0].b += 32 * ((npairs + 1) // 2)
            st[1].b += 32 * (npairs // 2)
            tl[0].b += 32 * ((npairs + 1) // 2)
            tl[1].b += 32 * (npairs // 2)
        pc.close()


    def ph_wprep(self, src_name, out_name, nbeta, ncols, hcross, natile,
                 su_name, mode, colrow_name=None, bounce_name="wbounce"):
        nc = self.nc
        d = self.dram
        srcd, outd, bounce = d[src_name], d[out_name], d[bounce_name]
        SBW = 2048
        nsbc = ncols // SBW
        scale = 1.0 / math.sqrt(128 * natile)
        pc = ExitStack()
        wtmp = [self.sbuf([128, SBW], F32, ctx=pc) for _ in range(4)]
        wtmpb = [self.sbuf([128, SBW], BF16, ctx=pc) for _ in range(4)]
        wbf = [self.sbuf([128, SBW], BF16, ctx=pc) for _ in range(2)]
        ob = [self.sbuf([128, SBW], BF16, ctx=pc) for _ in range(2)]
        pj = [self.psum([128, 512], F32, ctx=pc) for _ in range(8)]
        sucol = self.sbuf([128, natile], F32, ctx=pc)
        ld = [self.ctr("ld0"), self.ctr("ld1"), self.ctr("ld2"), self.ctr("ld3")]
        st = [self.ctr("st0"), self.ctr("st1")]
        sl = self.ctr("ld4")
        mm = self.ctr("mm")
        vc, sc = self.ctr("vch"), self.ctr("sch")
        if mode == "rowcol":
            colb = self.sbuf([128, ncols], F32, ctx=pc)
            crow = self.sbuf([1, ncols], F32, ctx=pc)

        def stage(srcap, dstap, lhsT, do_scale):
            bufs = wtmp if not do_scale else wtmpb
            nsb = nbeta * nsbc if not do_scale else natile * nsbc
            vcb, scb, mmb = vc.b, sc.b, mm.b
            moff = ncols // 512 if (do_scale and mode == "rowcol") else 0
            is_a = not do_scale
            vper = (1 + 4) if is_a else 4

            def vpos(k):
                return vcb + (moff if not is_a else 0) + vper * (k + 1)

            with nc.Block() as blk:
                @blk.sync
                def _(sync):
                    if do_scale and mode == "rowcol":
                        sync.dma_start(crow[:], d[colrow_name][:]).then_inc(sl.h, 16)
                    if is_a:
                        sync.dma_start(sucol[:], d[su_name][:]).then_inc(sl.h, 16)
                    for k in range(0, nsb, 2):
                        if k >= 4:
                            if is_a:
                                sync.wait_ge(vc.h, vpos(k - 4) - vper + 1)
                            else:
                                sync.wait_ge(mm.h, mmb + moff + 4 * (k - 4) + 4)
                        sync.dma_start(bufs[k % 4][:], srcap(k)).then_inc(ld[k % 4].h, 16)
                    sync.wait_ge(ld[0].h, ld[0].b + 16 * ((nsb + 3) // 4))
                    sync.wait_ge(ld[2].h, ld[2].b + 16 * ((nsb + 1) // 4))

                @blk.scalar
                def _(s):
                    for k in range(1, nsb, 2):
                        if k >= 4:
                            if is_a:
                                s.wait_ge(vc.h, vpos(k - 4) - vper + 1)
                            else:
                                s.wait_ge(mm.h, mmb + moff + 4 * (k - 4) + 4)
                        s.dma_start(bufs[k % 4][:], srcap(k)).then_inc(ld[k % 4].h, 16)
                    s.wait_ge(ld[1].h, ld[1].b + 16 * ((nsb + 2) // 4))
                    s.wait_ge(ld[3].h, ld[3].b + 16 * (nsb // 4))

                @blk.vector
                def _(v):
                    v = self.chained(v, "vch")
                    if do_scale and mode == "rowcol":
                        v.wait_ge(sl.h, sl.b + 16)
                        for ch in range(moff):
                            v.wait_ge(mm.h, mmb + ch + 1)
                            v.tensor_scalar(colb[:, 512 * ch:512 * (ch + 1)], pj[ch % 2][:],
                                            scale, None, ALU.mult)
                    for k in range(nsb):
                        a_ = k // nsbc
                        v.wait_ge(ld[k % 4].h, ld[k % 4].b + 16 * (k // 4 + 1))
                        if is_a:
                            v.tensor_copy(wbf[k % 2][:], bufs[k % 4][:])
                        if k >= 2:
                            v.wait_ge(st[k % 2].h, st[k % 2].b + 16 * (k // 2))
                        for q in range(4):
                            v.wait_ge(mm.h, mmb + moff + 4 * k + q + 1)
                            dst = ob[k % 2][:, 512 * q:512 * (q + 1)]
                            ps = pj[(k % 2) * 4 + q][:]
                            if not do_scale:
                                v.tensor_copy(dst, ps)
                            elif mode == "row":
                                v.tensor_scalar(dst, ps, sucol[:, a_:a_ + 1], scale,
                                                ALU.mult, ALU.mult)
                            else:
                                cs = colb[:, 512 * ((k % nsbc) * 4 + q):512 * ((k % nsbc) * 4 + q + 1)]
                                v.scalar_tensor_tensor(dst, ps, sucol[:, a_:a_ + 1], cs,
                                                       ALU.mult, ALU.mult)

                @blk.gpsimd
                def _(g):
                    for k in range(nsb):
                        g.wait_ge(vc.h, vpos(k))
                        if k >= 2:
                            g.wait_ge(st[k % 2].h, st[k % 2].b + 16 * (k // 2))
                        g.dma_start(dstap(k), ob[k % 2][:]).then_inc(st[k % 2].h, 16)
                    g.wait_ge(st[0].h, st[0].b + 16 * ((nsb + 1) // 2))
                    g.wait_ge(st[1].h, st[1].b + 16 * (nsb // 2))

                @blk.tensor
                def _(t):
                    if do_scale and mode == "rowcol":
                        t.wait_ge(sl.h, sl.b + 16 + (16 if is_a else 0))
                        for ch in range(moff):
                            if ch >= 2:
                                t.wait_ge(vc.h, vcb + ch - 1)
                            t.matmul(pj[ch % 2][:], self.onesrow[:],
                                     crow[:, 512 * ch:512 * (ch + 1)],
                                     start=True, stop=True).then_inc(mm.h, 1)
                        t.wait_ge(vc.h, vcb + moff)
                    if is_a:
                        t.wait_ge(sl.h, sl.b + 16)
                    for k in range(nsb):
                        if is_a:
                            t.wait_ge(vc.h, vcb + vper * k + 1)
                        else:
                            t.wait_ge(ld[k % 4].h, ld[k % 4].b + 16 * (k // 4 + 1))
                        if k >= 2:
                            t.wait_ge(vc.h, vpos(k - 2))
                        rhs = wbf[k % 2] if is_a else bufs[k % 4]
                        for q in range(4):
                            t.matmul(pj[(k % 2) * 4 + q][:], lhsT[:],
                                     rhs[:, 512 * q:512 * (q + 1)],
                                     start=True, stop=True).then_inc(mm.h, 1)

            for qq in range(4):
                ld[qq].b += 16 * ((nsb - qq + 3) // 4)
            st[0].b += 16 * ((nsb + 1) // 2)
            st[1].b += 16 * (nsb // 2)
            mm.b += moff + 4 * nsb
            sl.b += (16 if (do_scale and mode == "rowcol") else 0) + (16 if is_a else 0)
            assert vc.b == vpos(nsb - 1), (vc.b, vpos(nsb - 1))

        hx = self.h16x8b if hcross == 16 else self.h64x2b
        stage(lambda k: srcd[128 * (k // nsbc):128 * (k // nsbc + 1),
                             SBW * (k % nsbc):SBW * (k % nsbc + 1)],
              lambda k: bounce[128 * (k // nsbc):128 * (k // nsbc + 1),
                               SBW * (k % nsbc):SBW * (k % nsbc + 1)],
              hx, False)
        brd = bounce.rearrange("(p g) n -> g p n", g=natile)
        stage(lambda k: brd[k // nsbc, :, SBW * (k % nsbc):SBW * (k % nsbc + 1)],
              lambda k: outd[128 * (k // nsbc):128 * (k // nsbc + 1),
                             SBW * (k % nsbc):SBW * (k % nsbc + 1)],
              self.h128b, True)
        pc.close()

    def ph_wcast(self, src_name, out_name, nrows, ncols):
        nc = self.nc
        d = self.dram
        srcd, outd = d[src_name], d[out_name]
        CW = 2048
        nrt, nct = nrows // 128, ncols // CW
        pc = ExitStack()
        wtmp = [self.sbuf([128, CW], U8, ctx=pc) for _ in range(4)]
        ob = [self.sbuf([128, CW], BF16, ctx=pc) for _ in range(4)]
        ld = [self.ctr("ld0"), self.ctr("ld1"), self.ctr("ld2"), self.ctr("ld3")]
        st = [self.ctr("st0"), self.ctr("st1"), self.ctr("st2"), self.ctr("st3")]
        vc, sc = self.ctr("vch"), self.ctr("sch")
        vcb, scb = vc.b, sc.b
        nblk = nrt * nct

        def cpos(k):
            return (vcb if k % 2 == 0 else scb) + k // 2 + 1

        with nc.Block() as blk:
            @blk.sync
            def _(sync):
                for k in range(0, nblk, 2):
                    r, ch = k // nct, k % nct
                    if k >= 4:
                        sync.wait_ge(vc.h, vcb + (k - 4) // 2 + 1)
                    sync.dma_start(wtmp[k % 4][:],
                                   srcd[128 * r:128 * (r + 1), CW * ch:CW * (ch + 1)]).then_inc(ld[k % 4].h, 16)
                    if k >= 1:
                        kk = k - 1
                        r2, ch2 = kk // nct, kk % nct
                        sync.wait_ge(sc.h, cpos(kk))
                        if kk >= 4:
                            sync.wait_ge(st[kk % 4].h, st[kk % 4].b + 16 * (kk // 4))
                        sync.dma_start(outd[128 * r2:128 * (r2 + 1), CW * ch2:CW * (ch2 + 1)],
                                       ob[kk % 4][:]).then_inc(st[kk % 4].h, 16)
                if nblk >= 1:
                    kk = nblk - 1
                    r2, ch2 = kk // nct, kk % nct
                    sync.wait_ge(sc.h, cpos(kk))
                    if kk >= 4:
                        sync.wait_ge(st[kk % 4].h, st[kk % 4].b + 16 * (kk // 4))
                    sync.dma_start(outd[128 * r2:128 * (r2 + 1), CW * ch2:CW * (ch2 + 1)],
                                   ob[kk % 4][:]).then_inc(st[kk % 4].h, 16)
                sync.wait_ge(ld[0].h, ld[0].b + 16 * ((nblk + 3) // 4))
                sync.wait_ge(ld[2].h, ld[2].b + 16 * ((nblk + 1) // 4))
                sync.wait_ge(st[1].h, st[1].b + 16 * ((nblk + 2) // 4))
                sync.wait_ge(st[3].h, st[3].b + 16 * (nblk // 4))

            @blk.scalar
            def _(s):
                sch = self.chained(s, "sch")
                odds = list(range(1, nblk, 2))

                def dma_odd(k):
                    r, ch = k // nct, k % nct
                    s.dma_start(wtmp[k % 4][:],
                                srcd[128 * r:128 * (r + 1), CW * ch:CW * (ch + 1)]).then_inc(ld[k % 4].h, 16)
                for k in odds[:2]:
                    dma_odd(k)
                for i, k in enumerate(odds):
                    sch.wait_ge(ld[k % 4].h, ld[k % 4].b + 16 * (k // 4 + 1))
                    if k >= 4:
                        sch.wait_ge(st[k % 4].h, st[k % 4].b + 16 * (k // 4))
                    sch.activation(ob[k % 4][:], wtmp[k % 4][:], AF.Copy,
                                   bias=-128.0)
                    if i + 2 < len(odds):
                        s.wait_ge(sc.h, scb + i + 1)
                        dma_odd(odds[i + 2])
                s.wait_ge(ld[1].h, ld[1].b + 16 * ((nblk + 2) // 4))
                s.wait_ge(ld[3].h, ld[3].b + 16 * (nblk // 4))

            @blk.vector
            def _(v):
                vch = self.chained(v, "vch")
                for k in range(0, nblk, 2):
                    vch.wait_ge(ld[k % 4].h, ld[k % 4].b + 16 * (k // 4 + 1))
                    if k >= 4:
                        vch.wait_ge(st[k % 4].h, st[k % 4].b + 16 * (k // 4))
                    vch.tensor_scalar(ob[k % 4][:], wtmp[k % 4][:], 128.0, None, ALU.subtract)

            @blk.gpsimd
            def _(g):
                for k in range(0, nblk, 2):
                    r, ch = k // nct, k % nct
                    g.wait_ge(vc.h, cpos(k))
                    if k >= 4:
                        g.wait_ge(st[k % 4].h, st[k % 4].b + 16 * (k // 4))
                    g.dma_start(outd[128 * r:128 * (r + 1), CW * ch:CW * (ch + 1)],
                                ob[k % 4][:]).then_inc(st[k % 4].h, 16)
                g.wait_ge(st[0].h, st[0].b + 16 * ((nblk + 3) // 4))
                g.wait_ge(st[2].h, st[2].b + 16 * ((nblk + 1) // 4))
        for q in range(4):
            ld[q].b += 16 * ((nblk - q + 3) // 4)
            st[q].b += 16 * ((nblk - q + 3) // 4)
        assert vc.b == vcb + (nblk + 1) // 2 and sc.b == scb + nblk // 2
        pc.close()

    def ph_ffn_up(self, hup_name, hgate_name, scu_name, scg_name):
        nc = self.nc
        d = self.dram
        hup, hgate, t0, xnT = d[hup_name], d[hgate_name], d["t0"], d["xnT"]
        pc = ExitStack()
        TPH = S // 2
        NSTH = TPH // 512
        NSUP = 16
        xth = self.sbuf([128, NH * TPH], BF16, ctx=pc)
        wub = [self.sbuf([128, NH * 512], BF16, ctx=pc) for _ in range(2)]
        wgb = [self.sbuf([128, NH * 512], BF16, ctx=pc) for _ in range(2)]
        scu = self.sbuf([128, NI], F32, ctx=pc)
        scg = self.sbuf([128, NI], F32, ctx=pc)
        gs = [self.sbuf([128, 512], F32, ctx=pc) for _ in range(2)]
        ob = [self.sbuf([128, 512], BF16, ctx=pc) for _ in range(2)]
        pu = [self.psum([128, 512], F32, ctx=pc) for _ in range(4)]
        pg = [self.psum([128, 512], F32, ctx=pc) for _ in range(4)]
        xl = self.ctr("ld0")
        sl = self.ctr("ld1")
        wl = [self.ctr("ld2"), self.ctr("ld3")]
        st = [self.ctr("st0"), self.ctr("st1")]
        mm = self.ctr("mm")
        vc, sc = self.ctr("vch"), self.ctr("sch")
        vcb, scb, mmb = vc.b, sc.b, mm.b
        wtot, wrec = [0, 0], {}
        NIT = 2 * NSUP * 4
        KPH = NIT * NSTH

        with nc.Block() as blk:
            @blk.sync
            def _(sync):
                hup3 = hup.rearrange("(c p) n -> p c n", p=128)
                hgate3 = hgate.rearrange("(c p) n -> p c n", p=128)
                sync.dma_start(scu[:], d[scu_name][:]).then_inc(sl.h, 16)
                sync.dma_start(scg[:], d[scg_name][:]).then_inc(sl.h, 16)
                for half in range(2):
                    if half == 1:
                        sync.wait_ge(mm.h, mmb + NIT // 2)
                    for c in range(NH):
                        sync.dma_start(xth[:, TPH * c:TPH * (c + 1)],
                                       xnT[128 * c:128 * (c + 1), TPH * half:TPH * (half + 1)]).then_inc(xl.h, 16)
                    for isup in range(NSUP):
                        g_ = half * NSUP + isup
                        if g_ >= 2:
                            sync.wait_ge(mm.h, mmb + (g_ - 1) * 4)
                        wtot[isup % 2] += self.dmaN(sync, wl[isup % 2].h,
                                                    wub[isup % 2][:].rearrange("p (c n) -> p c n", n=512),
                                                    hup3[:, :, 512 * isup:512 * (isup + 1)])
                        wtot[isup % 2] += self.dmaN(sync, wl[isup % 2].h,
                                                    wgb[isup % 2][:].rearrange("p (c n) -> p c n", n=512),
                                                    hgate3[:, :, 512 * isup:512 * (isup + 1)])
                        wrec[g_] = wtot[isup % 2]
                sync.wait_ge(xl.h, xl.b + 16 * 2 * NH)
                sync.wait_ge(sl.h, sl.b + 32)
                sync.wait_ge(wl[0].h, wl[0].b + wtot[0])
                sync.wait_ge(wl[1].h, wl[1].b + wtot[1])

            @blk.tensor
            def _(t):
                for half in range(2):
                    t.wait_ge(xl.h, xl.b + 16 * NH * (half + 1))
                    for isup in range(NSUP):
                        g_ = half * NSUP + isup
                        t.wait_ge(wl[isup % 2].h, wl[isup % 2].b + wrec[g_])
                        for it in range(4):
                            u = half * 64 + isup * 4 + it
                            if u >= 1:
                                t.wait_ge(vc.h, vcb + u * NSTH)
                                t.wait_ge(sc.h, scb + u * NSTH)
                            for c in range(NH):
                                for st_ in range(NSTH):
                                    t.matmul(pu[st_][:],
                                             wub[isup % 2][:, 512 * c + 128 * it:512 * c + 128 * (it + 1)],
                                             xth[:, TPH * c + 512 * st_:TPH * c + 512 * (st_ + 1)],
                                             start=(c == 0), stop=(c == NH - 1))
                            for c in range(NH):
                                for st_ in range(NSTH):
                                    ins = t.matmul(pg[st_][:],
                                                   wgb[isup % 2][:, 512 * c + 128 * it:512 * c + 128 * (it + 1)],
                                                   xth[:, TPH * c + 512 * st_:TPH * c + 512 * (st_ + 1)],
                                                   start=(c == 0), stop=(c == NH - 1))
                            ins.then_inc(mm.h, 1)

            @blk.scalar
            def _(s):
                s = self.chained(s, "sch")
                s.wait_ge(sl.h, sl.b + 32)
                for half in range(2):
                    for isup in range(NSUP):
                        for it in range(4):
                            u = half * 64 + isup * 4 + it
                            i_abs = isup * 4 + it
                            s.wait_ge(mm.h, mmb + u + 1)
                            for st_ in range(NSTH):
                                k = u * NSTH + st_
                                if k >= 2:
                                    s.wait_ge(vc.h, vcb + k - 1)
                                s.activation(gs[k % 2][:], pg[st_][:], AF.Silu,
                                             scale=scg[:, i_abs:i_abs + 1])

            @blk.vector
            def _(v):
                v = self.chained(v, "vch")
                v.wait_ge(sl.h, sl.b + 32)
                for half in range(2):
                    for isup in range(NSUP):
                        for it in range(4):
                            i_abs = isup * 4 + it
                            for st_ in range(NSTH):
                                k = (half * 64 + isup * 4 + it) * NSTH + st_
                                v.wait_ge(sc.h, scb + k + 1)
                                if k >= 2:
                                    v.wait_ge(st[k % 2].h, st[k % 2].b + 16 * (k // 2))
                                v.scalar_tensor_tensor(ob[k % 2][:], pu[st_][:],
                                                       scu[:, i_abs:i_abs + 1], gs[k % 2][:],
                                                       ALU.mult, ALU.mult)

            @blk.gpsimd
            def _(g):
                for half in range(2):
                    for isup in range(NSUP):
                        for it in range(4):
                            i_abs = isup * 4 + it
                            for st_ in range(NSTH):
                                k = (half * 64 + isup * 4 + it) * NSTH + st_
                                st_abs = half * NSTH + st_
                                g.wait_ge(vc.h, vcb + k + 1)
                                if k >= 2:
                                    g.wait_ge(st[k % 2].h, st[k % 2].b + 16 * (k // 2))
                                g.dma_start(t0[128 * i_abs:128 * (i_abs + 1), 512 * st_abs:512 * (st_abs + 1)],
                                            ob[k % 2][:]).then_inc(st[k % 2].h, 16)
                g.wait_ge(st[0].h, st[0].b + 16 * (KPH // 2))
                g.wait_ge(st[1].h, st[1].b + 16 * (KPH // 2))

        xl.b += 32 * NH
        sl.b += 32
        wl[0].b += wtot[0]
        wl[1].b += wtot[1]
        st[0].b += 16 * (KPH // 2)
        st[1].b += 16 * (KPH // 2)
        mm.b += NIT
        assert vc.b == vcb + KPH and sc.b == scb + KPH
        pc.close()

    def ph_ffn_down(self, hdn_name, res_name, out_name, coln_name=None):
        nc = self.nc
        d = self.dram
        hdn, res, outd, t0 = d[hdn_name], d[res_name], d[out_name], d["t0"]
        pc = ExitStack()
        wdb = self.sbuf([128, NI * 1024], BF16, ctx=pc)
        tb = [self.sbuf([128, NI * 128], BF16, ctx=pc) for _ in range(2)]
        rb = [self.sbuf([128, 1024], F32, ctx=pc) for _ in range(2)]
        ob = [self.sbuf([128, 1024], F32, ctx=pc) for _ in range(2)]
        pj = [self.psum([128, 512], F32, ctx=pc) for _ in range(4)]
        scaled = coln_name is not None
        if scaled:
            pjr = [self.psum([128, 512], F32, ctx=pc) for _ in range(2)]
            colb = self.sbuf([128, 1024], F32, ctx=pc)
            crow = self.sbuf([1, H], F32, ctx=pc)
        wl = self.ctr("ld0")
        tl = [self.ctr("ld2"), self.ctr("ld3")]
        rl = [self.ctr("ld4"), self.ctr("ld5")]
        st = [self.ctr("st0"), self.ctr("st1")]
        mm = self.ctr("mm")
        vc = self.ctr("vch")
        vcb, mmb = vc.b, mm.b
        nevt = 4 if scaled else 2
        nrep = 2 if scaled else 0

        ttot, trec = [0, 0], {}
        t03b = t0.rearrange("(c p) n -> p c n", p=128)

        def posv(k):
            return vcb + nrep * (k // NT + 1) + nevt * (k + 1)

        def posm(k):
            return mmb + nrep * (k // NT + 1) + (k + 1)

        with nc.Block() as blk:
            @blk.sync
            def _(sync):
                t03 = t0.rearrange("(c p) n -> p c n", p=128)
                if scaled:
                    sync.dma_start(crow[:], d[coln_name][:]).then_inc(wl.h, 16)
                for half in range(2):
                    if half == 1:
                        sync.wait_ge(mm.h, posm(NT - 1))
                    for c in range(NI):
                        sync.dma_start(wdb[:, 1024 * c:1024 * (c + 1)],
                                       hdn[128 * c:128 * (c + 1), 1024 * half:1024 * (half + 1)]).then_inc(wl.h, 16)
                    for i in range(NT):
                        k = half * NT + i
                        if k % 2 == 1:
                            continue
                        if k >= 2:
                            sync.wait_ge(mm.h, posm(k - 2))
                        ttot[0] += self.dmaN(sync, tl[0].h,
                                             tb[0][:].rearrange("p (c n) -> p c n", n=128),
                                             t03[:, :, 128 * i:128 * (i + 1)])
                        trec[k] = ttot[0]
                sync.wait_ge(tl[0].h, tl[0].b + ttot[0])
                sync.wait_ge(wl.h, wl.b + (16 if scaled else 0) + 16 * 2 * NI)

            @blk.scalar
            def _(s):
                for half in range(2):
                    for i in range(NT):
                        k = half * NT + i
                        if k % 2 == 1:
                            if k >= 2:
                                s.wait_ge(mm.h, posm(k - 2))
                            ttot[1] += self.dmaN(s, tl[1].h,
                                                 tb[1][:].rearrange("p (c n) -> p c n", n=128),
                                                 t03b[:, :, 128 * i:128 * (i + 1)])
                            trec[k] = ttot[1]
                        if k >= 2:
                            s.wait_ge(vc.h, posv(k - 2))
                        s.dma_start(rb[k % 2][:],
                                    res[128 * i:128 * (i + 1), 1024 * half:1024 * (half + 1)]).then_inc(rl[k % 2].h, 16)
                s.wait_ge(tl[1].h, tl[1].b + ttot[1])
                s.wait_ge(rl[0].h, rl[0].b + 16 * NT)
                s.wait_ge(rl[1].h, rl[1].b + 16 * NT)

            @blk.tensor
            def _(t):
                for half in range(2):
                    t.wait_ge(wl.h, wl.b + (16 if scaled else 0) + 16 * NI * (half + 1))
                    if scaled:
                        for ch in range(2):
                            t.matmul(pjr[ch][:], self.onesrow[:],
                                     crow[:, 1024 * half + 512 * ch:1024 * half + 512 * (ch + 1)],
                                     start=True, stop=True).then_inc(mm.h, 1)
                    for i in range(NT):
                        k = half * NT + i
                        t.wait_ge(tl[k % 2].h, tl[k % 2].b + trec[k])
                        if k >= 2:
                            t.wait_ge(vc.h, posv(k - 2))
                        for c in range(NI):
                            for hc in range(2):
                                ins = t.matmul(pj[(k % 2) * 2 + hc][:],
                                               tb[k % 2][:, 128 * c:128 * (c + 1)],
                                               wdb[:, 1024 * c + 512 * hc:1024 * c + 512 * (hc + 1)],
                                               start=(c == 0), stop=(c == NI - 1))
                        ins.then_inc(mm.h, 1)

            @blk.vector
            def _(v):
                v = self.chained(v, "vch")


                for half in range(2):
                    if scaled:
                        for ch in range(2):
                            v.wait_ge(mm.h, mmb + nrep * half + (NT * half) + ch + 1)
                            v.tensor_copy(colb[:, 512 * ch:512 * (ch + 1)], pjr[ch][:])
                    for i in range(NT):
                        k = half * NT + i
                        v.wait_ge(mm.h, posm(k))
                        v.wait_ge(rl[k % 2].h, rl[k % 2].b + 16 * (k // 2 + 1))
                        if k >= 2:
                            v.wait_ge(st[k % 2].h, st[k % 2].b + 16 * (k // 2))
                        if not scaled:
                            for hc in range(2):
                                v.tensor_add(ob[k % 2][:, 512 * hc:512 * (hc + 1)],
                                             pj[(k % 2) * 2 + hc][:],
                                             rb[k % 2][:, 512 * hc:512 * (hc + 1)])
                        else:
                            for hc in range(2):
                                v.tensor_tensor(ob[k % 2][:, 512 * hc:512 * (hc + 1)],
                                                pj[(k % 2) * 2 + hc][:],
                                                colb[:, 512 * hc:512 * (hc + 1)], ALU.mult)
                            for hc in range(2):
                                v.tensor_add(ob[k % 2][:, 512 * hc:512 * (hc + 1)],
                                             ob[k % 2][:, 512 * hc:512 * (hc + 1)],
                                             rb[k % 2][:, 512 * hc:512 * (hc + 1)])

            @blk.gpsimd
            def _(g):
                for half in range(2):
                    for i in range(NT):
                        k = half * NT + i
                        g.wait_ge(vc.h, posv(k))
                        if k >= 2:
                            g.wait_ge(st[k % 2].h, st[k % 2].b + 16 * (k // 2))
                        g.dma_start(outd[128 * i:128 * (i + 1), 1024 * half:1024 * (half + 1)],
                                    ob[k % 2][:]).then_inc(st[k % 2].h, 16)
                g.wait_ge(st[0].h, st[0].b + 16 * NT)
                g.wait_ge(st[1].h, st[1].b + 16 * NT)

        wl.b += (16 if scaled else 0) + 32 * NI
        tl[0].b += ttot[0]
        tl[1].b += ttot[1]
        rl[0].b += 16 * NT
        rl[1].b += 16 * NT
        st[0].b += 16 * NT
        st[1].b += 16 * NT
        mm.b += 2 * nrep + 2 * NT
        assert vc.b == posv(2 * NT - 1), (vc.b, posv(2 * NT - 1))
        pc.close()


    def ph_attn1(self):
        nc = self.nc
        d = self.dram
        qT, kT1, v1, ctx = d["qT"], d["kT1"], d["v1"], d["ctx"]
        NST = S // 512
        pc = ExitStack()
        ksb = self.sbuf([128, 4 * S], BF16, ctx=pc)
        vsb = self.sbuf([128, NT * 512], BF16, ctx=pc)
        qts = [self.sbuf([128, 16 * 512], BF16, ctx=pc) for _ in range(2)]
        pbr = self.sbuf([128, 4 * 1024], BF16, ctx=pc)
        sqb = self.sbuf([128, 16 * 512], BF16, ctx=pc)
        qn2 = self.sbuf([1, 16 * 512], F32, ctx=pc)
        knrow = self.sbuf([1, S], F32, ctx=pc)
        mt = self.sbuf([1, 64], F32, ctx=pc)
        cbias = self.sbuf([128, 8], F32, ctx=pc)
        rrow = self.sbuf([1, 1024], F32, ctx=pc)
        rsb = self.sbuf([128, 1024], F32, ctx=pc)
        ob = [self.sbuf([128, 1024], BF16, ctx=pc) for _ in range(2)]
        psc = [self.psum([128, 1024], F32, ctx=pc) for _ in range(2)]
        pcs = [self.psum([128, 512], F32, ctx=pc) for _ in range(2)]
        patt = self.psum([128, 1024], F32, ctx=pc)
        kl, vl = self.ctr("ld0"), self.ctr("ld1")
        ql = [self.ctr("ld2"), self.ctr("ld3")]
        st = [self.ctr("st0"), self.ctr("st1")]
        mm = self.ctr("mm")
        vc, sc = self.ctr("vch"), self.ctr("sch")
        V0, S0, M0 = vc.b, sc.b, mm.b
        nj = lambda s_: 4 * s_ + 4
        NKCH = S // 512
        qtot, qrec, vstot = [0, 0], {}, [0]

        POS = {}
        vp, sp, mp = V0, S0, M0
        for g in range(4):
            vp += 1
            for ch in range(NKCH):
                mp += 1; POS[("kcs", g, ch)] = mp
                vp += 1; POS[("kev", g, ch)] = vp
            vp += 1; POS[("kmax", g)] = vp
            sp += 1; POS[("ksqrt", g)] = sp
        for s_ in range(NST):
            for h in range(16):
                vp += 1
            POS[("sqdone", s_)] = vp
            for h in range(16):
                mp += 1; POS[("qcs", s_, h)] = mp
                vp += 1; POS[("qev", s_, h)] = vp
                vp += 1; POS[("qmax", s_, h)] = vp
            for h in range(16):
                sp += 1; POS[("qsqrt", s_, h)] = sp
            for P in range(8):
                vp += 1
                vp += 1; POS[("negc", s_, P)] = vp
            for P in range(8):
                mp += 1; POS[("crepl", s_, P)] = mp
                vp += 1; POS[("cbias", s_, P)] = vp
            for P in range(8):
                n = nj(s_)
                for j in range(n):
                    mp += 1; POS[("sco", s_, P, j)] = mp
                    if j >= 1:
                        mp += 1; POS[("pv", s_, P, j - 1)] = mp
                mp += 1; POS[("cslast", s_, P)] = mp
                mp += 1; POS[("pv", s_, P, n - 1)] = mp
                mp += 2; POS[("repl", s_, P)] = mp
                for j in range(n):
                    sp += 1; POS[("exp", s_, P, j)] = sp
                for qc in range(4):
                    vp += 2 * qc + 2
                POS[("maskd", s_, P)] = vp
                vp += 2; POS[("recip", s_, P)] = vp
                vp += 1; POS[("rsb", s_, P)] = vp
                vp += 1; POS[("obtt", s_, P)] = vp
            POS[("stend", s_)] = mp
        VEND, SEND, MEND = vp, sp, mp

        def maskpos(s_, P, j):
            qc = j - 4 * s_
            base = POS[("cbias", s_, 7)] if False else None
            pref = sum(2 * q + 2 for q in range(qc + 1))
            return POS[("maskd", s_, P)] - sum(2 * q + 2 for q in range(4)) + pref

        with nc.Block() as blk:
            @blk.sync
            def _(sync):
                for g in range(4):
                    sync.dma_start(ksb[:, S * g:S * (g + 1)],
                                   kT1[128 * g:128 * (g + 1), :]).then_inc(kl.h, 16)
                vstot[0] = self.dmaN(sync, vl.h,
                                     vsb[:].rearrange("p (t n) -> p t n", n=512),
                                     v1.rearrange("(t p) n -> p t n", p=128))
                qT3 = qT.rearrange("(c p) n -> p c n", p=128)
                for s_ in range(NST):
                    if s_ >= 2:
                        sync.wait_ge(mm.h, POS[("stend", s_ - 1)])
                    qtot[s_ % 2] += self.dmaN(sync, ql[s_ % 2].h,
                                              qts[s_ % 2][:].rearrange("p (c n) -> p c n", n=512),
                                              qT3[:, :, 512 * s_:512 * (s_ + 1)])
                    qrec[s_] = qtot[s_ % 2]
                sync.wait_ge(ql[0].h, ql[0].b + qtot[0])
                sync.wait_ge(ql[1].h, ql[1].b + qtot[1])

            @blk.vector
            def _(v):
                v = self.chained(v, "vch")
                v.wait_ge(kl.h, kl.b + 64)
                for g in range(4):
                    if g >= 1:
                        v.wait_ge(mm.h, POS[("kcs", g - 1, NKCH - 1)])
                    v.tensor_tensor(sqb[:, 0:S], ksb[:, S * g:S * (g + 1)],
                                    ksb[:, S * g:S * (g + 1)], ALU.mult)
                    for ch in range(NKCH):
                        v.wait_ge(mm.h, POS[("kcs", g, ch)])
                        v.tensor_copy(knrow[:, 512 * ch:512 * (ch + 1)],
                                      pcs[(NKCH * g + ch) % 2][0:1, :])
                    v.reduce_max(mt[:, g:g + 1], knrow[:, 0:S], axis=AX.X)
                for s_ in range(NST):
                    v.wait_ge(ql[s_ % 2].h, ql[s_ % 2].b + qrec[s_])
                    if s_ >= 1:
                        v.wait_ge(mm.h, POS[("qcs", s_ - 1, 15)])
                    else:
                        v.wait_ge(mm.h, POS[("kcs", 3, NKCH - 1)])
                    for h in range(16):
                        v.tensor_tensor(sqb[:, 512 * h:512 * (h + 1)],
                                        qts[s_ % 2][:, 512 * h:512 * (h + 1)],
                                        qts[s_ % 2][:, 512 * h:512 * (h + 1)], ALU.mult)
                    for h in range(16):
                        v.wait_ge(mm.h, POS[("qcs", s_, h)])
                        v.tensor_copy(qn2[:, 512 * h:512 * (h + 1)], pcs[h % 2][0:1, :])
                        v.reduce_max(mt[:, 4 + h:5 + h], qn2[:, 512 * h:512 * (h + 1)], axis=AX.X)
                    for P in range(8):
                        v.wait_ge(sc.h, POS[("qsqrt", s_, 2 * P + 1)])
                        v.tensor_max(mt[:, 36 + P:37 + P], mt[:, 20 + 2 * P:21 + 2 * P],
                                     mt[:, 21 + 2 * P:22 + 2 * P])
                        v.tensor_scalar(mt[:, 36 + P:37 + P], mt[:, 36 + P:37 + P],
                                        mt[:, P // 2:P // 2 + 1], -1.0, ALU.mult, ALU.mult)
                    for P in range(8):
                        v.wait_ge(mm.h, POS[("crepl", s_, P)])
                        v.tensor_copy(cbias[:, P:P + 1], psc[0][:, P:P + 1])
                    for P in range(8):
                        n = nj(s_)
                        for qc in range(4):
                            j = 4 * s_ + qc
                            v.wait_ge(sc.h, POS[("exp", s_, P, j)])
                            pb = pbr[:, 1024 * (j % 4):1024 * (j % 4 + 1)]
                            for qc2 in range(qc):
                                v.memset(pb[:, 128 * qc2:128 * (qc2 + 1)], 0.0)
                                v.memset(pb[:, 512 + 128 * qc2:512 + 128 * (qc2 + 1)], 0.0)
                            v.tensor_tensor(pb[:, 128 * qc:128 * (qc + 1)],
                                            pb[:, 128 * qc:128 * (qc + 1)],
                                            self.utmaskb[:], ALU.mult)
                            v.tensor_tensor(pb[:, 512 + 128 * qc:512 + 128 * (qc + 1)],
                                            pb[:, 512 + 128 * qc:512 + 128 * (qc + 1)],
                                            self.utmaskb[:], ALU.mult)
                        v.wait_ge(mm.h, POS[("cslast", s_, P)])
                        v.reciprocal(rrow[:, 0:512], pcs[0][0:1, :])
                        v.reciprocal(rrow[:, 512:1024], pcs[1][0:1, :])
                        v.wait_ge(mm.h, POS[("repl", s_, P)])
                        v.tensor_copy(rsb[:], psc[0][:])
                        v.wait_ge(mm.h, POS[("pv", s_, P, n - 1)])
                        gi = 8 * s_ + P
                        if gi >= 2:
                            v.wait_ge(st[gi % 2].h, st[gi % 2].b + 32 * (gi // 2))
                        v.tensor_tensor(ob[gi % 2][:], patt[:], rsb[:], ALU.mult)

            @blk.scalar
            def _(s):
                s = self.chained(s, "sch")
                for g in range(4):
                    s.wait_ge(vc.h, POS[("kmax", g)])
                    s.activation(mt[:, g:g + 1], mt[:, g:g + 1], AF.Sqrt)
                for s_ in range(NST):
                    for h in range(16):
                        s.wait_ge(vc.h, POS[("qmax", s_, h)])
                        s.activation(mt[:, 20 + h:21 + h], mt[:, 4 + h:5 + h], AF.Sqrt)
                    for P in range(8):
                        n = nj(s_)
                        s.wait_ge(vc.h, POS[("cbias", s_, P)])
                        for j in range(n):
                            s.wait_ge(mm.h, POS[("sco", s_, P, j)])
                            if j >= 4:
                                s.wait_ge(mm.h, POS[("pv", s_, P, j - 4)])
                            s.activation(pbr[:, 1024 * (j % 4):1024 * (j % 4 + 1)],
                                         psc[j % 2][:], AF.Exp, bias=cbias[:, P:P + 1])

            @blk.gpsimd
            def _(g_):
                for s_ in range(NST):
                    for P in range(8):
                        gi = 8 * s_ + P
                        h0 = 2 * P
                        g_.wait_ge(vc.h, POS[("obtt", s_, P)])
                        if gi >= 2:
                            g_.wait_ge(st[gi % 2].h, st[gi % 2].b + 32 * (gi // 2))
                        g_.dma_start(ctx[128 * h0:128 * (h0 + 1), 512 * s_:512 * (s_ + 1)],
                                     ob[gi % 2][:, 0:512]).then_inc(st[gi % 2].h, 16)
                        g_.dma_start(ctx[128 * (h0 + 1):128 * (h0 + 2), 512 * s_:512 * (s_ + 1)],
                                     ob[gi % 2][:, 512:1024]).then_inc(st[gi % 2].h, 16)
                ngi = 8 * NST
                g_.wait_ge(st[0].h, st[0].b + 32 * (ngi // 2))
                g_.wait_ge(st[1].h, st[1].b + 32 * (ngi // 2))

            @blk.tensor
            def _(t):
                t.wait_ge(kl.h, kl.b + 64)
                for g in range(4):
                    t.wait_ge(vc.h, V0 + (NKCH + 2) * g + 1)
                    for ch in range(NKCH):
                        flat = NKCH * g + ch
                        if flat >= 2:
                            t.wait_ge(vc.h, POS[("kev", (flat - 2) // NKCH, (flat - 2) % NKCH)])
                        t.matmul(pcs[flat % 2][0:1, :], self.onescol[:],
                                 sqb[:, 512 * ch:512 * (ch + 1)],
                                 start=True, stop=True).then_inc(mm.h, 1)
                t.wait_ge(vl.h, vl.b + vstot[0])
                for s_ in range(NST):
                    n = nj(s_)
                    t.wait_ge(vc.h, POS[("sqdone", s_)])
                    if s_ >= 1:
                        t.wait_ge(vc.h, POS[("recip", s_ - 1, 7)])
                    for h in range(16):
                        if h >= 2:
                            t.wait_ge(vc.h, POS[("qev", s_, h - 2)])
                        t.matmul(pcs[h % 2][0:1, :], self.onescol[:],
                                 sqb[:, 512 * h:512 * (h + 1)],
                                 start=True, stop=True).then_inc(mm.h, 1)
                    if s_ >= 1:
                        t.wait_ge(vc.h, POS[("rsb", s_ - 1, 7)])
                    for P in range(8):
                        t.wait_ge(vc.h, POS[("negc", s_, P)])
                        t.matmul(psc[0][:, P:P + 1], self.onesrow[:],
                                 mt[:, 36 + P:37 + P],
                                 start=True, stop=True).then_inc(mm.h, 1)
                    for P in range(8):
                        gr = P // 2
                        h0 = 2 * P
                        if P == 0:
                            t.wait_ge(vc.h, POS[("cbias", s_, 7)])
                        else:
                            t.wait_ge(vc.h, POS[("rsb", s_, P - 1)])
                        for j in range(n):
                            if j >= 2:
                                t.wait_ge(sc.h, POS[("exp", s_, P, j - 2)])
                            lk = ksb[:, S * gr + 128 * j:S * gr + 128 * (j + 1)]
                            t.matmul(psc[j % 2][:, 0:512], lk,
                                     qts[s_ % 2][:, 512 * h0:512 * (h0 + 1)],
                                     start=True, stop=True)
                            t.matmul(psc[j % 2][:, 512:1024], lk,
                                     qts[s_ % 2][:, 512 * (h0 + 1):512 * (h0 + 2)],
                                     start=True, stop=True).then_inc(mm.h, 1)
                            if j >= 1:
                                jj = j - 1
                                pb = pbr[:, 1024 * (jj % 4):1024 * (jj % 4 + 1)]
                                if jj >= 4 * s_:
                                    t.wait_ge(vc.h, maskpos(s_, P, jj))
                                else:
                                    t.wait_ge(sc.h, POS[("exp", s_, P, jj)])
                                if jj == 0:
                                    t.wait_ge(vc.h, POS[("recip", s_, P - 1)] if P >= 1
                                              else (POS[("recip", s_ - 1, 7)] if s_ >= 1 else 0))
                                t.matmul(pcs[0][0:1, :], self.onescol[:], pb[:, 0:512],
                                         start=(jj == 0), stop=(jj == n - 1))
                                t.matmul(pcs[1][0:1, :], self.onescol[:], pb[:, 512:1024],
                                         start=(jj == 0), stop=(jj == n - 1))
                                lv = vsb[:, 512 * jj + 128 * gr:512 * jj + 128 * (gr + 1)]
                                if jj == 0:
                                    t.wait_ge(vc.h, POS[("obtt", s_, P - 1)] if P >= 1
                                              else (POS[("obtt", s_ - 1, 7)] if s_ >= 1 else 0))
                                t.matmul(patt[:, 0:512], lv, pb[:, 0:512],
                                         start=(jj == 0), stop=(jj == n - 1))
                                t.matmul(patt[:, 512:1024], lv, pb[:, 512:1024],
                                         start=(jj == 0), stop=(jj == n - 1)).then_inc(mm.h, 1)
                        jj = n - 1
                        pb = pbr[:, 1024 * (jj % 4):1024 * (jj % 4 + 1)]
                        t.wait_ge(vc.h, maskpos(s_, P, jj))
                        t.matmul(pcs[0][0:1, :], self.onescol[:], pb[:, 0:512],
                                 start=(jj == 0), stop=True)
                        t.matmul(pcs[1][0:1, :], self.onescol[:], pb[:, 512:1024],
                                 start=(jj == 0), stop=True).then_inc(mm.h, 1)
                        lv = vsb[:, 512 * jj + 128 * gr:512 * jj + 128 * (gr + 1)]
                        if jj == 0:
                            t.wait_ge(vc.h, POS[("obtt", s_, P - 1)] if P >= 1
                                      else (POS[("obtt", s_ - 1, 7)] if s_ >= 1 else 0))
                        t.matmul(patt[:, 0:512], lv, pb[:, 0:512],
                                 start=(jj == 0), stop=True)
                        t.matmul(patt[:, 512:1024], lv, pb[:, 512:1024],
                                 start=(jj == 0), stop=True).then_inc(mm.h, 1)
                        t.wait_ge(vc.h, POS[("recip", s_, P)])
                        t.matmul(psc[0][:, 0:512], self.onesrow[:], rrow[:, 0:512],
                                 start=True, stop=True).then_inc(mm.h, 1)
                        t.matmul(psc[0][:, 512:1024], self.onesrow[:], rrow[:, 512:1024],
                                 start=True, stop=True).then_inc(mm.h, 1)

        ngi = 8 * NST
        kl.b += 64
        vl.b += vstot[0]
        ql[0].b += qtot[0]
        ql[1].b += qtot[1]
        st[0].b += 32 * (ngi // 2)
        st[1].b += 32 * (ngi // 2)
        mm.b = MEND
        assert vc.b == VEND, (vc.b, VEND)
        assert sc.b == SEND, (sc.b, SEND)
        pc.close()

def build_program():
    nc = bass.Bass("TRN2", target_bir_lowering=False)
    p = Prog(nc)
    f32in = [("x0", [S, H]), ("wqp", [H, H]), ("wkp", [H, H]), ("wv", [H, H]),
             ("wupts", [H, I]), ("wgatets", [H, I]), ("wdnts", [I, H]),
             ("su_ug", [128, 16]), ("su_dp", [128, 64]),
             ("wsc_up", [128, 64]), ("wsc_gate", [128, 64]), ("wsc_dp", [1, H]),
             ("wq1", [H, H]), ("wkv1", [H, 1024]), ("wdense1", [H, H]),
             ("sc_up1", [128, 64]), ("sc_gate1", [128, 64]), ("sc_dn1", [1, H]),
             ("qcos", [1024, S]), ("qsin", [1024, S]),
             ("kcos", [1024, S]), ("ksin", [1024, S]),
             ("ident", [128, 128]), ("h128", [128, 128]), ("h16x8", [128, 128]),
             ("h64x2", [128, 128]), ("utmask", [128, 128])]
    for n, shp in f32in:
        p.dram_in(n, shp, F32)
    for n, shp in [("upt1", [H, I]), ("gatet1", [H, I]), ("dnt1", [I, H])]:
        p.dram_in(n, shp, U8)
    for n, shp, dt in [("xnT", [H, S], BF16), ("qT", [H, S], BF16),
                       ("kT", [H, S], BF16), ("v0", [S, H], BF16),
                       ("psp", [S, S], BF16), ("r1", [S, H], F32),
                       ("x1", [S, H], F32), ("wbounce", [H, I], BF16),
                       ("wbounce2", [I, H], BF16), ("hwup", [H, I], BF16),
                       ("hwgate", [H, I], BF16), ("hwdn", [I, H], BF16),
                       ("t0", [I, S], BF16), ("kT1", [512, S], BF16),
                       ("v1", [S, 512], BF16), ("ctx", [H, S], BF16)]:
        p.dram_tmp(n, shp, dt)
    p.dram_out("out", [S, H], F32)

    p.setup_consts()
    p.ph_wprep("wupts", "hwup", 16, I, 16, 16, "su_ug", "row")
    p.ph_wprep("wgatets", "hwgate", 16, I, 16, 16, "su_ug", "row")
    p.ph_wprep("wdnts", "hwdn", 64, H, 64, 64, "su_dp", "rowcol",
               colrow_name="wsc_dp", bounce_name="wbounce2")
    p.ph_ln("x0")
    p.ph_projT("wqp", "qT", 16, xpos=("qcos", "qsin"))
    p.ph_projT("wkp", "kT", 16, xpos=("kcos", "ksin"))
    p.ph_projN("wv", "v0", H)
    p.ph_scores0(build_consts()["cdec"])
    p.ph_pv0()
    p.ph_ffn_up("hwup", "hwgate", "wsc_up", "wsc_gate")
    p.ph_ffn_down("hwdn", "r1", "x1")
    p.ph_wcast("upt1", "hwup", H, I)
    p.ph_wcast("gatet1", "hwgate", H, I)
    p.ph_wcast("dnt1", "hwdn", I, H)
    p.ph_ln("x1")
    p.ph_projT("wq1", "qT", 16, scale=float(KVC ** -0.5))
    p.ph_projT("wkv1", "kT1", 4, col_off=0)
    p.ph_projN("wkv1", "v1", 512, col_off=512)
    p.ph_attn1()
    p.ph_projN("wdense1", "r1", H, src_name="ctx", res_name="x1")
    p.ph_ffn_up("hwup", "hwgate", "sc_up1", "sc_gate1")
    p.ph_ffn_down("hwdn", "r1", "out", coln_name="sc_dn1")
    p.ctx.close()
    nc.finalize()
    return nc


def kernel(**inputs):
    X = np.ascontiguousarray(inputs["L0_attn_X"].astype(np.float32, copy=False))
    prep = host_prep(inputs)
    c = build_consts()
    nc = build_program()
    base = dict(prep)
    base.update(qcos=c["qcos"], qsin=c["qsin"], kcos=c["kcos"], ksin=c["ksin"],
                ident=c["ident"], h128=c["h128"], h16x8=c["h16x8"],
                h64x2=c["h64x2"], utmask=c["utmask"])
    in_maps = []
    for core in range(8):
        m = dict(base)
        m["x0"] = np.ascontiguousarray(X[core])
        in_maps.append(m)
    res = run_bass_kernel_spmd(nc, in_maps, core_ids=list(range(8)))
    return np.stack([r["out"] for r in res.results], axis=0)

if __name__ == "__main__":
    def _xpos_np(x, downscale=False):
        L, d = x.shape
        half = d // 2
        scale_vec = (np.arange(0, d, 2, dtype=np.float64) + 0.4 * d) / (1.4 * d)
        power = np.arange(L, dtype=np.float64) / SCALE_BASE
        scale = scale_vec[None, :] ** power[:, None]
        inv_freq = 1.0 / 10000.0 ** (np.arange(half, dtype=np.float64) / half)
        sinusoid = np.arange(L, dtype=np.float64)[:, None] * inv_freq[None, :]
        sin, cos = np.sin(sinusoid), np.cos(sinusoid)
        if downscale:
            scale = 1.0 / scale
        sin_s = np.repeat(sin * scale, 2, axis=-1)
        cos_s = np.repeat(cos * scale, 2, axis=-1)
        x1, x2 = x[:, 0::2], x[:, 1::2]
        rot = np.stack((-x2, x1), axis=-1).reshape(x.shape)
        return x * cos_s + rot * sin_s

    rng = np.random.default_rng(0)
    Ltest = 256
    X = rng.standard_normal((Ltest, H)).astype(np.float64)
    c = build_consts()
    dq = (GAMMA ** (np.arange(S) % 128)).astype(np.float64)
    for name, ds, tc, ts_, dd in (
        ("q", False, c["qcos"], c["qsin"], dq),
        ("k", True, c["kcos"], c["ksin"], 1.0 / dq),
    ):
        ref = _xpos_np(X, ds)
        Qe, Qo = X[:, 0::2], X[:, 1::2]
        cosv = tc[:, :Ltest].T.astype(np.float64) / dd[:Ltest, None]
        sinv = ts_[:, :Ltest].T.astype(np.float64) / dd[:Ltest, None]
        oute = Qe * cosv - Qo * sinv
        outo = Qo * cosv + Qe * sinv
        got = np.empty_like(X)
        got[:, 0::2], got[:, 1::2] = oute, outo
        err = np.abs(got - ref).max() / (np.abs(ref).max() + 1e-9)
        print(f"xpos {name}: relmax {err:.2e}")

    def fwht(x):
        n = x.shape[-1]
        h = 1
        x = x.copy()
        while h < n:
            x = x.reshape(-1, n // (2 * h), 2, h)
            a, b = x[:, :, 0, :].copy(), x[:, :, 1, :].copy()
            x = np.stack([a + b, a - b], axis=2).reshape(-1, n)
            h *= 2
        return x / math.sqrt(n)

    for (n_feat, ntile, hcross) in ((H, 16, c["h16x8"]), ):
        Wup = rng.standard_normal((I, H)) * 0.02
        su = np.where(rng.random(H) > 0.5, 1.0, -1.0)
        ws = rng.uniform(0.5, 1.5, I)
        x = rng.standard_normal((4, H))
        ref = (fwht(x * su) / 64.0) @ Wup.T * (ws * 64.0)
        wupts = _swiz(np.ascontiguousarray(Wup.T), 16)
        step1 = np.zeros_like(wupts)
        for beta in range(16):
            step1[128*beta:128*(beta+1)] = c["h16x8"].astype(np.float64) @ wupts[128*beta:128*(beta+1)]
        aord = np.zeros_like(step1)
        for alpha in range(16):
            aord[128*alpha:128*(alpha+1)] = step1[alpha::16]
        what = np.zeros_like(aord)
        h128 = c["h128"].astype(np.float64)
        for alpha in range(16):
            blk = h128 @ aord[128*alpha:128*(alpha+1)]
            what[128*alpha:128*(alpha+1)] = blk * su[128*alpha:128*(alpha+1), None] / math.sqrt(H)
        got = (x @ what) * ws
        print("fwht fold up: relmax", np.abs(got - ref).max() / np.abs(ref).max())

    Wdn = rng.standard_normal((H, I)) * 0.02
    su = np.where(rng.random(I) > 0.5, 1.0, -1.0)
    wsd = rng.uniform(0.5, 1.5, H)
    t = rng.standard_normal((4, I))
    ref = (fwht(t * su) / 64.0) @ Wdn.T * (wsd * 64.0)
    wdnts = _swiz(np.ascontiguousarray(Wdn.T), 64)
    step1 = np.zeros_like(wdnts)
    for beta in range(64):
        step1[128*beta:128*(beta+1)] = c["h64x2"].astype(np.float64) @ wdnts[128*beta:128*(beta+1)]
    aord = np.zeros_like(step1)
    for alpha in range(64):
        aord[128*alpha:128*(alpha+1)] = step1[alpha::64]
    what = np.zeros_like(aord)
    h128 = c["h128"].astype(np.float64)
    sdpb = wsd / math.sqrt(I)
    for alpha in range(64):
        blk = h128 @ aord[128*alpha:128*(alpha+1)]
        what[128*alpha:128*(alpha+1)] = blk * su[128*alpha:128*(alpha+1), None] * sdpb[None, :]
    got = t @ what
    print("fwht fold dn: relmax", np.abs(got - ref).max() / np.abs(ref).max())


# version 07
# speedup vs baseline: 1.1458x; 1.1458x over previous
import math
from contextlib import ExitStack

import numpy as np
import ml_dtypes

import concourse.bass as bass
import concourse.mybir as mybir
from concourse.bass_utils import run_bass_kernel_spmd

F32 = mybir.dt.float32
BF16 = mybir.dt.bfloat16
I32 = mybir.dt.int32
U8 = mybir.dt.uint8
AF = mybir.ActivationFunctionType
ALU = mybir.AluOpType
AX = mybir.AxisListType

S = 4096
H = 2048
I = 8192
NT = S // 128
NH = H // 128
NI = I // 128
GAMMA = 0.96875
SCALE_BASE = 512
KVC = 128
HEADS = 16
GROUPS = 4

BF = ml_dtypes.bfloat16


def _hadamard(n):
    h = np.array([[1.0]])
    while h.shape[0] < n:
        h = np.block([[h, h], [h, -h]])
    return h


def build_consts():
    c = {}
    c["ident"] = np.eye(128, dtype=np.float32)
    c["h128"] = _hadamard(128).astype(np.float32)
    c["h16x8"] = np.kron(np.eye(8), _hadamard(16)).astype(np.float32)
    c["h64x2"] = np.kron(np.eye(2), _hadamard(64)).astype(np.float32)
    c["utmask"] = np.triu(np.ones((128, 128), dtype=np.float32))
    d = H
    half = d // 2
    hi = np.arange(half, dtype=np.float64)
    n = np.arange(S, dtype=np.float64)
    scale_vec = (2.0 * hi + 0.4 * d) / (1.4 * d)
    power = n / SCALE_BASE
    scale = scale_vec[None, :] ** power[:, None]
    inv_freq = 1.0 / 10000.0 ** (hi / half)
    sinus = n[:, None] * inv_freq[None, :]
    sin, cos = np.sin(sinus), np.cos(sinus)
    dq = GAMMA ** (n % 128)
    dk = GAMMA ** (-(n % 128))
    c["qcos"] = (cos * scale * dq[:, None]).T.astype(np.float32)
    c["qsin"] = (sin * scale * dq[:, None]).T.astype(np.float32)
    c["kcos"] = (cos / scale * dk[:, None]).T.astype(np.float32)
    c["ksin"] = (sin / scale * dk[:, None]).T.astype(np.float32)
    c["cdec"] = (GAMMA ** (128.0 * np.arange(NT, dtype=np.float64))).astype(np.float32)
    return c


def _swiz(wt, ntile):
    R, C = wt.shape
    assert R == ntile * 128
    return np.ascontiguousarray(
        wt.reshape(ntile, 128, C).transpose(1, 0, 2).reshape(R, C))


def _perm_eo(w):
    return np.ascontiguousarray(np.concatenate([w[:, 0::2], w[:, 1::2]], axis=1))


def host_prep(inputs):
    p = {}
    asc = np.ascontiguousarray
    p["wqp"] = _perm_eo(inputs["L0_attn_W_Q"].astype(np.float32, copy=False))
    p["wkp"] = _perm_eo(inputs["L0_attn_W_K"].astype(np.float32, copy=False))
    p["wv"] = asc(inputs["L0_attn_W_V"].astype(np.float32, copy=False))
    p["wupts"] = _swiz(asc(inputs["L0_ffn_W_up"].T), 16)
    p["wgatets"] = _swiz(asc(inputs["L0_ffn_W_gate"].T), 16)
    p["wdnts"] = _swiz(asc(inputs["L0_ffn_W_down"].T), 64)
    col = lambda v, nt: asc(v.reshape(nt, 128).T.astype(np.float32))
    p["su_ug"] = col(inputs["L0_ffn_SU_ug"], 16)
    p["su_dp"] = col(inputs["L0_ffn_SU_dp"], 64)
    p["wsc_up"] = col(inputs["L0_ffn_Wscale_ug"][:I], 64)
    p["wsc_gate"] = col(inputs["L0_ffn_Wscale_ug"][I:], 64)
    p["wsc_dp"] = asc(inputs["L0_ffn_Wscale_dp"].reshape(1, H).astype(np.float32))
    p["wq1"] = asc(inputs["L1_attn_w_q"].astype(np.float32, copy=False))
    p["wkv1"] = asc(inputs["L1_attn_w_kv"].astype(np.float32, copy=False))
    p["wdense1"] = asc(inputs["L1_attn_w_dense"].astype(np.float32, copy=False))
    def lowbyte(a):
        a = asc(a)
        return asc(a.view(np.uint8).reshape(a.shape[0], a.shape[1], 4)[:, :, 0])
    p["upt1"] = lowbyte(inputs["L1_ffn_up_w"].T)
    p["gatet1"] = lowbyte(inputs["L1_ffn_gate_w"].T)
    p["dnt1"] = lowbyte(inputs["L1_ffn_down_w"].T)
    p["sc_up1"] = col(inputs["L1_ffn_up_scale"], 64)
    p["sc_gate1"] = col(inputs["L1_ffn_gate_scale"], 64)
    p["sc_dn1"] = asc(inputs["L1_ffn_down_scale"].reshape(1, H).astype(np.float32))
    return p



class Prog:

    def __init__(self, nc):
        self.nc = nc
        self.ctx = ExitStack()
        self._n = 0
        self.dram = {}

    def name(self, p):
        self._n += 1
        return f"{p}{self._n}"

    def sbuf(self, shape, dt, p="sb", ctx=None):
        return (ctx or self.ctx).enter_context(self.nc.sbuf_tensor(self.name(p), shape, dt))

    def psum(self, shape, dt, p="ps", ctx=None):
        return (ctx or self.ctx).enter_context(self.nc.psum_tensor(self.name(p), shape, dt))

    def sem(self, p="sem", ctx=None):
        return (ctx or self.ctx).enter_context(self.nc.semaphore(self.name(p)))

    _CHAIN_OPS = ("tensor_copy", "tensor_tensor", "tensor_add", "tensor_sub",
                  "tensor_mul", "tensor_max", "tensor_scalar", "tensor_single_scalar",
                  "scalar_tensor_tensor", "tensor_reduce", "reduce_sum", "reduce_max",
                  "reciprocal", "memset", "select", "activation", "copy", "add",
                  "mul", "square", "sqrt", "sign", "iota")

    def chained(self, eng, cname):
        prog = self
        c = self.ctr(cname)

        class _W:
            def __getattr__(self, k):
                f = getattr(eng, k)
                if k in prog._CHAIN_OPS:
                    def g(*a, **kw):
                        eng.wait_ge(c.h, c.b)
                        ins = f(*a, **kw)
                        ins.then_inc(c.h, 1)
                        c.b += 1
                        return ins
                    return g
                return f
        w = _W()
        w.c = c
        return w

    def dmaN(self, eng, ctrh, dst, src):
        n0 = self.nc.n_instructions()
        ins = eng.dma_start(dst, src)
        parts = self.nc.n_instructions() - n0
        ins.then_inc(ctrh, 16)
        return 16 * parts

    def ctr(self, name):
        if not hasattr(self, "_ctrs"):
            self._ctrs = {}
        if name not in self._ctrs:
            h = self.ctx.enter_context(self.nc.semaphore("c_" + name))
            c = type("Ctr", (), {})()
            c.h, c.b = h, 0
            self._ctrs[name] = c
        return self._ctrs[name]

    def dram_in(self, name, shape, dt):
        self.dram[name] = self.nc.dram_tensor(name, list(shape), dt, kind="ExternalInput").ap()
        return self.dram[name]

    def dram_out(self, name, shape, dt):
        self.dram[name] = self.nc.dram_tensor(name, list(shape), dt, kind="ExternalOutput").ap()
        return self.dram[name]

    def dram_tmp(self, name, shape, dt):
        self.dram[name] = self.nc.dram_tensor(name, list(shape), dt, kind="Internal").ap()
        return self.dram[name]

    def setup_consts(self):
        nc = self.nc
        d = self.dram
        self.identb = self.sbuf([128, 128], BF16, "identb")
        self.h128b = self.sbuf([128, 128], BF16, "h128b")
        self.h16x8b = self.sbuf([128, 128], BF16, "h16x8b")
        self.h64x2b = self.sbuf([128, 128], BF16, "h64x2b")
        self.utmaskb = self.sbuf([128, 128], BF16, "utmaskb")
        self.onescol = self.sbuf([128, 1], BF16, "onescol")
        self.onesrow = self.sbuf([1, 128], F32, "onesrow")
        self.epscol = self.sbuf([128, 1], F32, "epscol")
        self.m128col = self.sbuf([128, 1], F32, "m128col")
        names = ["ident", "h128", "h16x8", "h64x2", "utmask"]
        dsts = [self.identb, self.h128b, self.h16x8b, self.h64x2b, self.utmaskb]
        pc = ExitStack()
        tmp = self.sbuf([128, 5 * 128], F32, "ctmp", ctx=pc)
        ld = self.ctr("ld0")
        with nc.Block() as blk:
            @blk.sync
            def _(sync):
                for i, n in enumerate(names):
                    sync.dma_start(tmp[:, 128 * i:128 * (i + 1)], d[n][:]).then_inc(ld.h, 16)
                sync.wait_ge(ld.h, ld.b + 80)

            @blk.vector
            def _(v):
                v = self.chained(v, "vch")
                v.wait_ge(ld.h, ld.b + 80)
                for i, dst in enumerate(dsts):
                    v.tensor_copy(dst[:], tmp[:, 128 * i:128 * (i + 1)])
                v.memset(self.onescol[:], 1.0)
                v.memset(self.onesrow[:], 1.0)
                v.memset(self.epscol[:], 1e-5)
                v.memset(self.m128col[:], -128.0)
        self.nc.const_aps.register(F32, -128.0, self.m128col[:])
        ld.b += 80
        pc.close()

    def ph_ln(self, x_name):
        nc = self.nc
        x = self.dram[x_name]
        xnT = self.dram["xnT"]
        pc = ExitStack()
        xb = [self.sbuf([128, H], F32, ctx=pc) for _ in range(2)]
        sq = [self.sbuf([128, H], F32, ctx=pc) for _ in range(2)]
        xnb = [self.sbuf([128, H], BF16, ctx=pc) for _ in range(2)]
        stat = [self.sbuf([128, 8], F32, ctx=pc) for _ in range(2)]
        stg = [self.sbuf([128, 2048], BF16, ctx=pc) for _ in range(2)]
        ptb = [self.psum([128, 1024], BF16, ctx=pc) for _ in range(4)]
        ld = [self.ctr("ld0"), self.ctr("ld1")]
        st = [self.ctr("st0"), self.ctr("st1")]
        tp = self.ctr("tp")
        vc, sc = self.ctr("vch"), self.ctr("sch")
        vcb, scb, tpb = vc.b, sc.b, tp.b
        with nc.Block() as blk:
            @blk.sync
            def _(sync):
                for i in range(NT):
                    if i >= 2:
                        sync.wait_ge(vc.h, vcb + 8 * (i - 1))
                    sync.dma_start(xb[i % 2][:], x[128 * i:128 * (i + 1), :]).then_inc(ld[i % 2].h, 16)
                sync.wait_ge(ld[0].h, ld[0].b + 16 * (NT // 2))
                sync.wait_ge(ld[1].h, ld[1].b + 16 * (NT // 2))

            @blk.vector
            def _(v):
                v = self.chained(v, "vch")
                for i in range(NT):
                    b = i % 2
                    v.wait_ge(ld[b].h, ld[b].b + 16 * (i // 2 + 1))
                    v.tensor_tensor(sq[b][:], xb[b][:], xb[b][:], ALU.mult)
                    v.reduce_sum(stat[b][:, 0:1], xb[b][:], axis=AX.X)
                    v.reduce_sum(stat[b][:, 1:2], sq[b][:], axis=AX.X)
                    v.tensor_scalar(stat[b][:, 2:3], stat[b][:, 0:1], -1.0 / H, None, ALU.mult)
                    v.tensor_tensor(stat[b][:, 3:4], stat[b][:, 2:3], stat[b][:, 2:3], ALU.mult)
                    v.scalar_tensor_tensor(stat[b][:, 4:5], stat[b][:, 1:2], 1.0 / H,
                                           stat[b][:, 3:4], ALU.mult, ALU.subtract)
                    v.wait_ge(sc.h, scb + 3 * i + 1)
                    v.reciprocal(stat[b][:, 6:7], stat[b][:, 5:6])
                    if i >= 2:
                        v.wait_ge(tp.h, tpb + 2 * (i - 1))
                    v.tensor_scalar(xnb[b][:], xb[b][:], stat[b][:, 2:3], stat[b][:, 6:7],
                                    ALU.add, ALU.mult)

            @blk.scalar
            def _(s):
                s = self.chained(s, "sch")
                for i in range(NT):
                    b = i % 2
                    s.wait_ge(vc.h, vcb + 8 * i + 6)
                    s.activation(stat[b][:, 5:6], stat[b][:, 4:5], AF.Sqrt, bias=self.epscol[:])
                    s.wait_ge(tp.h, tpb + 2 * i + 1)
                    if i >= 2:
                        s.wait_ge(st[b].h, st[b].b + 256 * (i // 2))
                    s.copy(stg[b][:, 0:1024], ptb[(2 * i) % 4][:])
                    s.wait_ge(tp.h, tpb + 2 * i + 2)
                    s.copy(stg[b][:, 1024:2048], ptb[(2 * i + 1) % 4][:])

            @blk.tensor
            def _(t):
                for i in range(NT):
                    b = i % 2
                    t.wait_ge(vc.h, vcb + 8 * (i + 1))
                    if i >= 2:
                        t.wait_ge(sc.h, scb + 3 * (i - 1))
                    for half in range(2):
                        p = ptb[(2 * i + half) % 4]
                        for c in range(8):
                            cc = 8 * half + c
                            ins = t.transpose(p[:, 128 * c:128 * (c + 1)],
                                              xnb[b][:, 128 * cc:128 * (cc + 1)], self.identb[:])
                        ins.then_inc(tp.h, 1)

            @blk.gpsimd
            def _(g):
                for i in range(NT):
                    b = i % 2
                    g.wait_ge(sc.h, scb + 3 * (i + 1))
                    if i >= 2:
                        g.wait_ge(st[b].h, st[b].b + 256 * (i // 2))
                    for c in range(16):
                        g.dma_start(xnT[128 * c:128 * (c + 1), 128 * i:128 * (i + 1)],
                                    stg[b][:, 128 * c:128 * (c + 1)]).then_inc(st[b].h, 16)
                g.wait_ge(st[0].h, st[0].b + 256 * (NT // 2))
                g.wait_ge(st[1].h, st[1].b + 256 * (NT // 2))
        ld[0].b += 16 * (NT // 2)
        ld[1].b += 16 * (NT // 2)
        st[0].b += 256 * (NT // 2)
        st[1].b += 256 * (NT // 2)
        tp.b += 2 * NT
        assert vc.b == vcb + 8 * NT, (vc.b, vcb)
        assert sc.b == scb + 3 * NT, (sc.b, scb)
        pc.close()


    def ph_projN(self, w_name, out_name, E, src_name="xnT", col_off=0,
                 res_name=None, w_cols=None):
        nc = self.nc
        d = self.dram
        w, outd, srcd = d[w_name], d[out_name], d[src_name]
        C = srcd.shape[0]
        NC = C // 128
        NE = E // 512
        w_cols = w_cols if w_cols is not None else E
        pc = ExitStack()
        wsb = self.sbuf([128, NC * E], BF16, ctx=pc)
        wtmp = [self.sbuf([128, E], F32, ctx=pc) for _ in range(2)]
        lb = [self.sbuf([128, 16 * 128], BF16, ctx=pc) for _ in range(2)]
        ob = [self.sbuf([128, E], BF16 if res_name is None else F32, ctx=pc) for _ in range(2)]
        rb = [self.sbuf([128, E], F32, ctx=pc) for _ in range(2)] if res_name else None
        pj = [self.psum([128, 512], F32, ctx=pc) for _ in range(8)]
        ld = [self.ctr("ld0"), self.ctr("ld1")]
        xl = [self.ctr("ld2"), self.ctr("ld3")]
        rl = [self.ctr("ld4"), self.ctr("ld5")]
        st = [self.ctr("st0"), self.ctr("st1")]
        mm = self.ctr("mm")
        vc = self.ctr("vch")
        vcb, mmb = vc.b, mm.b
        xtot = [0, 0]
        xrec = {}

        with nc.Block() as blk:
            @blk.sync
            def _(sync):
                for c in range(NC):
                    if c >= 2:
                        sync.wait_ge(vc.h, vcb + c - 1)
                    sync.dma_start(wtmp[c % 2][:, 0:w_cols],
                                   w[128 * c:128 * (c + 1), col_off:col_off + w_cols]).then_inc(ld[c % 2].h, 16)
                src3 = srcd.rearrange("(c p) n -> p c n", p=128)
                for i in range(NT):
                    if i >= 2:
                        sync.wait_ge(mm.h, mmb + (i - 1) * NE)
                    xtot[i % 2] += self.dmaN(sync, xl[i % 2].h,
                                             lb[i % 2][:].rearrange("p (c n) -> p c n", n=128),
                                             src3[:, :, 128 * i:128 * (i + 1)])
                    xrec[i] = xtot[i % 2]
                sync.wait_ge(xl[0].h, xl[0].b + xtot[0])
                sync.wait_ge(xl[1].h, xl[1].b + xtot[1])
                sync.wait_ge(ld[0].h, ld[0].b + 16 * ((NC + 1) // 2))
                sync.wait_ge(ld[1].h, ld[1].b + 16 * (NC // 2))

            @blk.scalar
            def _(s):
                if res_name is None:
                    return
                res = d[res_name]
                for i in range(NT):
                    if i >= 2:
                        s.wait_ge(vc.h, vcb + NC + NE * (i - 1))
                    s.dma_start(rb[i % 2][:], res[128 * i:128 * (i + 1), :]).then_inc(rl[i % 2].h, 16)
                s.wait_ge(rl[0].h, rl[0].b + 16 * (NT // 2))
                s.wait_ge(rl[1].h, rl[1].b + 16 * (NT // 2))

            @blk.vector
            def _(v):
                v = self.chained(v, "vch")
                for c in range(NC):
                    v.wait_ge(ld[c % 2].h, ld[c % 2].b + 16 * (c // 2 + 1))
                    v.tensor_copy(wsb[:, E * c + 0:E * c + w_cols], wtmp[c % 2][:, 0:w_cols])
                for i in range(NT):
                    b = i % 2
                    if res_name is not None:
                        v.wait_ge(rl[b].h, rl[b].b + 16 * (i // 2 + 1))
                    if i >= 2:
                        v.wait_ge(st[b].h, st[b].b + 16 * (i // 2))
                    for hc in range(NE):
                        k = i * NE + hc
                        v.wait_ge(mm.h, mmb + k + 1)
                        if res_name is None:
                            v.tensor_copy(ob[b][:, 512 * hc:512 * (hc + 1)], pj[k % 8][:])
                        else:
                            v.tensor_add(ob[b][:, 512 * hc:512 * (hc + 1)], pj[k % 8][:],
                                         rb[b][:, 512 * hc:512 * (hc + 1)])

            @blk.gpsimd
            def _(g):
                for i in range(NT):
                    b = i % 2
                    g.wait_ge(vc.h, vcb + NC + NE * (i + 1))
                    if i >= 2:
                        g.wait_ge(st[b].h, st[b].b + 16 * (i // 2))
                    g.dma_start(outd[128 * i:128 * (i + 1), :], ob[b][:]).then_inc(st[b].h, 16)
                g.wait_ge(st[0].h, st[0].b + 16 * (NT // 2))
                g.wait_ge(st[1].h, st[1].b + 16 * (NT // 2))

            @blk.tensor
            def _(t):
                t.wait_ge(vc.h, vcb + NC)
                for i in range(NT):
                    t.wait_ge(xl[i % 2].h, xl[i % 2].b + xrec[i])
                    if i >= 1:
                        t.wait_ge(vc.h, vcb + NC + NE * i)
                    for c in range(NC):
                        for hc in range(NE):
                            k = i * NE + hc
                            ins = t.matmul(pj[k % 8][:],
                                           lb[i % 2][:, 128 * c:128 * (c + 1)],
                                           wsb[:, E * c + 512 * hc:E * c + 512 * (hc + 1)],
                                           start=(c == 0), stop=(c == NC - 1))
                            if c == NC - 1:
                                ins.then_inc(mm.h, 1)

        ld[0].b += 16 * ((NC + 1) // 2)
        ld[1].b += 16 * (NC // 2)
        xl[0].b += xtot[0]
        xl[1].b += xtot[1]
        if res_name is not None:
            rl[0].b += 16 * (NT // 2)
            rl[1].b += 16 * (NT // 2)
        st[0].b += 16 * (NT // 2)
        st[1].b += 16 * (NT // 2)
        mm.b += NT * NE
        assert vc.b == vcb + NC + NE * NT
        pc.close()

    def ph_scores0(self, cdec):
        nc = self.nc
        d = self.dram
        qT, kT, psp = d["qT"], d["kT"], d["psp"]
        NST = S // 512
        pc = ExitStack()
        qts = [self.sbuf([128, 512 * NH], BF16, ctx=pc) for _ in range(2)]
        ktb = [self.sbuf([128, 16 * 128], BF16, ctx=pc) for _ in range(2)]
        ob = [self.sbuf([128, 512], BF16, ctx=pc) for _ in range(2)]
        pj = [self.psum([128, 512], F32, ctx=pc) for _ in range(2)]
        ql = [self.ctr("ld0"), self.ctr("ld1")]
        kl = [self.ctr("ld2"), self.ctr("ld3")]
        st = [self.ctr("st0"), self.ctr("st1")]
        mm = self.ctr("mm")
        vc = self.ctr("vch")
        vcb, mmb = vc.b, mm.b
        pairs = []
        for s_ in range(NST):
            for j in range(4 * s_ + 4):
                pairs.append((s_, j))
        qtot, qrec = [0, 0], {}
        ktot, krec = [0, 0], {}

        with nc.Block() as blk:
            @blk.sync
            def _(sync):
                qT3 = qT.rearrange("(c p) n -> p c n", p=128)
                for s_ in range(NST):
                    if s_ >= 2:
                        done = sum(1 for (ss2, _) in pairs if ss2 <= s_ - 1)
                        sync.wait_ge(mm.h, mmb + done)
                    qtot[s_ % 2] += self.dmaN(sync, ql[s_ % 2].h,
                                              qts[s_ % 2][:].rearrange("p (c n) -> p c n", n=512),
                                              qT3[:, :, 512 * s_:512 * (s_ + 1)])
                    qrec[s_] = qtot[s_ % 2]
                sync.wait_ge(ql[0].h, ql[0].b + qtot[0])
                sync.wait_ge(ql[1].h, ql[1].b + qtot[1])

            @blk.scalar
            def _(s):
                kT3 = kT.rearrange("(c p) n -> p c n", p=128)
                for idx, (s_, j) in enumerate(pairs):
                    if idx >= 2:
                        s.wait_ge(mm.h, mmb + idx - 1)
                    ktot[idx % 2] += self.dmaN(s, kl[idx % 2].h,
                                               ktb[idx % 2][:].rearrange("p (c n) -> p c n", n=128),
                                               kT3[:, :, 128 * j:128 * (j + 1)])
                    krec[idx] = ktot[idx % 2]
                s.wait_ge(kl[0].h, kl[0].b + ktot[0])
                s.wait_ge(kl[1].h, kl[1].b + ktot[1])

            @blk.vector
            def _(v):
                v = self.chained(v, "vch")
                for idx, (s_, j) in enumerate(pairs):
                    v.wait_ge(mm.h, mmb + idx + 1)
                    b = idx % 2
                    if idx >= 2:
                        v.wait_ge(st[b].h, st[b].b + 16 * (idx // 2))
                    for qc in range(4):
                        it = 4 * s_ + qc
                        dst = ob[b][:, 128 * qc:128 * (qc + 1)]
                        ps = pj[b][:, 128 * qc:128 * (qc + 1)]
                        if j > it:
                            v.memset(dst, 0.0)
                        elif j == it:
                            v.tensor_tensor(dst, ps, self.utmaskb[:], ALU.mult)
                        else:
                            v.tensor_scalar(dst, ps, float(cdec[it - j]), None, ALU.mult)

            @blk.gpsimd
            def _(g):
                for idx, (s_, j) in enumerate(pairs):
                    b = idx % 2
                    g.wait_ge(vc.h, vcb + 4 * (idx + 1))
                    if idx >= 2:
                        g.wait_ge(st[b].h, st[b].b + 16 * (idx // 2))
                    g.dma_start(psp[128 * j:128 * (j + 1), 512 * s_:512 * (s_ + 1)],
                                ob[b][:]).then_inc(st[b].h, 16)
                g.wait_ge(st[0].h, st[0].b + 16 * ((len(pairs) + 1) // 2))
                g.wait_ge(st[1].h, st[1].b + 16 * (len(pairs) // 2))

            @blk.tensor
            def _(t):
                for idx, (s_, j) in enumerate(pairs):
                    b = idx % 2
                    t.wait_ge(ql[s_ % 2].h, ql[s_ % 2].b + qrec[s_])
                    t.wait_ge(kl[b].h, kl[b].b + krec[idx])
                    if idx >= 2:
                        t.wait_ge(vc.h, vcb + 4 * (idx - 1))
                    for c in range(NH):
                        ins = t.matmul(pj[b][:],
                                       ktb[b][:, 128 * c:128 * (c + 1)],
                                       qts[s_ % 2][:, 512 * c:512 * (c + 1)],
                                       start=(c == 0), stop=(c == NH - 1))
                    ins.then_inc(mm.h, 1)

        npair = len(pairs)
        ql[0].b += qtot[0]
        ql[1].b += qtot[1]
        kl[0].b += ktot[0]
        kl[1].b += ktot[1]
        st[0].b += 16 * ((npair + 1) // 2)
        st[1].b += 16 * (npair // 2)
        mm.b += npair
        assert vc.b == vcb + 4 * npair
        pc.close()

    def ph_pv0(self):
        nc = self.nc
        d = self.dram
        psp, v0, x0, r1 = d["psp"], d["v0"], d["x0"], d["r1"]
        pc = ExitStack()
        vsb = self.sbuf([128, NT * 2048], BF16, ctx=pc)
        pb = [self.sbuf([128, NT * 128], BF16, ctx=pc) for _ in range(2)]
        xb = [self.sbuf([128, 2048], F32, ctx=pc) for _ in range(2)]
        ob = [self.sbuf([128, 2048], F32, ctx=pc) for _ in range(2)]
        pj = [self.psum([128, 512], F32, ctx=pc) for _ in range(8)]
        vl = self.ctr("ld0")
        plq = [self.ctr("ld2"), self.ctr("ld3")]
        rl = [self.ctr("ld4"), self.ctr("ld5")]
        st = [self.ctr("st0"), self.ctr("st1")]
        mm = self.ctr("mm")
        vc = self.ctr("vch")
        vcb, mmb = vc.b, mm.b
        steps = [(i, j) for i in range(NT) for j in range(i + 1)]
        ptot, prec = [0, 0], {}

        with nc.Block() as blk:
            @blk.sync
            def _(sync):
                for t_ in range(NT):
                    sync.dma_start(vsb[:, 2048 * t_:2048 * (t_ + 1)],
                                   v0[128 * t_:128 * (t_ + 1), :]).then_inc(vl.h, 16)
                psp3 = psp.rearrange("(j p) n -> p j n", p=128)
                for i in range(NT):
                    if i >= 2:
                        sync.wait_ge(mm.h, mmb + steps.index((i - 1, i - 1)))
                    ptot[i % 2] += self.dmaN(sync, plq[i % 2].h,
                                             pb[i % 2][:].rearrange("p (j n) -> p j n", n=128)[:, 0:i + 1, :],
                                             psp3[:, 0:i + 1, 128 * i:128 * (i + 1)])
                    prec[i] = ptot[i % 2]
                sync.wait_ge(plq[0].h, plq[0].b + ptot[0])
                sync.wait_ge(plq[1].h, plq[1].b + ptot[1])

            @blk.scalar
            def _(s):
                for i in range(NT):
                    if i >= 2:
                        s.wait_ge(vc.h, vcb + 4 * (i - 1))
                    s.dma_start(xb[i % 2][:], x0[128 * i:128 * (i + 1), :]).then_inc(rl[i % 2].h, 16)
                s.wait_ge(rl[0].h, rl[0].b + 16 * (NT // 2))
                s.wait_ge(rl[1].h, rl[1].b + 16 * (NT // 2))

            @blk.vector
            def _(v):
                v = self.chained(v, "vch")
                for i in range(NT):
                    b = i % 2
                    v.wait_ge(mm.h, mmb + sum(1 for (i2, _) in steps if i2 <= i))
                    v.wait_ge(rl[b].h, rl[b].b + 16 * (i // 2 + 1))
                    if i >= 2:
                        v.wait_ge(st[b].h, st[b].b + 16 * (i // 2))
                    for hc in range(4):
                        v.tensor_add(ob[b][:, 512 * hc:512 * (hc + 1)],
                                     pj[(i % 2) * 4 + hc][:], xb[b][:, 512 * hc:512 * (hc + 1)])

            @blk.gpsimd
            def _(g):
                for i in range(NT):
                    b = i % 2
                    g.wait_ge(vc.h, vcb + 4 * (i + 1))
                    if i >= 2:
                        g.wait_ge(st[b].h, st[b].b + 16 * (i // 2))
                    g.dma_start(r1[128 * i:128 * (i + 1), :], ob[b][:]).then_inc(st[b].h, 16)
                g.wait_ge(st[0].h, st[0].b + 16 * (NT // 2))
                g.wait_ge(st[1].h, st[1].b + 16 * (NT // 2))

            @blk.tensor
            def _(t):
                t.wait_ge(vl.h, vl.b + 16 * NT)
                for idx, (i, j) in enumerate(steps):
                    if j == 0:
                        t.wait_ge(plq[i % 2].h, plq[i % 2].b + prec[i])
                        if i >= 2:
                            t.wait_ge(vc.h, vcb + 4 * (i - 1))
                    for hc in range(4):
                        ins = t.matmul(pj[(i % 2) * 4 + hc][:],
                                       pb[i % 2][:, 128 * j:128 * (j + 1)],
                                       vsb[:, 2048 * j + 512 * hc:2048 * j + 512 * (hc + 1)],
                                       start=(j == 0), stop=(j == i))
                    ins.then_inc(mm.h, 1)

        nsteps = len(steps)
        vl.b += 16 * NT
        plq[0].b += ptot[0]
        plq[1].b += ptot[1]
        rl[0].b += 16 * (NT // 2)
        rl[1].b += 16 * (NT // 2)
        st[0].b += 16 * (NT // 2)
        st[1].b += 16 * (NT // 2)
        mm.b += nsteps
        assert vc.b == vcb + 4 * NT
        pc.close()

    def ph_projT(self, w_name, out_name, ne_tiles, xpos=None, scale=1.0, col_off=0):
        nc = self.nc
        d = self.dram
        w, outd, xnT = d[w_name], d[out_name], d["xnT"]
        E = ne_tiles * 128
        NST = S // 512
        pc = ExitStack()
        GS = 4 if NST % 4 == 0 else (2 if NST % 2 == 0 else 1)
        NXB = 2 if xpos is not None else GS
        wsb = self.sbuf([128, NH * E], BF16, ctx=pc)
        wtmp = [self.sbuf([128, E], F32, ctx=pc) for _ in range(2)]
        xts = [self.sbuf([128, 512 * NH], BF16, ctx=pc) for _ in range(NXB)]
        ob = [self.sbuf([128, 512], BF16, ctx=pc) for _ in range(4)]
        pj = [self.psum([128, 512], F32, ctx=pc) for _ in range(8)]
        ld = [self.ctr("ld0"), self.ctr("ld1")]
        xl = [self.ctr("ld2"), self.ctr("ld3"), self.ctr("ld4"), self.ctr("ld5")][:NXB]
        st = [self.ctr("st0"), self.ctr("st1"), self.ctr("st2"), self.ctr("st3")]
        mm = self.ctr("mm")
        vc, sc = self.ctr("vch"), self.ctr("sch")
        vcb, scb, mmb = vc.b, sc.b, mm.b
        if xpos is not None:
            qraw = self.sbuf([128, 512 * 16], F32, ctx=pc)
            tcos = [self.sbuf([128, 512], F32, ctx=pc) for _ in range(2)]
            tsin = [self.sbuf([128, 512], F32, ctx=pc) for _ in range(2)]
            tmp1 = self.sbuf([128, 512], F32, ctx=pc)
            tmp2 = self.sbuf([128, 512], F32, ctx=pc)
            tl = [self.ctr("tld0"), self.ctr("tld1")]
        npairs = 8 * NST
        nk = NST * ne_tiles
        xtot = [0] * NXB
        xrec = {}

        with nc.Block() as blk:
            @blk.sync
            def _(sync):
                for c in range(NH):
                    if c >= 2:
                        sync.wait_ge(vc.h, vcb + c - 1)
                    sync.dma_start(wtmp[c % 2][:],
                                   w[128 * c:128 * (c + 1), col_off:col_off + E]).then_inc(ld[c % 2].h, 16)
                xnT3 = xnT.rearrange("(c p) n -> p c n", p=128)
                for s_ in range(NST):
                    sl_ = s_ % NXB
                    if xpos is not None and s_ >= 2:
                        sync.wait_ge(mm.h, mmb + (s_ - 1) * ne_tiles)
                    if xpos is None and s_ >= GS:
                        sync.wait_ge(mm.h, mmb + (s_ // GS) * ne_tiles * GS)
                    xtot[sl_] += self.dmaN(sync, xl[sl_].h,
                                           xts[sl_][:].rearrange("p (c n) -> p c n", n=512),
                                           xnT3[:, :, 512 * s_:512 * (s_ + 1)])
                    xrec[s_] = xtot[sl_]
                    if xpos is not None:
                        qct, qst = d[xpos[0]], d[xpos[1]]
                        for p in range(8):
                            j = s_ * 8 + p
                            if j >= 2:
                                sync.wait_ge(vc.h, vcb + NH + 6 * (j - 2) + 5)
                            sync.dma_start(tcos[j % 2][:], qct[128 * p:128 * (p + 1), 512 * s_:512 * (s_ + 1)]).then_inc(tl[j % 2].h, 16)
                            sync.dma_start(tsin[j % 2][:], qst[128 * p:128 * (p + 1), 512 * s_:512 * (s_ + 1)]).then_inc(tl[j % 2].h, 16)
                for q in range(NXB):
                    sync.wait_ge(xl[q].h, xl[q].b + xtot[q])
                sync.wait_ge(ld[0].h, ld[0].b + 16 * ((NH + 1) // 2))
                sync.wait_ge(ld[1].h, ld[1].b + 16 * (NH // 2))

            @blk.vector
            def _(v):
                v = self.chained(v, "vch")
                for c in range(NH):
                    v.wait_ge(ld[c % 2].h, ld[c % 2].b + 16 * (c // 2 + 1))
                    v.tensor_copy(wsb[:, E * c:E * (c + 1)], wtmp[c % 2][:])
                if xpos is None:
                    for sg in range(NST // GS):
                        for et in range(ne_tiles):
                            for s4 in range(GS):
                                k = (sg * ne_tiles + et) * GS + s4
                                v.wait_ge(mm.h, mmb + k + 1)
                                if k >= 4:
                                    v.wait_ge(st[k % 4].h, st[k % 4].b + 16 * (k // 4))
                                v.tensor_scalar(ob[k % 4][:], pj[(et % 2) * GS + s4][:],
                                                scale, None, ALU.mult)
                else:
                    for s_ in range(NST):
                        for p in range(8):
                            j = s_ * 8 + p
                            v.wait_ge(sc.h, scb + 16 * s_ + 8 + p + 1)
                            v.wait_ge(tl[j % 2].h, tl[j % 2].b + 32 * (j // 2 + 1))
                            qe = qraw[:, 512 * p:512 * (p + 1)]
                            qo = qraw[:, 512 * (8 + p):512 * (9 + p)]
                            co, si = tcos[j % 2][:], tsin[j % 2][:]
                            if j >= 2:
                                v.wait_ge(st[j % 2].h, st[j % 2].b + 32 * (j // 2))
                            v.tensor_tensor(tmp1[:], qe, co, ALU.mult)
                            v.tensor_tensor(tmp2[:], qo, si, ALU.mult)
                            v.tensor_tensor(ob[(2 * j) % 4][:], tmp1[:], tmp2[:], ALU.subtract)
                            v.tensor_tensor(tmp1[:], qo, co, ALU.mult)
                            v.tensor_tensor(tmp2[:], qe, si, ALU.mult)
                            v.tensor_tensor(ob[(2 * j + 1) % 4][:], tmp1[:], tmp2[:], ALU.add)

            @blk.scalar
            def _(s):
                if xpos is None:
                    return
                s = self.chained(s, "sch")
                for s_ in range(NST):
                    if s_ >= 1:
                        s.wait_ge(vc.h, vcb + NH + 6 * 8 * s_)
                    for et in range(ne_tiles):
                        k = s_ * ne_tiles + et
                        s.wait_ge(mm.h, mmb + k + 1)
                        s.copy(qraw[:, 512 * et:512 * (et + 1)], pj[k % 4][:])

            @blk.gpsimd
            def _(g):
                if xpos is None:
                    for sg in range(NST // GS):
                        for et in range(ne_tiles):
                            for s4 in range(GS):
                                s_ = GS * sg + s4
                                k = (sg * ne_tiles + et) * GS + s4
                                g.wait_ge(vc.h, vcb + NH + k + 1)
                                if k >= 4:
                                    g.wait_ge(st[k % 4].h, st[k % 4].b + 16 * (k // 4))
                                g.dma_start(outd[128 * et:128 * (et + 1), 512 * s_:512 * (s_ + 1)],
                                            ob[k % 4][:]).then_inc(st[k % 4].h, 16)
                    nkk = nk * GS // GS
                    for q in range(4):
                        g.wait_ge(st[q].h, st[q].b + 16 * ((nkk - q + 3) // 4))
                else:
                    for s_ in range(NST):
                        for p in range(8):
                            j = s_ * 8 + p
                            g.wait_ge(vc.h, vcb + NH + 6 * (j + 1))
                            if j >= 2:
                                g.wait_ge(st[j % 2].h, st[j % 2].b + 32 * (j // 2))
                            g.dma_start(outd[128 * p:128 * (p + 1), 512 * s_:512 * (s_ + 1)],
                                        ob[(2 * j) % 4][:]).then_inc(st[j % 2].h, 16)
                            g.dma_start(outd[1024 + 128 * p:1024 + 128 * (p + 1), 512 * s_:512 * (s_ + 1)],
                                        ob[(2 * j + 1) % 4][:]).then_inc(st[j % 2].h, 16)
                    g.wait_ge(st[0].h, st[0].b + 32 * ((npairs + 1) // 2))
                    g.wait_ge(st[1].h, st[1].b + 32 * (npairs // 2))

            @blk.tensor
            def _(t):
                t.wait_ge(vc.h, vcb + NH)
                if xpos is not None:
                    for s_ in range(NST):
                        t.wait_ge(xl[s_ % NXB].h, xl[s_ % NXB].b + xrec[s_])
                        for et in range(ne_tiles):
                            k = s_ * ne_tiles + et
                            if k >= 4:
                                t.wait_ge(sc.h, scb + k - 3)
                            for c in range(NH):
                                ins = t.matmul(pj[k % 4][:],
                                               wsb[:, E * c + 128 * et:E * c + 128 * (et + 1)],
                                               xts[s_ % NXB][:, 512 * c:512 * (c + 1)],
                                               start=(c == 0), stop=(c == NH - 1))
                            ins.then_inc(mm.h, 1)
                else:
                    for sg in range(NST // GS):
                        for s4 in range(GS):
                            s_ = GS * sg + s4
                            t.wait_ge(xl[s_ % NXB].h, xl[s_ % NXB].b + xrec[s_])
                        for et in range(ne_tiles):
                            ke = sg * ne_tiles + et
                            if ke >= 2:
                                t.wait_ge(vc.h, vcb + NH + (ke - 2) * GS + GS)
                            for c in range(NH):
                                for s4 in range(GS):
                                    s_ = GS * sg + s4
                                    ins = t.matmul(pj[(et % 2) * GS + s4][:],
                                                   wsb[:, E * c + 128 * et:E * c + 128 * (et + 1)],
                                                   xts[s_ % NXB][:, 512 * c:512 * (c + 1)],
                                                   start=(c == 0), stop=(c == NH - 1))
                                    if c == NH - 1:
                                        ins.then_inc(mm.h, 1)

        ld[0].b += 16 * ((NH + 1) // 2)
        ld[1].b += 16 * (NH // 2)
        for q in range(NXB):
            xl[q].b += xtot[q]
        mm.b += nk
        if xpos is None:
            assert vc.b == vcb + NH + nk
            for q in range(4):
                st[q].b += 16 * ((nk - q + 3) // 4)
        else:
            assert vc.b == vcb + NH + 6 * npairs, (vc.b, vcb)
            assert sc.b == scb + nk
            st[0].b += 32 * ((npairs + 1) // 2)
            st[1].b += 32 * (npairs // 2)
            tl[0].b += 32 * ((npairs + 1) // 2)
            tl[1].b += 32 * (npairs // 2)
        pc.close()


    def ph_wprep(self, src_name, out_name, nbeta, ncols, hcross, natile,
                 su_name, mode, colrow_name=None, bounce_name="wbounce"):
        nc = self.nc
        d = self.dram
        srcd, outd, bounce = d[src_name], d[out_name], d[bounce_name]
        SBW = 2048
        nsbc = ncols // SBW
        scale = 1.0 / math.sqrt(128 * natile)
        pc = ExitStack()
        wtmp = [self.sbuf([128, SBW], F32, ctx=pc) for _ in range(4)]
        wtmpb = [self.sbuf([128, SBW], BF16, ctx=pc) for _ in range(4)]
        wbf = [self.sbuf([128, SBW], BF16, ctx=pc) for _ in range(2)]
        ob = [self.sbuf([128, SBW], BF16, ctx=pc) for _ in range(2)]
        pj = [self.psum([128, 512], F32, ctx=pc) for _ in range(8)]
        sucol = self.sbuf([128, natile], F32, ctx=pc)
        ld = [self.ctr("ld0"), self.ctr("ld1"), self.ctr("ld2"), self.ctr("ld3")]
        st = [self.ctr("st0"), self.ctr("st1")]
        sl = self.ctr("ld4")
        mm = self.ctr("mm")
        vc, sc = self.ctr("vch"), self.ctr("sch")
        if mode == "rowcol":
            colb = self.sbuf([128, ncols], F32, ctx=pc)
            crow = self.sbuf([1, ncols], F32, ctx=pc)

        def stage(srcap, dstap, lhsT, do_scale):
            bufs = wtmp if not do_scale else wtmpb
            nsb = nbeta * nsbc if not do_scale else natile * nsbc
            vcb, scb, mmb = vc.b, sc.b, mm.b
            moff = ncols // 512 if (do_scale and mode == "rowcol") else 0
            is_a = not do_scale
            vper = (1 + 4) if is_a else 4

            def vpos(k):
                return vcb + (moff if not is_a else 0) + vper * (k + 1)

            with nc.Block() as blk:
                @blk.sync
                def _(sync):
                    if do_scale and mode == "rowcol":
                        sync.dma_start(crow[:], d[colrow_name][:]).then_inc(sl.h, 16)
                    if is_a:
                        sync.dma_start(sucol[:], d[su_name][:]).then_inc(sl.h, 16)
                    for k in range(0, nsb, 2):
                        if k >= 4:
                            if is_a:
                                sync.wait_ge(vc.h, vpos(k - 4) - vper + 1)
                            else:
                                sync.wait_ge(mm.h, mmb + moff + 4 * (k - 4) + 4)
                        sync.dma_start(bufs[k % 4][:], srcap(k)).then_inc(ld[k % 4].h, 16)
                    sync.wait_ge(ld[0].h, ld[0].b + 16 * ((nsb + 3) // 4))
                    sync.wait_ge(ld[2].h, ld[2].b + 16 * ((nsb + 1) // 4))

                @blk.scalar
                def _(s):
                    for k in range(1, nsb, 2):
                        if k >= 4:
                            if is_a:
                                s.wait_ge(vc.h, vpos(k - 4) - vper + 1)
                            else:
                                s.wait_ge(mm.h, mmb + moff + 4 * (k - 4) + 4)
                        s.dma_start(bufs[k % 4][:], srcap(k)).then_inc(ld[k % 4].h, 16)
                    s.wait_ge(ld[1].h, ld[1].b + 16 * ((nsb + 2) // 4))
                    s.wait_ge(ld[3].h, ld[3].b + 16 * (nsb // 4))

                @blk.vector
                def _(v):
                    v = self.chained(v, "vch")
                    if do_scale and mode == "rowcol":
                        v.wait_ge(sl.h, sl.b + 16)
                        for ch in range(moff):
                            v.wait_ge(mm.h, mmb + ch + 1)
                            v.tensor_scalar(colb[:, 512 * ch:512 * (ch + 1)], pj[ch % 2][:],
                                            scale, None, ALU.mult)
                    for k in range(nsb):
                        a_ = k // nsbc
                        v.wait_ge(ld[k % 4].h, ld[k % 4].b + 16 * (k // 4 + 1))
                        if is_a:
                            v.tensor_copy(wbf[k % 2][:], bufs[k % 4][:])
                        if k >= 2:
                            v.wait_ge(st[k % 2].h, st[k % 2].b + 16 * (k // 2))
                        for q in range(4):
                            v.wait_ge(mm.h, mmb + moff + 4 * k + q + 1)
                            dst = ob[k % 2][:, 512 * q:512 * (q + 1)]
                            ps = pj[(k % 2) * 4 + q][:]
                            if not do_scale:
                                v.tensor_copy(dst, ps)
                            elif mode == "row":
                                v.tensor_scalar(dst, ps, sucol[:, a_:a_ + 1], scale,
                                                ALU.mult, ALU.mult)
                            else:
                                cs = colb[:, 512 * ((k % nsbc) * 4 + q):512 * ((k % nsbc) * 4 + q + 1)]
                                v.scalar_tensor_tensor(dst, ps, sucol[:, a_:a_ + 1], cs,
                                                       ALU.mult, ALU.mult)

                @blk.gpsimd
                def _(g):
                    for k in range(nsb):
                        g.wait_ge(vc.h, vpos(k))
                        if k >= 2:
                            g.wait_ge(st[k % 2].h, st[k % 2].b + 16 * (k // 2))
                        g.dma_start(dstap(k), ob[k % 2][:]).then_inc(st[k % 2].h, 16)
                    g.wait_ge(st[0].h, st[0].b + 16 * ((nsb + 1) // 2))
                    g.wait_ge(st[1].h, st[1].b + 16 * (nsb // 2))

                @blk.tensor
                def _(t):
                    if do_scale and mode == "rowcol":
                        t.wait_ge(sl.h, sl.b + 16 + (16 if is_a else 0))
                        for ch in range(moff):
                            if ch >= 2:
                                t.wait_ge(vc.h, vcb + ch - 1)
                            t.matmul(pj[ch % 2][:], self.onesrow[:],
                                     crow[:, 512 * ch:512 * (ch + 1)],
                                     start=True, stop=True).then_inc(mm.h, 1)
                        t.wait_ge(vc.h, vcb + moff)
                    if is_a:
                        t.wait_ge(sl.h, sl.b + 16)
                    for k in range(nsb):
                        if is_a:
                            t.wait_ge(vc.h, vcb + vper * k + 1)
                        else:
                            t.wait_ge(ld[k % 4].h, ld[k % 4].b + 16 * (k // 4 + 1))
                        if k >= 2:
                            t.wait_ge(vc.h, vpos(k - 2))
                        rhs = wbf[k % 2] if is_a else bufs[k % 4]
                        for q in range(4):
                            t.matmul(pj[(k % 2) * 4 + q][:], lhsT[:],
                                     rhs[:, 512 * q:512 * (q + 1)],
                                     start=True, stop=True).then_inc(mm.h, 1)

            for qq in range(4):
                ld[qq].b += 16 * ((nsb - qq + 3) // 4)
            st[0].b += 16 * ((nsb + 1) // 2)
            st[1].b += 16 * (nsb // 2)
            mm.b += moff + 4 * nsb
            sl.b += (16 if (do_scale and mode == "rowcol") else 0) + (16 if is_a else 0)
            assert vc.b == vpos(nsb - 1), (vc.b, vpos(nsb - 1))

        hx = self.h16x8b if hcross == 16 else self.h64x2b
        stage(lambda k: srcd[128 * (k // nsbc):128 * (k // nsbc + 1),
                             SBW * (k % nsbc):SBW * (k % nsbc + 1)],
              lambda k: bounce[128 * (k // nsbc):128 * (k // nsbc + 1),
                               SBW * (k % nsbc):SBW * (k % nsbc + 1)],
              hx, False)
        brd = bounce.rearrange("(p g) n -> g p n", g=natile)
        stage(lambda k: brd[k // nsbc, :, SBW * (k % nsbc):SBW * (k % nsbc + 1)],
              lambda k: outd[128 * (k // nsbc):128 * (k // nsbc + 1),
                             SBW * (k % nsbc):SBW * (k % nsbc + 1)],
              self.h128b, True)
        pc.close()

    def ph_wcast(self, src_name, out_name, nrows, ncols):
        nc = self.nc
        d = self.dram
        srcd, outd = d[src_name], d[out_name]
        CW = 2048
        nrt, nct = nrows // 128, ncols // CW
        pc = ExitStack()
        wtmp = [self.sbuf([128, CW], U8, ctx=pc) for _ in range(4)]
        ob = [self.sbuf([128, CW], BF16, ctx=pc) for _ in range(4)]
        ld = [self.ctr("ld0"), self.ctr("ld1"), self.ctr("ld2"), self.ctr("ld3")]
        st = [self.ctr("st0"), self.ctr("st1"), self.ctr("st2"), self.ctr("st3")]
        vc, sc = self.ctr("vch"), self.ctr("sch")
        vcb, scb = vc.b, sc.b
        nblk = nrt * nct

        def cpos(k):
            return (vcb if k % 2 == 0 else scb) + k // 2 + 1

        with nc.Block() as blk:
            @blk.sync
            def _(sync):
                for k in range(0, nblk, 2):
                    r, ch = k // nct, k % nct
                    if k >= 4:
                        sync.wait_ge(vc.h, vcb + (k - 4) // 2 + 1)
                    sync.dma_start(wtmp[k % 4][:],
                                   srcd[128 * r:128 * (r + 1), CW * ch:CW * (ch + 1)]).then_inc(ld[k % 4].h, 16)
                    if k >= 1:
                        kk = k - 1
                        r2, ch2 = kk // nct, kk % nct
                        sync.wait_ge(sc.h, cpos(kk))
                        if kk >= 4:
                            sync.wait_ge(st[kk % 4].h, st[kk % 4].b + 16 * (kk // 4))
                        sync.dma_start(outd[128 * r2:128 * (r2 + 1), CW * ch2:CW * (ch2 + 1)],
                                       ob[kk % 4][:]).then_inc(st[kk % 4].h, 16)
                if nblk >= 1:
                    kk = nblk - 1
                    r2, ch2 = kk // nct, kk % nct
                    sync.wait_ge(sc.h, cpos(kk))
                    if kk >= 4:
                        sync.wait_ge(st[kk % 4].h, st[kk % 4].b + 16 * (kk // 4))
                    sync.dma_start(outd[128 * r2:128 * (r2 + 1), CW * ch2:CW * (ch2 + 1)],
                                   ob[kk % 4][:]).then_inc(st[kk % 4].h, 16)
                sync.wait_ge(ld[0].h, ld[0].b + 16 * ((nblk + 3) // 4))
                sync.wait_ge(ld[2].h, ld[2].b + 16 * ((nblk + 1) // 4))
                sync.wait_ge(st[1].h, st[1].b + 16 * ((nblk + 2) // 4))
                sync.wait_ge(st[3].h, st[3].b + 16 * (nblk // 4))

            @blk.scalar
            def _(s):
                sch = self.chained(s, "sch")
                odds = list(range(1, nblk, 2))

                def dma_odd(k):
                    r, ch = k // nct, k % nct
                    s.dma_start(wtmp[k % 4][:],
                                srcd[128 * r:128 * (r + 1), CW * ch:CW * (ch + 1)]).then_inc(ld[k % 4].h, 16)
                for k in odds[:2]:
                    dma_odd(k)
                for i, k in enumerate(odds):
                    sch.wait_ge(ld[k % 4].h, ld[k % 4].b + 16 * (k // 4 + 1))
                    if k >= 4:
                        sch.wait_ge(st[k % 4].h, st[k % 4].b + 16 * (k // 4))
                    sch.activation(ob[k % 4][:], wtmp[k % 4][:], AF.Copy,
                                   bias=-128.0)
                    if i + 2 < len(odds):
                        s.wait_ge(sc.h, scb + i + 1)
                        dma_odd(odds[i + 2])
                s.wait_ge(ld[1].h, ld[1].b + 16 * ((nblk + 2) // 4))
                s.wait_ge(ld[3].h, ld[3].b + 16 * (nblk // 4))

            @blk.vector
            def _(v):
                vch = self.chained(v, "vch")
                for k in range(0, nblk, 2):
                    vch.wait_ge(ld[k % 4].h, ld[k % 4].b + 16 * (k // 4 + 1))
                    if k >= 4:
                        vch.wait_ge(st[k % 4].h, st[k % 4].b + 16 * (k // 4))
                    vch.tensor_scalar(ob[k % 4][:], wtmp[k % 4][:], 128.0, None, ALU.subtract)

            @blk.gpsimd
            def _(g):
                for k in range(0, nblk, 2):
                    r, ch = k // nct, k % nct
                    g.wait_ge(vc.h, cpos(k))
                    if k >= 4:
                        g.wait_ge(st[k % 4].h, st[k % 4].b + 16 * (k // 4))
                    g.dma_start(outd[128 * r:128 * (r + 1), CW * ch:CW * (ch + 1)],
                                ob[k % 4][:]).then_inc(st[k % 4].h, 16)
                g.wait_ge(st[0].h, st[0].b + 16 * ((nblk + 3) // 4))
                g.wait_ge(st[2].h, st[2].b + 16 * ((nblk + 1) // 4))
        for q in range(4):
            ld[q].b += 16 * ((nblk - q + 3) // 4)
            st[q].b += 16 * ((nblk - q + 3) // 4)
        assert vc.b == vcb + (nblk + 1) // 2 and sc.b == scb + nblk // 2
        pc.close()

    def ph_ffn_up(self, hup_name, hgate_name, scu_name, scg_name):
        nc = self.nc
        d = self.dram
        hup, hgate, t0, xnT = d[hup_name], d[hgate_name], d["t0"], d["xnT"]
        pc = ExitStack()
        xth = self.sbuf([128, NH * (S // 2)], BF16, ctx=pc)
        wub = [self.sbuf([128, NH * 512], BF16, ctx=pc) for _ in range(2)]
        wgb = [self.sbuf([128, NH * 512], BF16, ctx=pc) for _ in range(2)]
        scu = self.sbuf([128, NI], F32, ctx=pc)
        scg = self.sbuf([128, NI], F32, ctx=pc)
        gs = [self.sbuf([128, 512], F32, ctx=pc) for _ in range(2)]
        ob = [self.sbuf([128, 512], BF16, ctx=pc) for _ in range(2)]
        pu = [self.psum([128, 512], F32, ctx=pc) for _ in range(4)]
        pg = [self.psum([128, 512], F32, ctx=pc) for _ in range(4)]
        xl = self.ctr("ld0")
        sl = self.ctr("ld1")
        wl = [self.ctr("ld2"), self.ctr("ld3")]
        st = [self.ctr("st0"), self.ctr("st1")]
        mm = self.ctr("mm")
        vc, sc = self.ctr("vch"), self.ctr("sch")
        vcb, scb, mmb = vc.b, sc.b, mm.b
        NSUP = 16
        TPH = S // 2
        NSTH = TPH // 512
        KPH = NSUP * NSTH * 4
        wtot, wrec = [0, 0], {}
        with nc.Block() as blk:
            @blk.sync
            def _(sync):
                hup3 = hup.rearrange("(c p) n -> p c n", p=128)
                hgate3 = hgate.rearrange("(c p) n -> p c n", p=128)
                sync.dma_start(scu[:], d[scu_name][:]).then_inc(sl.h, 16)
                sync.dma_start(scg[:], d[scg_name][:]).then_inc(sl.h, 16)
                for half in range(2):
                    if half == 1:
                        sync.wait_ge(mm.h, mmb + KPH)
                    for c in range(NH):
                        sync.dma_start(xth[:, TPH * c:TPH * (c + 1)],
                                       xnT[128 * c:128 * (c + 1), TPH * half:TPH * (half + 1)]).then_inc(xl.h, 16)
                    for isup in range(NSUP):
                        g_ = half * NSUP + isup
                        if g_ >= 2:
                            sync.wait_ge(mm.h, mmb + 4 * NSTH * (g_ - 1))
                        wtot[isup % 2] += self.dmaN(sync, wl[isup % 2].h,
                                                    wub[isup % 2][:].rearrange("p (c n) -> p c n", n=512),
                                                    hup3[:, :, 512 * isup:512 * (isup + 1)])
                        wtot[isup % 2] += self.dmaN(sync, wl[isup % 2].h,
                                                    wgb[isup % 2][:].rearrange("p (c n) -> p c n", n=512),
                                                    hgate3[:, :, 512 * isup:512 * (isup + 1)])
                        wrec[g_] = wtot[isup % 2]
                sync.wait_ge(xl.h, xl.b + 16 * 2 * NH)
                sync.wait_ge(sl.h, sl.b + 32)
                sync.wait_ge(wl[0].h, wl[0].b + wtot[0])
                sync.wait_ge(wl[1].h, wl[1].b + wtot[1])

            @blk.tensor
            def _(t):
                for half in range(2):
                    t.wait_ge(xl.h, xl.b + 16 * NH * (half + 1))
                    for isup in range(NSUP):
                        g_ = half * NSUP + isup
                        t.wait_ge(wl[isup % 2].h, wl[isup % 2].b + wrec[g_])
                        for st_ in range(NSTH):
                            for it in range(4):
                                k = g_ * 4 * NSTH + st_ * 4 + it
                                if k >= 4:
                                    t.wait_ge(vc.h, vcb + k - 3)
                                    t.wait_ge(sc.h, scb + k - 3)
                                for c in range(NH):
                                    t.matmul(pu[k % 4][:],
                                             wub[isup % 2][:, 512 * c + 128 * it:512 * c + 128 * (it + 1)],
                                             xth[:, TPH * c + 512 * st_:TPH * c + 512 * (st_ + 1)],
                                             start=(c == 0), stop=(c == NH - 1))
                                for c in range(NH):
                                    ins = t.matmul(pg[k % 4][:],
                                                   wgb[isup % 2][:, 512 * c + 128 * it:512 * c + 128 * (it + 1)],
                                                   xth[:, TPH * c + 512 * st_:TPH * c + 512 * (st_ + 1)],
                                                   start=(c == 0), stop=(c == NH - 1))
                                ins.then_inc(mm.h, 1)

            @blk.scalar
            def _(s):
                s = self.chained(s, "sch")
                s.wait_ge(sl.h, sl.b + 32)
                for half in range(2):
                    for isup in range(NSUP):
                        for st_ in range(NSTH):
                            for it in range(4):
                                k = (half * NSUP + isup) * 4 * NSTH + st_ * 4 + it
                                i_abs = isup * 4 + it
                                s.wait_ge(mm.h, mmb + k + 1)
                                if k >= 2:
                                    s.wait_ge(vc.h, vcb + k - 1)
                                s.activation(gs[k % 2][:], pg[k % 4][:], AF.Silu,
                                             scale=scg[:, i_abs:i_abs + 1])

            @blk.vector
            def _(v):
                v = self.chained(v, "vch")
                v.wait_ge(sl.h, sl.b + 32)
                for half in range(2):
                    for isup in range(NSUP):
                        for st_ in range(NSTH):
                            for it in range(4):
                                k = (half * NSUP + isup) * 4 * NSTH + st_ * 4 + it
                                i_abs = isup * 4 + it
                                v.wait_ge(sc.h, scb + k + 1)
                                if k >= 2:
                                    v.wait_ge(st[k % 2].h, st[k % 2].b + 16 * (k // 2))
                                v.scalar_tensor_tensor(ob[k % 2][:], pu[k % 4][:],
                                                       scu[:, i_abs:i_abs + 1], gs[k % 2][:],
                                                       ALU.mult, ALU.mult)

            @blk.gpsimd
            def _(g):
                for half in range(2):
                    for isup in range(NSUP):
                        for st_ in range(NSTH):
                            for it in range(4):
                                k = (half * NSUP + isup) * 4 * NSTH + st_ * 4 + it
                                i_abs = isup * 4 + it
                                st_abs = half * NSTH + st_
                                g.wait_ge(vc.h, vcb + k + 1)
                                if k >= 2:
                                    g.wait_ge(st[k % 2].h, st[k % 2].b + 16 * (k // 2))
                                g.dma_start(t0[128 * i_abs:128 * (i_abs + 1), 512 * st_abs:512 * (st_abs + 1)],
                                            ob[k % 2][:]).then_inc(st[k % 2].h, 16)
                g.wait_ge(st[0].h, st[0].b + 16 * KPH)
                g.wait_ge(st[1].h, st[1].b + 16 * KPH)

        xl.b += 32 * NH
        sl.b += 32
        wl[0].b += wtot[0]
        wl[1].b += wtot[1]
        st[0].b += 16 * KPH
        st[1].b += 16 * KPH
        mm.b += 2 * KPH
        assert vc.b == vcb + 2 * KPH and sc.b == scb + 2 * KPH
        pc.close()

    def ph_ffn_down(self, hdn_name, res_name, out_name, coln_name=None):
        nc = self.nc
        d = self.dram
        hdn, res, outd, t0 = d[hdn_name], d[res_name], d[out_name], d["t0"]
        pc = ExitStack()
        wdb = self.sbuf([128, NI * 1024], BF16, ctx=pc)
        tb = [self.sbuf([128, NI * 128], BF16, ctx=pc) for _ in range(2)]
        rb = [self.sbuf([128, 1024], F32, ctx=pc) for _ in range(2)]
        ob = [self.sbuf([128, 1024], F32, ctx=pc) for _ in range(2)]
        pj = [self.psum([128, 512], F32, ctx=pc) for _ in range(4)]
        scaled = coln_name is not None
        if scaled:
            pjr = [self.psum([128, 512], F32, ctx=pc) for _ in range(2)]
            colb = self.sbuf([128, 1024], F32, ctx=pc)
            crow = self.sbuf([1, H], F32, ctx=pc)
        wl = self.ctr("ld0")
        tl = [self.ctr("ld2"), self.ctr("ld3")]
        rl = [self.ctr("ld4"), self.ctr("ld5")]
        st = [self.ctr("st0"), self.ctr("st1")]
        mm = self.ctr("mm")
        vc = self.ctr("vch")
        vcb, mmb = vc.b, mm.b
        nevt = 4 if scaled else 2
        nrep = 2 if scaled else 0

        ttot, trec = [0, 0], {}
        t03b = t0.rearrange("(c p) n -> p c n", p=128)

        def posv(k):
            return vcb + nrep * (k // NT + 1) + nevt * (k + 1)

        def posm(k):
            return mmb + nrep * (k // NT + 1) + (k + 1)

        with nc.Block() as blk:
            @blk.sync
            def _(sync):
                t03 = t0.rearrange("(c p) n -> p c n", p=128)
                if scaled:
                    sync.dma_start(crow[:], d[coln_name][:]).then_inc(wl.h, 16)
                for half in range(2):
                    if half == 1:
                        sync.wait_ge(mm.h, posm(NT - 1))
                    for c in range(NI):
                        sync.dma_start(wdb[:, 1024 * c:1024 * (c + 1)],
                                       hdn[128 * c:128 * (c + 1), 1024 * half:1024 * (half + 1)]).then_inc(wl.h, 16)
                    for i in range(NT):
                        k = half * NT + i
                        if k % 2 == 1:
                            continue
                        if k >= 2:
                            sync.wait_ge(mm.h, posm(k - 2))
                        ttot[0] += self.dmaN(sync, tl[0].h,
                                             tb[0][:].rearrange("p (c n) -> p c n", n=128),
                                             t03[:, :, 128 * i:128 * (i + 1)])
                        trec[k] = ttot[0]
                sync.wait_ge(tl[0].h, tl[0].b + ttot[0])
                sync.wait_ge(wl.h, wl.b + (16 if scaled else 0) + 16 * 2 * NI)

            @blk.scalar
            def _(s):
                for half in range(2):
                    for i in range(NT):
                        k = half * NT + i
                        if k % 2 == 1:
                            if k >= 2:
                                s.wait_ge(mm.h, posm(k - 2))
                            ttot[1] += self.dmaN(s, tl[1].h,
                                                 tb[1][:].rearrange("p (c n) -> p c n", n=128),
                                                 t03b[:, :, 128 * i:128 * (i + 1)])
                            trec[k] = ttot[1]
                        if k >= 2:
                            s.wait_ge(vc.h, posv(k - 2))
                        s.dma_start(rb[k % 2][:],
                                    res[128 * i:128 * (i + 1), 1024 * half:1024 * (half + 1)]).then_inc(rl[k % 2].h, 16)
                s.wait_ge(tl[1].h, tl[1].b + ttot[1])
                s.wait_ge(rl[0].h, rl[0].b + 16 * NT)
                s.wait_ge(rl[1].h, rl[1].b + 16 * NT)

            @blk.tensor
            def _(t):
                for half in range(2):
                    t.wait_ge(wl.h, wl.b + (16 if scaled else 0) + 16 * NI * (half + 1))
                    if scaled:
                        for ch in range(2):
                            t.matmul(pjr[ch][:], self.onesrow[:],
                                     crow[:, 1024 * half + 512 * ch:1024 * half + 512 * (ch + 1)],
                                     start=True, stop=True).then_inc(mm.h, 1)
                    for i in range(NT):
                        k = half * NT + i
                        t.wait_ge(tl[k % 2].h, tl[k % 2].b + trec[k])
                        if k >= 2:
                            t.wait_ge(vc.h, posv(k - 2))
                        for c in range(NI):
                            for hc in range(2):
                                ins = t.matmul(pj[(k % 2) * 2 + hc][:],
                                               tb[k % 2][:, 128 * c:128 * (c + 1)],
                                               wdb[:, 1024 * c + 512 * hc:1024 * c + 512 * (hc + 1)],
                                               start=(c == 0), stop=(c == NI - 1))
                        ins.then_inc(mm.h, 1)

            @blk.vector
            def _(v):
                v = self.chained(v, "vch")


                for half in range(2):
                    if scaled:
                        for ch in range(2):
                            v.wait_ge(mm.h, mmb + nrep * half + (NT * half) + ch + 1)
                            v.tensor_copy(colb[:, 512 * ch:512 * (ch + 1)], pjr[ch][:])
                    for i in range(NT):
                        k = half * NT + i
                        v.wait_ge(mm.h, posm(k))
                        v.wait_ge(rl[k % 2].h, rl[k % 2].b + 16 * (k // 2 + 1))
                        if k >= 2:
                            v.wait_ge(st[k % 2].h, st[k % 2].b + 16 * (k // 2))
                        if not scaled:
                            for hc in range(2):
                                v.tensor_add(ob[k % 2][:, 512 * hc:512 * (hc + 1)],
                                             pj[(k % 2) * 2 + hc][:],
                                             rb[k % 2][:, 512 * hc:512 * (hc + 1)])
                        else:
                            for hc in range(2):
                                v.tensor_tensor(ob[k % 2][:, 512 * hc:512 * (hc + 1)],
                                                pj[(k % 2) * 2 + hc][:],
                                                colb[:, 512 * hc:512 * (hc + 1)], ALU.mult)
                            for hc in range(2):
                                v.tensor_add(ob[k % 2][:, 512 * hc:512 * (hc + 1)],
                                             ob[k % 2][:, 512 * hc:512 * (hc + 1)],
                                             rb[k % 2][:, 512 * hc:512 * (hc + 1)])

            @blk.gpsimd
            def _(g):
                for half in range(2):
                    for i in range(NT):
                        k = half * NT + i
                        g.wait_ge(vc.h, posv(k))
                        if k >= 2:
                            g.wait_ge(st[k % 2].h, st[k % 2].b + 16 * (k // 2))
                        g.dma_start(outd[128 * i:128 * (i + 1), 1024 * half:1024 * (half + 1)],
                                    ob[k % 2][:]).then_inc(st[k % 2].h, 16)
                g.wait_ge(st[0].h, st[0].b + 16 * NT)
                g.wait_ge(st[1].h, st[1].b + 16 * NT)

        wl.b += (16 if scaled else 0) + 32 * NI
        tl[0].b += ttot[0]
        tl[1].b += ttot[1]
        rl[0].b += 16 * NT
        rl[1].b += 16 * NT
        st[0].b += 16 * NT
        st[1].b += 16 * NT
        mm.b += 2 * nrep + 2 * NT
        assert vc.b == posv(2 * NT - 1), (vc.b, posv(2 * NT - 1))
        pc.close()


    def ph_attn1(self):
        nc = self.nc
        d = self.dram
        qT, kT1, v1, ctx = d["qT"], d["kT1"], d["v1"], d["ctx"]
        NST = S // 512
        pc = ExitStack()
        ksb = self.sbuf([128, 4 * S], BF16, ctx=pc)
        vsb = self.sbuf([128, NT * 512], BF16, ctx=pc)
        qts = [self.sbuf([128, 16 * 512], BF16, ctx=pc) for _ in range(2)]
        pbr = self.sbuf([128, 4 * 1024], BF16, ctx=pc)
        sqb = self.sbuf([128, 16 * 512], BF16, ctx=pc)
        qn2 = self.sbuf([1, 16 * 512], F32, ctx=pc)
        knrow = self.sbuf([1, S], F32, ctx=pc)
        mt = self.sbuf([1, 64], F32, ctx=pc)
        cbias = self.sbuf([128, 8], F32, ctx=pc)
        rrow = self.sbuf([1, 1024], F32, ctx=pc)
        rsb = self.sbuf([128, 1024], F32, ctx=pc)
        ob = [self.sbuf([128, 1024], BF16, ctx=pc) for _ in range(2)]
        psc = [self.psum([128, 1024], F32, ctx=pc) for _ in range(2)]
        pcs = [self.psum([128, 512], F32, ctx=pc) for _ in range(2)]
        patt = self.psum([128, 1024], F32, ctx=pc)
        kl, vl = self.ctr("ld0"), self.ctr("ld1")
        ql = [self.ctr("ld2"), self.ctr("ld3")]
        st = [self.ctr("st0"), self.ctr("st1")]
        mm = self.ctr("mm")
        vc, sc = self.ctr("vch"), self.ctr("sch")
        V0, S0, M0 = vc.b, sc.b, mm.b
        nj = lambda s_: 4 * s_ + 4
        NKCH = S // 512
        qtot, qrec, vstot = [0, 0], {}, [0]

        POS = {}
        vp, sp, mp = V0, S0, M0
        for g in range(4):
            vp += 1
            for ch in range(NKCH):
                mp += 1; POS[("kcs", g, ch)] = mp
                vp += 1; POS[("kev", g, ch)] = vp
            vp += 1; POS[("kmax", g)] = vp
            sp += 1; POS[("ksqrt", g)] = sp
        for s_ in range(NST):
            for h in range(16):
                vp += 1
            POS[("sqdone", s_)] = vp
            for h in range(16):
                mp += 1; POS[("qcs", s_, h)] = mp
                vp += 1; POS[("qev", s_, h)] = vp
                vp += 1; POS[("qmax", s_, h)] = vp
            for h in range(16):
                sp += 1; POS[("qsqrt", s_, h)] = sp
            for P in range(8):
                vp += 1
                vp += 1; POS[("negc", s_, P)] = vp
            for P in range(8):
                mp += 1; POS[("crepl", s_, P)] = mp
                vp += 1; POS[("cbias", s_, P)] = vp
            for P in range(8):
                n = nj(s_)
                for j in range(n):
                    mp += 1; POS[("sco", s_, P, j)] = mp
                    if j >= 1:
                        mp += 1; POS[("pv", s_, P, j - 1)] = mp
                mp += 1; POS[("cslast", s_, P)] = mp
                mp += 1; POS[("pv", s_, P, n - 1)] = mp
                mp += 2; POS[("repl", s_, P)] = mp
                for j in range(n):
                    sp += 1; POS[("exp", s_, P, j)] = sp
                for qc in range(4):
                    vp += 2 * qc + 2
                POS[("maskd", s_, P)] = vp
                vp += 2; POS[("recip", s_, P)] = vp
                vp += 1; POS[("rsb", s_, P)] = vp
                vp += 1; POS[("obtt", s_, P)] = vp
            POS[("stend", s_)] = mp
        VEND, SEND, MEND = vp, sp, mp

        def maskpos(s_, P, j):
            qc = j - 4 * s_
            base = POS[("cbias", s_, 7)] if False else None
            pref = sum(2 * q + 2 for q in range(qc + 1))
            return POS[("maskd", s_, P)] - sum(2 * q + 2 for q in range(4)) + pref

        with nc.Block() as blk:
            @blk.sync
            def _(sync):
                for g in range(4):
                    sync.dma_start(ksb[:, S * g:S * (g + 1)],
                                   kT1[128 * g:128 * (g + 1), :]).then_inc(kl.h, 16)
                vstot[0] = self.dmaN(sync, vl.h,
                                     vsb[:].rearrange("p (t n) -> p t n", n=512),
                                     v1.rearrange("(t p) n -> p t n", p=128))
                qT3 = qT.rearrange("(c p) n -> p c n", p=128)
                for s_ in range(NST):
                    if s_ >= 2:
                        sync.wait_ge(mm.h, POS[("stend", s_ - 1)])
                    qtot[s_ % 2] += self.dmaN(sync, ql[s_ % 2].h,
                                              qts[s_ % 2][:].rearrange("p (c n) -> p c n", n=512),
                                              qT3[:, :, 512 * s_:512 * (s_ + 1)])
                    qrec[s_] = qtot[s_ % 2]
                sync.wait_ge(ql[0].h, ql[0].b + qtot[0])
                sync.wait_ge(ql[1].h, ql[1].b + qtot[1])

            @blk.vector
            def _(v):
                v = self.chained(v, "vch")
                v.wait_ge(kl.h, kl.b + 64)
                for g in range(4):
                    if g >= 1:
                        v.wait_ge(mm.h, POS[("kcs", g - 1, NKCH - 1)])
                    v.tensor_tensor(sqb[:, 0:S], ksb[:, S * g:S * (g + 1)],
                                    ksb[:, S * g:S * (g + 1)], ALU.mult)
                    for ch in range(NKCH):
                        v.wait_ge(mm.h, POS[("kcs", g, ch)])
                        v.tensor_copy(knrow[:, 512 * ch:512 * (ch + 1)],
                                      pcs[(NKCH * g + ch) % 2][0:1, :])
                    v.reduce_max(mt[:, g:g + 1], knrow[:, 0:S], axis=AX.X)
                for s_ in range(NST):
                    v.wait_ge(ql[s_ % 2].h, ql[s_ % 2].b + qrec[s_])
                    if s_ >= 1:
                        v.wait_ge(mm.h, POS[("qcs", s_ - 1, 15)])
                    else:
                        v.wait_ge(mm.h, POS[("kcs", 3, NKCH - 1)])
                    for h in range(16):
                        v.tensor_tensor(sqb[:, 512 * h:512 * (h + 1)],
                                        qts[s_ % 2][:, 512 * h:512 * (h + 1)],
                                        qts[s_ % 2][:, 512 * h:512 * (h + 1)], ALU.mult)
                    for h in range(16):
                        v.wait_ge(mm.h, POS[("qcs", s_, h)])
                        v.tensor_copy(qn2[:, 512 * h:512 * (h + 1)], pcs[h % 2][0:1, :])
                        v.reduce_max(mt[:, 4 + h:5 + h], qn2[:, 512 * h:512 * (h + 1)], axis=AX.X)
                    for P in range(8):
                        v.wait_ge(sc.h, POS[("qsqrt", s_, 2 * P + 1)])
                        v.tensor_max(mt[:, 36 + P:37 + P], mt[:, 20 + 2 * P:21 + 2 * P],
                                     mt[:, 21 + 2 * P:22 + 2 * P])
                        v.tensor_scalar(mt[:, 36 + P:37 + P], mt[:, 36 + P:37 + P],
                                        mt[:, P // 2:P // 2 + 1], -1.0, ALU.mult, ALU.mult)
                    for P in range(8):
                        v.wait_ge(mm.h, POS[("crepl", s_, P)])
                        v.tensor_copy(cbias[:, P:P + 1], psc[0][:, P:P + 1])
                    for P in range(8):
                        n = nj(s_)
                        for qc in range(4):
                            j = 4 * s_ + qc
                            v.wait_ge(sc.h, POS[("exp", s_, P, j)])
                            pb = pbr[:, 1024 * (j % 4):1024 * (j % 4 + 1)]
                            for qc2 in range(qc):
                                v.memset(pb[:, 128 * qc2:128 * (qc2 + 1)], 0.0)
                                v.memset(pb[:, 512 + 128 * qc2:512 + 128 * (qc2 + 1)], 0.0)
                            v.tensor_tensor(pb[:, 128 * qc:128 * (qc + 1)],
                                            pb[:, 128 * qc:128 * (qc + 1)],
                                            self.utmaskb[:], ALU.mult)
                            v.tensor_tensor(pb[:, 512 + 128 * qc:512 + 128 * (qc + 1)],
                                            pb[:, 512 + 128 * qc:512 + 128 * (qc + 1)],
                                            self.utmaskb[:], ALU.mult)
                        v.wait_ge(mm.h, POS[("cslast", s_, P)])
                        v.reciprocal(rrow[:, 0:512], pcs[0][0:1, :])
                        v.reciprocal(rrow[:, 512:1024], pcs[1][0:1, :])
                        v.wait_ge(mm.h, POS[("repl", s_, P)])
                        v.tensor_copy(rsb[:], psc[0][:])
                        v.wait_ge(mm.h, POS[("pv", s_, P, n - 1)])
                        gi = 8 * s_ + P
                        if gi >= 2:
                            v.wait_ge(st[gi % 2].h, st[gi % 2].b + 32 * (gi // 2))
                        v.tensor_tensor(ob[gi % 2][:], patt[:], rsb[:], ALU.mult)

            @blk.scalar
            def _(s):
                s = self.chained(s, "sch")
                for g in range(4):
                    s.wait_ge(vc.h, POS[("kmax", g)])
                    s.activation(mt[:, g:g + 1], mt[:, g:g + 1], AF.Sqrt)
                for s_ in range(NST):
                    for h in range(16):
                        s.wait_ge(vc.h, POS[("qmax", s_, h)])
                        s.activation(mt[:, 20 + h:21 + h], mt[:, 4 + h:5 + h], AF.Sqrt)
                    for P in range(8):
                        n = nj(s_)
                        s.wait_ge(vc.h, POS[("cbias", s_, P)])
                        for j in range(n):
                            s.wait_ge(mm.h, POS[("sco", s_, P, j)])
                            if j >= 4:
                                s.wait_ge(mm.h, POS[("pv", s_, P, j - 4)])
                            s.activation(pbr[:, 1024 * (j % 4):1024 * (j % 4 + 1)],
                                         psc[j % 2][:], AF.Exp, bias=cbias[:, P:P + 1])

            @blk.gpsimd
            def _(g_):
                for s_ in range(NST):
                    for P in range(8):
                        gi = 8 * s_ + P
                        h0 = 2 * P
                        g_.wait_ge(vc.h, POS[("obtt", s_, P)])
                        if gi >= 2:
                            g_.wait_ge(st[gi % 2].h, st[gi % 2].b + 32 * (gi // 2))
                        g_.dma_start(ctx[128 * h0:128 * (h0 + 1), 512 * s_:512 * (s_ + 1)],
                                     ob[gi % 2][:, 0:512]).then_inc(st[gi % 2].h, 16)
                        g_.dma_start(ctx[128 * (h0 + 1):128 * (h0 + 2), 512 * s_:512 * (s_ + 1)],
                                     ob[gi % 2][:, 512:1024]).then_inc(st[gi % 2].h, 16)
                ngi = 8 * NST
                g_.wait_ge(st[0].h, st[0].b + 32 * (ngi // 2))
                g_.wait_ge(st[1].h, st[1].b + 32 * (ngi // 2))

            @blk.tensor
            def _(t):
                t.wait_ge(kl.h, kl.b + 64)
                for g in range(4):
                    t.wait_ge(vc.h, V0 + (NKCH + 2) * g + 1)
                    for ch in range(NKCH):
                        flat = NKCH * g + ch
                        if flat >= 2:
                            t.wait_ge(vc.h, POS[("kev", (flat - 2) // NKCH, (flat - 2) % NKCH)])
                        t.matmul(pcs[flat % 2][0:1, :], self.onescol[:],
                                 sqb[:, 512 * ch:512 * (ch + 1)],
                                 start=True, stop=True).then_inc(mm.h, 1)
                t.wait_ge(vl.h, vl.b + vstot[0])
                for s_ in range(NST):
                    n = nj(s_)
                    t.wait_ge(vc.h, POS[("sqdone", s_)])
                    if s_ >= 1:
                        t.wait_ge(vc.h, POS[("recip", s_ - 1, 7)])
                    for h in range(16):
                        if h >= 2:
                            t.wait_ge(vc.h, POS[("qev", s_, h - 2)])
                        t.matmul(pcs[h % 2][0:1, :], self.onescol[:],
                                 sqb[:, 512 * h:512 * (h + 1)],
                                 start=True, stop=True).then_inc(mm.h, 1)
                    if s_ >= 1:
                        t.wait_ge(vc.h, POS[("rsb", s_ - 1, 7)])
                    for P in range(8):
                        t.wait_ge(vc.h, POS[("negc", s_, P)])
                        t.matmul(psc[0][:, P:P + 1], self.onesrow[:],
                                 mt[:, 36 + P:37 + P],
                                 start=True, stop=True).then_inc(mm.h, 1)
                    for P in range(8):
                        gr = P // 2
                        h0 = 2 * P
                        if P == 0:
                            t.wait_ge(vc.h, POS[("cbias", s_, 7)])
                        else:
                            t.wait_ge(vc.h, POS[("rsb", s_, P - 1)])
                        for j in range(n):
                            if j >= 2:
                                t.wait_ge(sc.h, POS[("exp", s_, P, j - 2)])
                            lk = ksb[:, S * gr + 128 * j:S * gr + 128 * (j + 1)]
                            t.matmul(psc[j % 2][:, 0:512], lk,
                                     qts[s_ % 2][:, 512 * h0:512 * (h0 + 1)],
                                     start=True, stop=True)
                            t.matmul(psc[j % 2][:, 512:1024], lk,
                                     qts[s_ % 2][:, 512 * (h0 + 1):512 * (h0 + 2)],
                                     start=True, stop=True).then_inc(mm.h, 1)
                            if j >= 1:
                                jj = j - 1
                                pb = pbr[:, 1024 * (jj % 4):1024 * (jj % 4 + 1)]
                                if jj >= 4 * s_:
                                    t.wait_ge(vc.h, maskpos(s_, P, jj))
                                else:
                                    t.wait_ge(sc.h, POS[("exp", s_, P, jj)])
                                if jj == 0:
                                    t.wait_ge(vc.h, POS[("recip", s_, P - 1)] if P >= 1
                                              else (POS[("recip", s_ - 1, 7)] if s_ >= 1 else 0))
                                t.matmul(pcs[0][0:1, :], self.onescol[:], pb[:, 0:512],
                                         start=(jj == 0), stop=(jj == n - 1))
                                t.matmul(pcs[1][0:1, :], self.onescol[:], pb[:, 512:1024],
                                         start=(jj == 0), stop=(jj == n - 1))
                                lv = vsb[:, 512 * jj + 128 * gr:512 * jj + 128 * (gr + 1)]
                                if jj == 0:
                                    t.wait_ge(vc.h, POS[("obtt", s_, P - 1)] if P >= 1
                                              else (POS[("obtt", s_ - 1, 7)] if s_ >= 1 else 0))
                                t.matmul(patt[:, 0:512], lv, pb[:, 0:512],
                                         start=(jj == 0), stop=(jj == n - 1))
                                t.matmul(patt[:, 512:1024], lv, pb[:, 512:1024],
                                         start=(jj == 0), stop=(jj == n - 1)).then_inc(mm.h, 1)
                        jj = n - 1
                        pb = pbr[:, 1024 * (jj % 4):1024 * (jj % 4 + 1)]
                        t.wait_ge(vc.h, maskpos(s_, P, jj))
                        t.matmul(pcs[0][0:1, :], self.onescol[:], pb[:, 0:512],
                                 start=(jj == 0), stop=True)
                        t.matmul(pcs[1][0:1, :], self.onescol[:], pb[:, 512:1024],
                                 start=(jj == 0), stop=True).then_inc(mm.h, 1)
                        lv = vsb[:, 512 * jj + 128 * gr:512 * jj + 128 * (gr + 1)]
                        if jj == 0:
                            t.wait_ge(vc.h, POS[("obtt", s_, P - 1)] if P >= 1
                                      else (POS[("obtt", s_ - 1, 7)] if s_ >= 1 else 0))
                        t.matmul(patt[:, 0:512], lv, pb[:, 0:512],
                                 start=(jj == 0), stop=True)
                        t.matmul(patt[:, 512:1024], lv, pb[:, 512:1024],
                                 start=(jj == 0), stop=True).then_inc(mm.h, 1)
                        t.wait_ge(vc.h, POS[("recip", s_, P)])
                        t.matmul(psc[0][:, 0:512], self.onesrow[:], rrow[:, 0:512],
                                 start=True, stop=True).then_inc(mm.h, 1)
                        t.matmul(psc[0][:, 512:1024], self.onesrow[:], rrow[:, 512:1024],
                                 start=True, stop=True).then_inc(mm.h, 1)

        ngi = 8 * NST
        kl.b += 64
        vl.b += vstot[0]
        ql[0].b += qtot[0]
        ql[1].b += qtot[1]
        st[0].b += 32 * (ngi // 2)
        st[1].b += 32 * (ngi // 2)
        mm.b = MEND
        assert vc.b == VEND, (vc.b, VEND)
        assert sc.b == SEND, (sc.b, SEND)
        pc.close()

def build_program():
    nc = bass.Bass("TRN2", target_bir_lowering=False)
    p = Prog(nc)
    f32in = [("x0", [S, H]), ("wqp", [H, H]), ("wkp", [H, H]), ("wv", [H, H]),
             ("wupts", [H, I]), ("wgatets", [H, I]), ("wdnts", [I, H]),
             ("su_ug", [128, 16]), ("su_dp", [128, 64]),
             ("wsc_up", [128, 64]), ("wsc_gate", [128, 64]), ("wsc_dp", [1, H]),
             ("wq1", [H, H]), ("wkv1", [H, 1024]), ("wdense1", [H, H]),
             ("sc_up1", [128, 64]), ("sc_gate1", [128, 64]), ("sc_dn1", [1, H]),
             ("qcos", [1024, S]), ("qsin", [1024, S]),
             ("kcos", [1024, S]), ("ksin", [1024, S]),
             ("ident", [128, 128]), ("h128", [128, 128]), ("h16x8", [128, 128]),
             ("h64x2", [128, 128]), ("utmask", [128, 128])]
    for n, shp in f32in:
        p.dram_in(n, shp, F32)
    for n, shp in [("upt1", [H, I]), ("gatet1", [H, I]), ("dnt1", [I, H])]:
        p.dram_in(n, shp, U8)
    for n, shp, dt in [("xnT", [H, S], BF16), ("qT", [H, S], BF16),
                       ("kT", [H, S], BF16), ("v0", [S, H], BF16),
                       ("psp", [S, S], BF16), ("r1", [S, H], F32),
                       ("x1", [S, H], F32), ("wbounce", [H, I], BF16),
                       ("wbounce2", [I, H], BF16), ("hwup", [H, I], BF16),
                       ("hwgate", [H, I], BF16), ("hwdn", [I, H], BF16),
                       ("t0", [I, S], BF16), ("kT1", [512, S], BF16),
                       ("v1", [S, 512], BF16), ("ctx", [H, S], BF16)]:
        p.dram_tmp(n, shp, dt)
    p.dram_out("out", [S, H], F32)

    p.setup_consts()
    p.ph_wprep("wupts", "hwup", 16, I, 16, 16, "su_ug", "row")
    p.ph_wprep("wgatets", "hwgate", 16, I, 16, 16, "su_ug", "row")
    p.ph_wprep("wdnts", "hwdn", 64, H, 64, 64, "su_dp", "rowcol",
               colrow_name="wsc_dp", bounce_name="wbounce2")
    p.ph_ln("x0")
    p.ph_projT("wqp", "qT", 16, xpos=("qcos", "qsin"))
    p.ph_projT("wkp", "kT", 16, xpos=("kcos", "ksin"))
    p.ph_projN("wv", "v0", H)
    p.ph_scores0(build_consts()["cdec"])
    p.ph_pv0()
    p.ph_ffn_up("hwup", "hwgate", "wsc_up", "wsc_gate")
    p.ph_ffn_down("hwdn", "r1", "x1")
    p.ph_wcast("upt1", "hwup", H, I)
    p.ph_wcast("gatet1", "hwgate", H, I)
    p.ph_wcast("dnt1", "hwdn", I, H)
    p.ph_ln("x1")
    p.ph_projT("wq1", "qT", 16, scale=float(KVC ** -0.5))
    p.ph_projT("wkv1", "kT1", 4, col_off=0)
    p.ph_projN("wkv1", "v1", 512, col_off=512)
    p.ph_attn1()
    p.ph_projN("wdense1", "r1", H, src_name="ctx", res_name="x1")
    p.ph_ffn_up("hwup", "hwgate", "sc_up1", "sc_gate1")
    p.ph_ffn_down("hwdn", "r1", "out", coln_name="sc_dn1")
    p.ctx.close()
    nc.finalize()
    return nc


def kernel(**inputs):
    X = np.ascontiguousarray(inputs["L0_attn_X"].astype(np.float32, copy=False))
    prep = host_prep(inputs)
    c = build_consts()
    nc = build_program()
    base = dict(prep)
    base.update(qcos=c["qcos"], qsin=c["qsin"], kcos=c["kcos"], ksin=c["ksin"],
                ident=c["ident"], h128=c["h128"], h16x8=c["h16x8"],
                h64x2=c["h64x2"], utmask=c["utmask"])
    in_maps = []
    for core in range(8):
        m = dict(base)
        m["x0"] = np.ascontiguousarray(X[core])
        in_maps.append(m)
    res = run_bass_kernel_spmd(nc, in_maps, core_ids=list(range(8)))
    return np.stack([r["out"] for r in res.results], axis=0)

if __name__ == "__main__":
    def _xpos_np(x, downscale=False):
        L, d = x.shape
        half = d // 2
        scale_vec = (np.arange(0, d, 2, dtype=np.float64) + 0.4 * d) / (1.4 * d)
        power = np.arange(L, dtype=np.float64) / SCALE_BASE
        scale = scale_vec[None, :] ** power[:, None]
        inv_freq = 1.0 / 10000.0 ** (np.arange(half, dtype=np.float64) / half)
        sinusoid = np.arange(L, dtype=np.float64)[:, None] * inv_freq[None, :]
        sin, cos = np.sin(sinusoid), np.cos(sinusoid)
        if downscale:
            scale = 1.0 / scale
        sin_s = np.repeat(sin * scale, 2, axis=-1)
        cos_s = np.repeat(cos * scale, 2, axis=-1)
        x1, x2 = x[:, 0::2], x[:, 1::2]
        rot = np.stack((-x2, x1), axis=-1).reshape(x.shape)
        return x * cos_s + rot * sin_s

    rng = np.random.default_rng(0)
    Ltest = 256
    X = rng.standard_normal((Ltest, H)).astype(np.float64)
    c = build_consts()
    dq = (GAMMA ** (np.arange(S) % 128)).astype(np.float64)
    for name, ds, tc, ts_, dd in (
        ("q", False, c["qcos"], c["qsin"], dq),
        ("k", True, c["kcos"], c["ksin"], 1.0 / dq),
    ):
        ref = _xpos_np(X, ds)
        Qe, Qo = X[:, 0::2], X[:, 1::2]
        cosv = tc[:, :Ltest].T.astype(np.float64) / dd[:Ltest, None]
        sinv = ts_[:, :Ltest].T.astype(np.float64) / dd[:Ltest, None]
        oute = Qe * cosv - Qo * sinv
        outo = Qo * cosv + Qe * sinv
        got = np.empty_like(X)
        got[:, 0::2], got[:, 1::2] = oute, outo
        err = np.abs(got - ref).max() / (np.abs(ref).max() + 1e-9)
        print(f"xpos {name}: relmax {err:.2e}")

    def fwht(x):
        n = x.shape[-1]
        h = 1
        x = x.copy()
        while h < n:
            x = x.reshape(-1, n // (2 * h), 2, h)
            a, b = x[:, :, 0, :].copy(), x[:, :, 1, :].copy()
            x = np.stack([a + b, a - b], axis=2).reshape(-1, n)
            h *= 2
        return x / math.sqrt(n)

    for (n_feat, ntile, hcross) in ((H, 16, c["h16x8"]), ):
        Wup = rng.standard_normal((I, H)) * 0.02
        su = np.where(rng.random(H) > 0.5, 1.0, -1.0)
        ws = rng.uniform(0.5, 1.5, I)
        x = rng.standard_normal((4, H))
        ref = (fwht(x * su) / 64.0) @ Wup.T * (ws * 64.0)
        wupts = _swiz(np.ascontiguousarray(Wup.T), 16)
        step1 = np.zeros_like(wupts)
        for beta in range(16):
            step1[128*beta:128*(beta+1)] = c["h16x8"].astype(np.float64) @ wupts[128*beta:128*(beta+1)]
        aord = np.zeros_like(step1)
        for alpha in range(16):
            aord[128*alpha:128*(alpha+1)] = step1[alpha::16]
        what = np.zeros_like(aord)
        h128 = c["h128"].astype(np.float64)
        for alpha in range(16):
            blk = h128 @ aord[128*alpha:128*(alpha+1)]
            what[128*alpha:128*(alpha+1)] = blk * su[128*alpha:128*(alpha+1), None] / math.sqrt(H)
        got = (x @ what) * ws
        print("fwht fold up: relmax", np.abs(got - ref).max() / np.abs(ref).max())

    Wdn = rng.standard_normal((H, I)) * 0.02
    su = np.where(rng.random(I) > 0.5, 1.0, -1.0)
    wsd = rng.uniform(0.5, 1.5, H)
    t = rng.standard_normal((4, I))
    ref = (fwht(t * su) / 64.0) @ Wdn.T * (wsd * 64.0)
    wdnts = _swiz(np.ascontiguousarray(Wdn.T), 64)
    step1 = np.zeros_like(wdnts)
    for beta in range(64):
        step1[128*beta:128*(beta+1)] = c["h64x2"].astype(np.float64) @ wdnts[128*beta:128*(beta+1)]
    aord = np.zeros_like(step1)
    for alpha in range(64):
        aord[128*alpha:128*(alpha+1)] = step1[alpha::64]
    what = np.zeros_like(aord)
    h128 = c["h128"].astype(np.float64)
    sdpb = wsd / math.sqrt(I)
    for alpha in range(64):
        blk = h128 @ aord[128*alpha:128*(alpha+1)]
        what[128*alpha:128*(alpha+1)] = blk * su[128*alpha:128*(alpha+1), None] * sdpb[None, :]
    got = t @ what
    print("fwht fold dn: relmax", np.abs(got - ref).max() / np.abs(ref).max())


# version 08
# speedup vs baseline: 1.1990x; 1.0464x over previous
import math
from contextlib import ExitStack

import numpy as np
import ml_dtypes

import concourse.bass as bass
import concourse.mybir as mybir
from concourse.bass_utils import run_bass_kernel_spmd

F32 = mybir.dt.float32
BF16 = mybir.dt.bfloat16
I32 = mybir.dt.int32
U8 = mybir.dt.uint8
AF = mybir.ActivationFunctionType
ALU = mybir.AluOpType
AX = mybir.AxisListType

S = 4096
H = 2048
I = 8192
NT = S // 128
NH = H // 128
NI = I // 128
GAMMA = 0.96875
SCALE_BASE = 512
KVC = 128
HEADS = 16
GROUPS = 4

BF = ml_dtypes.bfloat16


def _hadamard(n):
    h = np.array([[1.0]])
    while h.shape[0] < n:
        h = np.block([[h, h], [h, -h]])
    return h


def build_consts():
    c = {}
    c["ident"] = np.eye(128, dtype=np.float32)
    c["h128"] = _hadamard(128).astype(np.float32)
    c["h16x8"] = np.kron(np.eye(8), _hadamard(16)).astype(np.float32)
    c["h64x2"] = np.kron(np.eye(2), _hadamard(64)).astype(np.float32)
    c["utmask"] = np.triu(np.ones((128, 128), dtype=np.float32))
    d = H
    half = d // 2
    hi = np.arange(half, dtype=np.float64)
    n = np.arange(S, dtype=np.float64)
    scale_vec = (2.0 * hi + 0.4 * d) / (1.4 * d)
    power = n / SCALE_BASE
    scale = scale_vec[None, :] ** power[:, None]
    inv_freq = 1.0 / 10000.0 ** (hi / half)
    sinus = n[:, None] * inv_freq[None, :]
    sin, cos = np.sin(sinus), np.cos(sinus)
    dq = GAMMA ** (n % 128)
    dk = GAMMA ** (-(n % 128))
    c["qcos"] = (cos * scale * dq[:, None]).T.astype(np.float32)
    c["qsin"] = (sin * scale * dq[:, None]).T.astype(np.float32)
    c["kcos"] = (cos / scale * dk[:, None]).T.astype(np.float32)
    c["ksin"] = (sin / scale * dk[:, None]).T.astype(np.float32)
    c["cdec"] = (GAMMA ** (128.0 * np.arange(NT, dtype=np.float64))).astype(np.float32)
    return c


def _swiz(wt, ntile):
    R, C = wt.shape
    assert R == ntile * 128
    return np.ascontiguousarray(
        wt.reshape(ntile, 128, C).transpose(1, 0, 2).reshape(R, C))


def _perm_eo(w):
    return np.ascontiguousarray(np.concatenate([w[:, 0::2], w[:, 1::2]], axis=1))


def host_prep(inputs):
    p = {}
    asc = np.ascontiguousarray
    p["wqp"] = _perm_eo(inputs["L0_attn_W_Q"].astype(np.float32, copy=False))
    p["wkp"] = _perm_eo(inputs["L0_attn_W_K"].astype(np.float32, copy=False))
    p["wv"] = asc(inputs["L0_attn_W_V"].astype(np.float32, copy=False))
    p["wupts"] = _swiz(asc(inputs["L0_ffn_W_up"].T), 16)
    p["wgatets"] = _swiz(asc(inputs["L0_ffn_W_gate"].T), 16)
    p["wdnts"] = _swiz(asc(inputs["L0_ffn_W_down"].T), 64)
    col = lambda v, nt: asc(v.reshape(nt, 128).T.astype(np.float32))
    p["su_ug"] = col(inputs["L0_ffn_SU_ug"], 16)
    p["su_dp"] = col(inputs["L0_ffn_SU_dp"], 64)
    p["wsc_up"] = col(inputs["L0_ffn_Wscale_ug"][:I], 64)
    p["wsc_gate"] = col(inputs["L0_ffn_Wscale_ug"][I:], 64)
    p["wsc_dp"] = asc(inputs["L0_ffn_Wscale_dp"].reshape(1, H).astype(np.float32))
    p["wq1"] = asc(inputs["L1_attn_w_q"].astype(np.float32, copy=False))
    p["wkv1"] = asc(inputs["L1_attn_w_kv"].astype(np.float32, copy=False))
    p["wdense1"] = asc(inputs["L1_attn_w_dense"].astype(np.float32, copy=False))
    def lowbyte(a):
        a = asc(a)
        return asc(a.view(np.uint8).reshape(a.shape[0], a.shape[1], 4)[:, :, 0])
    p["upt1"] = lowbyte(inputs["L1_ffn_up_w"].T)
    p["gatet1"] = lowbyte(inputs["L1_ffn_gate_w"].T)
    p["dnt1"] = lowbyte(inputs["L1_ffn_down_w"].T)
    p["sc_up1"] = col(inputs["L1_ffn_up_scale"], 64)
    p["sc_gate1"] = col(inputs["L1_ffn_gate_scale"], 64)
    p["sc_dn1"] = asc(inputs["L1_ffn_down_scale"].reshape(1, H).astype(np.float32))
    return p



class Prog:

    def __init__(self, nc):
        self.nc = nc
        self.ctx = ExitStack()
        self._n = 0
        self.dram = {}

    def name(self, p):
        self._n += 1
        return f"{p}{self._n}"

    def sbuf(self, shape, dt, p="sb", ctx=None):
        return (ctx or self.ctx).enter_context(self.nc.sbuf_tensor(self.name(p), shape, dt))

    def psum(self, shape, dt, p="ps", ctx=None):
        return (ctx or self.ctx).enter_context(self.nc.psum_tensor(self.name(p), shape, dt))

    def sem(self, p="sem", ctx=None):
        return (ctx or self.ctx).enter_context(self.nc.semaphore(self.name(p)))

    _CHAIN_OPS = ("tensor_copy", "tensor_tensor", "tensor_add", "tensor_sub",
                  "tensor_mul", "tensor_max", "tensor_scalar", "tensor_single_scalar",
                  "scalar_tensor_tensor", "tensor_reduce", "reduce_sum", "reduce_max",
                  "reciprocal", "memset", "select", "activation", "copy", "add",
                  "mul", "square", "sqrt", "sign", "iota")

    def chained(self, eng, cname):
        prog = self
        c = self.ctr(cname)

        class _W:
            def __getattr__(self, k):
                f = getattr(eng, k)
                if k in prog._CHAIN_OPS:
                    def g(*a, **kw):
                        eng.wait_ge(c.h, c.b)
                        ins = f(*a, **kw)
                        ins.then_inc(c.h, 1)
                        c.b += 1
                        return ins
                    return g
                return f
        w = _W()
        w.c = c
        return w

    def dmaN(self, eng, ctrh, dst, src):
        n0 = self.nc.n_instructions()
        ins = eng.dma_start(dst, src)
        parts = self.nc.n_instructions() - n0
        ins.then_inc(ctrh, 16)
        return 16 * parts

    def ctr(self, name):
        if not hasattr(self, "_ctrs"):
            self._ctrs = {}
        if name not in self._ctrs:
            h = self.ctx.enter_context(self.nc.semaphore("c_" + name))
            c = type("Ctr", (), {})()
            c.h, c.b = h, 0
            self._ctrs[name] = c
        return self._ctrs[name]

    def dram_in(self, name, shape, dt):
        self.dram[name] = self.nc.dram_tensor(name, list(shape), dt, kind="ExternalInput").ap()
        return self.dram[name]

    def dram_out(self, name, shape, dt):
        self.dram[name] = self.nc.dram_tensor(name, list(shape), dt, kind="ExternalOutput").ap()
        return self.dram[name]

    def dram_tmp(self, name, shape, dt):
        self.dram[name] = self.nc.dram_tensor(name, list(shape), dt, kind="Internal").ap()
        return self.dram[name]

    def setup_consts(self):
        nc = self.nc
        d = self.dram
        self.identb = self.sbuf([128, 128], BF16, "identb")
        self.h128b = self.sbuf([128, 128], BF16, "h128b")
        self.h16x8b = self.sbuf([128, 128], BF16, "h16x8b")
        self.h64x2b = self.sbuf([128, 128], BF16, "h64x2b")
        self.utmaskb = self.sbuf([128, 128], BF16, "utmaskb")
        self.onescol = self.sbuf([128, 1], BF16, "onescol")
        self.onesrow = self.sbuf([1, 128], F32, "onesrow")
        self.epscol = self.sbuf([128, 1], F32, "epscol")
        self.m128col = self.sbuf([128, 1], F32, "m128col")
        names = ["ident", "h128", "h16x8", "h64x2", "utmask"]
        dsts = [self.identb, self.h128b, self.h16x8b, self.h64x2b, self.utmaskb]
        pc = ExitStack()
        tmp = self.sbuf([128, 5 * 128], F32, "ctmp", ctx=pc)
        ld = self.ctr("ld0")
        with nc.Block() as blk:
            @blk.sync
            def _(sync):
                for i, n in enumerate(names):
                    sync.dma_start(tmp[:, 128 * i:128 * (i + 1)], d[n][:]).then_inc(ld.h, 16)
                sync.wait_ge(ld.h, ld.b + 80)

            @blk.vector
            def _(v):
                v = self.chained(v, "vch")
                v.wait_ge(ld.h, ld.b + 80)
                for i, dst in enumerate(dsts):
                    v.tensor_copy(dst[:], tmp[:, 128 * i:128 * (i + 1)])
                v.memset(self.onescol[:], 1.0)
                v.memset(self.onesrow[:], 1.0)
                v.memset(self.epscol[:], 1e-5)
                v.memset(self.m128col[:], -128.0)
        self.nc.const_aps.register(F32, -128.0, self.m128col[:])
        ld.b += 80
        pc.close()

    def ph_ln(self, x_name):
        nc = self.nc
        x = self.dram[x_name]
        xnT = self.dram["xnT"]
        pc = ExitStack()
        xb = [self.sbuf([128, H], F32, ctx=pc) for _ in range(2)]
        sq = [self.sbuf([128, H], F32, ctx=pc) for _ in range(2)]
        xnb = [self.sbuf([128, H], BF16, ctx=pc) for _ in range(2)]
        stat = [self.sbuf([128, 8], F32, ctx=pc) for _ in range(2)]
        stg = [self.sbuf([128, 2048], BF16, ctx=pc) for _ in range(2)]
        ptb = [self.psum([128, 1024], BF16, ctx=pc) for _ in range(4)]
        ld = [self.ctr("ld0"), self.ctr("ld1")]
        st = [self.ctr("st0"), self.ctr("st1")]
        tp = self.ctr("tp")
        vc, sc = self.ctr("vch"), self.ctr("sch")
        vcb, scb, tpb = vc.b, sc.b, tp.b
        with nc.Block() as blk:
            @blk.sync
            def _(sync):
                for i in range(NT):
                    if i >= 2:
                        sync.wait_ge(vc.h, vcb + 8 * (i - 1))
                    sync.dma_start(xb[i % 2][:], x[128 * i:128 * (i + 1), :]).then_inc(ld[i % 2].h, 16)
                sync.wait_ge(ld[0].h, ld[0].b + 16 * (NT // 2))
                sync.wait_ge(ld[1].h, ld[1].b + 16 * (NT // 2))

            @blk.vector
            def _(v):
                v = self.chained(v, "vch")
                for i in range(NT):
                    b = i % 2
                    v.wait_ge(ld[b].h, ld[b].b + 16 * (i // 2 + 1))
                    v.tensor_tensor(sq[b][:], xb[b][:], xb[b][:], ALU.mult)
                    v.reduce_sum(stat[b][:, 0:1], xb[b][:], axis=AX.X)
                    v.reduce_sum(stat[b][:, 1:2], sq[b][:], axis=AX.X)
                    v.tensor_scalar(stat[b][:, 2:3], stat[b][:, 0:1], -1.0 / H, None, ALU.mult)
                    v.tensor_tensor(stat[b][:, 3:4], stat[b][:, 2:3], stat[b][:, 2:3], ALU.mult)
                    v.scalar_tensor_tensor(stat[b][:, 4:5], stat[b][:, 1:2], 1.0 / H,
                                           stat[b][:, 3:4], ALU.mult, ALU.subtract)
                    v.wait_ge(sc.h, scb + 3 * i + 1)
                    v.reciprocal(stat[b][:, 6:7], stat[b][:, 5:6])
                    if i >= 2:
                        v.wait_ge(tp.h, tpb + 2 * (i - 1))
                    v.tensor_scalar(xnb[b][:], xb[b][:], stat[b][:, 2:3], stat[b][:, 6:7],
                                    ALU.add, ALU.mult)

            @blk.scalar
            def _(s):
                s = self.chained(s, "sch")
                for i in range(NT):
                    b = i % 2
                    s.wait_ge(vc.h, vcb + 8 * i + 6)
                    s.activation(stat[b][:, 5:6], stat[b][:, 4:5], AF.Sqrt, bias=self.epscol[:])
                    s.wait_ge(tp.h, tpb + 2 * i + 1)
                    if i >= 2:
                        s.wait_ge(st[b].h, st[b].b + 256 * (i // 2))
                    s.copy(stg[b][:, 0:1024], ptb[(2 * i) % 4][:])
                    s.wait_ge(tp.h, tpb + 2 * i + 2)
                    s.copy(stg[b][:, 1024:2048], ptb[(2 * i + 1) % 4][:])

            @blk.tensor
            def _(t):
                for i in range(NT):
                    b = i % 2
                    t.wait_ge(vc.h, vcb + 8 * (i + 1))
                    if i >= 2:
                        t.wait_ge(sc.h, scb + 3 * (i - 1))
                    for half in range(2):
                        p = ptb[(2 * i + half) % 4]
                        for c in range(8):
                            cc = 8 * half + c
                            ins = t.transpose(p[:, 128 * c:128 * (c + 1)],
                                              xnb[b][:, 128 * cc:128 * (cc + 1)], self.identb[:])
                        ins.then_inc(tp.h, 1)

            @blk.gpsimd
            def _(g):
                for i in range(NT):
                    b = i % 2
                    g.wait_ge(sc.h, scb + 3 * (i + 1))
                    if i >= 2:
                        g.wait_ge(st[b].h, st[b].b + 256 * (i // 2))
                    for c in range(16):
                        g.dma_start(xnT[128 * c:128 * (c + 1), 128 * i:128 * (i + 1)],
                                    stg[b][:, 128 * c:128 * (c + 1)]).then_inc(st[b].h, 16)
                g.wait_ge(st[0].h, st[0].b + 256 * (NT // 2))
                g.wait_ge(st[1].h, st[1].b + 256 * (NT // 2))
        ld[0].b += 16 * (NT // 2)
        ld[1].b += 16 * (NT // 2)
        st[0].b += 256 * (NT // 2)
        st[1].b += 256 * (NT // 2)
        tp.b += 2 * NT
        assert vc.b == vcb + 8 * NT, (vc.b, vcb)
        assert sc.b == scb + 3 * NT, (sc.b, scb)
        pc.close()


    def ph_projN(self, w_name, out_name, E, src_name="xnT", col_off=0,
                 res_name=None, w_cols=None):
        nc = self.nc
        d = self.dram
        w, outd, srcd = d[w_name], d[out_name], d[src_name]
        C = srcd.shape[0]
        NC = C // 128
        NE = E // 512
        w_cols = w_cols if w_cols is not None else E
        pc = ExitStack()
        wsb = self.sbuf([128, NC * E], BF16, ctx=pc)
        wtmp = [self.sbuf([128, E], F32, ctx=pc) for _ in range(2)]
        lb = [self.sbuf([128, 16 * 128], BF16, ctx=pc) for _ in range(2)]
        ob = [self.sbuf([128, E], BF16 if res_name is None else F32, ctx=pc) for _ in range(2)]
        rb = [self.sbuf([128, E], F32, ctx=pc) for _ in range(2)] if res_name else None
        pj = [self.psum([128, 512], F32, ctx=pc) for _ in range(8)]
        ld = [self.ctr("ld0"), self.ctr("ld1")]
        xl = [self.ctr("ld2"), self.ctr("ld3")]
        rl = [self.ctr("ld4"), self.ctr("ld5")]
        st = [self.ctr("st0"), self.ctr("st1")]
        mm = self.ctr("mm")
        vc = self.ctr("vch")
        vcb, mmb = vc.b, mm.b
        xtot = [0, 0]
        xrec = {}

        with nc.Block() as blk:
            @blk.sync
            def _(sync):
                for c in range(NC):
                    if c >= 2:
                        sync.wait_ge(vc.h, vcb + c - 1)
                    sync.dma_start(wtmp[c % 2][:, 0:w_cols],
                                   w[128 * c:128 * (c + 1), col_off:col_off + w_cols]).then_inc(ld[c % 2].h, 16)
                src3 = srcd.rearrange("(c p) n -> p c n", p=128)
                for i in range(NT):
                    if i >= 2:
                        sync.wait_ge(mm.h, mmb + (i - 1) * NE)
                    xtot[i % 2] += self.dmaN(sync, xl[i % 2].h,
                                             lb[i % 2][:].rearrange("p (c n) -> p c n", n=128),
                                             src3[:, :, 128 * i:128 * (i + 1)])
                    xrec[i] = xtot[i % 2]
                sync.wait_ge(xl[0].h, xl[0].b + xtot[0])
                sync.wait_ge(xl[1].h, xl[1].b + xtot[1])
                sync.wait_ge(ld[0].h, ld[0].b + 16 * ((NC + 1) // 2))
                sync.wait_ge(ld[1].h, ld[1].b + 16 * (NC // 2))

            @blk.scalar
            def _(s):
                if res_name is None:
                    return
                res = d[res_name]
                for i in range(NT):
                    if i >= 2:
                        s.wait_ge(vc.h, vcb + NC + NE * (i - 1))
                    s.dma_start(rb[i % 2][:], res[128 * i:128 * (i + 1), :]).then_inc(rl[i % 2].h, 16)
                s.wait_ge(rl[0].h, rl[0].b + 16 * (NT // 2))
                s.wait_ge(rl[1].h, rl[1].b + 16 * (NT // 2))

            @blk.vector
            def _(v):
                v = self.chained(v, "vch")
                for c in range(NC):
                    v.wait_ge(ld[c % 2].h, ld[c % 2].b + 16 * (c // 2 + 1))
                    v.tensor_copy(wsb[:, E * c + 0:E * c + w_cols], wtmp[c % 2][:, 0:w_cols])
                for i in range(NT):
                    b = i % 2
                    if res_name is not None:
                        v.wait_ge(rl[b].h, rl[b].b + 16 * (i // 2 + 1))
                    if i >= 2:
                        v.wait_ge(st[b].h, st[b].b + 16 * (i // 2))
                    for hc in range(NE):
                        k = i * NE + hc
                        v.wait_ge(mm.h, mmb + k + 1)
                        if res_name is None:
                            v.tensor_copy(ob[b][:, 512 * hc:512 * (hc + 1)], pj[k % 8][:])
                        else:
                            v.tensor_add(ob[b][:, 512 * hc:512 * (hc + 1)], pj[k % 8][:],
                                         rb[b][:, 512 * hc:512 * (hc + 1)])

            @blk.gpsimd
            def _(g):
                for i in range(NT):
                    b = i % 2
                    g.wait_ge(vc.h, vcb + NC + NE * (i + 1))
                    if i >= 2:
                        g.wait_ge(st[b].h, st[b].b + 16 * (i // 2))
                    g.dma_start(outd[128 * i:128 * (i + 1), :], ob[b][:]).then_inc(st[b].h, 16)
                g.wait_ge(st[0].h, st[0].b + 16 * (NT // 2))
                g.wait_ge(st[1].h, st[1].b + 16 * (NT // 2))

            @blk.tensor
            def _(t):
                t.wait_ge(vc.h, vcb + NC)
                for i in range(NT):
                    t.wait_ge(xl[i % 2].h, xl[i % 2].b + xrec[i])
                    if i >= 1:
                        t.wait_ge(vc.h, vcb + NC + NE * i)
                    for c in range(NC):
                        for hc in range(NE):
                            k = i * NE + hc
                            ins = t.matmul(pj[k % 8][:],
                                           lb[i % 2][:, 128 * c:128 * (c + 1)],
                                           wsb[:, E * c + 512 * hc:E * c + 512 * (hc + 1)],
                                           start=(c == 0), stop=(c == NC - 1))
                            if c == NC - 1:
                                ins.then_inc(mm.h, 1)

        ld[0].b += 16 * ((NC + 1) // 2)
        ld[1].b += 16 * (NC // 2)
        xl[0].b += xtot[0]
        xl[1].b += xtot[1]
        if res_name is not None:
            rl[0].b += 16 * (NT // 2)
            rl[1].b += 16 * (NT // 2)
        st[0].b += 16 * (NT // 2)
        st[1].b += 16 * (NT // 2)
        mm.b += NT * NE
        assert vc.b == vcb + NC + NE * NT
        pc.close()

    def ph_scores0(self, cdec):
        nc = self.nc
        d = self.dram
        qT, kT, psp = d["qT"], d["kT"], d["psp"]
        NST = S // 512
        pc = ExitStack()
        qts = [self.sbuf([128, 512 * NH], BF16, ctx=pc) for _ in range(2)]
        ktb = [self.sbuf([128, 16 * 128], BF16, ctx=pc) for _ in range(4)]
        ob = [self.sbuf([128, 512], BF16, ctx=pc) for _ in range(2)]
        pj = [self.psum([128, 512], F32, ctx=pc) for _ in range(2)]
        ql = [self.ctr("ld0"), self.ctr("ld1")]
        kl = [self.ctr("ld2"), self.ctr("ld3"), self.ctr("ld4"), self.ctr("ld5")]
        st = [self.ctr("st0"), self.ctr("st1")]
        mm = self.ctr("mm")
        vc = self.ctr("vch")
        vcb, mmb = vc.b, mm.b
        pairs = []
        for s_ in range(NST):
            for j in range(4 * s_ + 4):
                pairs.append((s_, j))
        qtot, qrec = [0, 0], {}
        ktot, krec = [0, 0, 0, 0], {}

        with nc.Block() as blk:
            @blk.sync
            def _(sync):
                qT3 = qT.rearrange("(c p) n -> p c n", p=128)
                for s_ in range(NST):
                    if s_ >= 2:
                        done = sum(1 for (ss2, _) in pairs if ss2 <= s_ - 1)
                        sync.wait_ge(mm.h, mmb + done)
                    qtot[s_ % 2] += self.dmaN(sync, ql[s_ % 2].h,
                                              qts[s_ % 2][:].rearrange("p (c n) -> p c n", n=512),
                                              qT3[:, :, 512 * s_:512 * (s_ + 1)])
                    qrec[s_] = qtot[s_ % 2]
                sync.wait_ge(ql[0].h, ql[0].b + qtot[0])
                sync.wait_ge(ql[1].h, ql[1].b + qtot[1])

            @blk.scalar
            def _(s):
                kT3 = kT.rearrange("(c p) n -> p c n", p=128)
                for idx, (s_, j) in enumerate(pairs):
                    if idx >= 4:
                        s.wait_ge(mm.h, mmb + idx - 3)
                    ktot[idx % 4] += self.dmaN(s, kl[idx % 4].h,
                                               ktb[idx % 4][:].rearrange("p (c n) -> p c n", n=128),
                                               kT3[:, :, 128 * j:128 * (j + 1)])
                    krec[idx] = ktot[idx % 4]
                for q in range(4):
                    s.wait_ge(kl[q].h, kl[q].b + ktot[q])

            @blk.vector
            def _(v):
                v = self.chained(v, "vch")
                for idx, (s_, j) in enumerate(pairs):
                    v.wait_ge(mm.h, mmb + idx + 1)
                    b = idx % 2
                    if idx >= 2:
                        v.wait_ge(st[b].h, st[b].b + 16 * (idx // 2))
                    for qc in range(4):
                        it = 4 * s_ + qc
                        dst = ob[b][:, 128 * qc:128 * (qc + 1)]
                        ps = pj[b][:, 128 * qc:128 * (qc + 1)]
                        if j > it:
                            v.memset(dst, 0.0)
                        elif j == it:
                            v.tensor_tensor(dst, ps, self.utmaskb[:], ALU.mult)
                        else:
                            v.tensor_scalar(dst, ps, float(cdec[it - j]), None, ALU.mult)

            @blk.gpsimd
            def _(g):
                for idx, (s_, j) in enumerate(pairs):
                    b = idx % 2
                    g.wait_ge(vc.h, vcb + 4 * (idx + 1))
                    if idx >= 2:
                        g.wait_ge(st[b].h, st[b].b + 16 * (idx // 2))
                    g.dma_start(psp[128 * j:128 * (j + 1), 512 * s_:512 * (s_ + 1)],
                                ob[b][:]).then_inc(st[b].h, 16)
                g.wait_ge(st[0].h, st[0].b + 16 * ((len(pairs) + 1) // 2))
                g.wait_ge(st[1].h, st[1].b + 16 * (len(pairs) // 2))

            @blk.tensor
            def _(t):
                for idx, (s_, j) in enumerate(pairs):
                    b = idx % 2
                    t.wait_ge(ql[s_ % 2].h, ql[s_ % 2].b + qrec[s_])
                    t.wait_ge(kl[idx % 4].h, kl[idx % 4].b + krec[idx])
                    if idx >= 2:
                        t.wait_ge(vc.h, vcb + 4 * (idx - 1))
                    for c in range(NH):
                        ins = t.matmul(pj[b][:],
                                       ktb[idx % 4][:, 128 * c:128 * (c + 1)],
                                       qts[s_ % 2][:, 512 * c:512 * (c + 1)],
                                       start=(c == 0), stop=(c == NH - 1))
                    ins.then_inc(mm.h, 1)

        npair = len(pairs)
        ql[0].b += qtot[0]
        ql[1].b += qtot[1]
        for q in range(4):
            kl[q].b += ktot[q]
        st[0].b += 16 * ((npair + 1) // 2)
        st[1].b += 16 * (npair // 2)
        mm.b += npair
        assert vc.b == vcb + 4 * npair
        pc.close()

    def ph_pv0(self):
        nc = self.nc
        d = self.dram
        psp, v0, x0, r1 = d["psp"], d["v0"], d["x0"], d["r1"]
        pc = ExitStack()
        vsb = self.sbuf([128, NT * 2048], BF16, ctx=pc)
        pb = [self.sbuf([128, NT * 128], BF16, ctx=pc) for _ in range(2)]
        xb = [self.sbuf([128, 2048], F32, ctx=pc) for _ in range(2)]
        ob = [self.sbuf([128, 2048], F32, ctx=pc) for _ in range(2)]
        pj = [self.psum([128, 512], F32, ctx=pc) for _ in range(8)]
        vl = self.ctr("ld0")
        plq = [self.ctr("ld2"), self.ctr("ld3")]
        rl = [self.ctr("ld4"), self.ctr("ld5")]
        st = [self.ctr("st0"), self.ctr("st1")]
        mm = self.ctr("mm")
        vc = self.ctr("vch")
        vcb, mmb = vc.b, mm.b
        steps = [(i, j) for i in range(NT) for j in range(i + 1)]
        ptot, prec = [0, 0], {}

        with nc.Block() as blk:
            @blk.sync
            def _(sync):
                for t_ in range(NT):
                    sync.dma_start(vsb[:, 2048 * t_:2048 * (t_ + 1)],
                                   v0[128 * t_:128 * (t_ + 1), :]).then_inc(vl.h, 16)
                psp3 = psp.rearrange("(j p) n -> p j n", p=128)
                for i in range(NT):
                    if i >= 2:
                        sync.wait_ge(mm.h, mmb + steps.index((i - 1, i - 1)))
                    ptot[i % 2] += self.dmaN(sync, plq[i % 2].h,
                                             pb[i % 2][:].rearrange("p (j n) -> p j n", n=128)[:, 0:i + 1, :],
                                             psp3[:, 0:i + 1, 128 * i:128 * (i + 1)])
                    prec[i] = ptot[i % 2]
                sync.wait_ge(plq[0].h, plq[0].b + ptot[0])
                sync.wait_ge(plq[1].h, plq[1].b + ptot[1])

            @blk.scalar
            def _(s):
                for i in range(NT):
                    if i >= 2:
                        s.wait_ge(vc.h, vcb + 4 * (i - 1))
                    s.dma_start(xb[i % 2][:], x0[128 * i:128 * (i + 1), :]).then_inc(rl[i % 2].h, 16)
                s.wait_ge(rl[0].h, rl[0].b + 16 * (NT // 2))
                s.wait_ge(rl[1].h, rl[1].b + 16 * (NT // 2))

            @blk.vector
            def _(v):
                v = self.chained(v, "vch")
                for i in range(NT):
                    b = i % 2
                    v.wait_ge(mm.h, mmb + sum(1 for (i2, _) in steps if i2 <= i))
                    v.wait_ge(rl[b].h, rl[b].b + 16 * (i // 2 + 1))
                    if i >= 2:
                        v.wait_ge(st[b].h, st[b].b + 16 * (i // 2))
                    for hc in range(4):
                        v.tensor_add(ob[b][:, 512 * hc:512 * (hc + 1)],
                                     pj[(i % 2) * 4 + hc][:], xb[b][:, 512 * hc:512 * (hc + 1)])

            @blk.gpsimd
            def _(g):
                for i in range(NT):
                    b = i % 2
                    g.wait_ge(vc.h, vcb + 4 * (i + 1))
                    if i >= 2:
                        g.wait_ge(st[b].h, st[b].b + 16 * (i // 2))
                    g.dma_start(r1[128 * i:128 * (i + 1), :], ob[b][:]).then_inc(st[b].h, 16)
                g.wait_ge(st[0].h, st[0].b + 16 * (NT // 2))
                g.wait_ge(st[1].h, st[1].b + 16 * (NT // 2))

            @blk.tensor
            def _(t):
                t.wait_ge(vl.h, vl.b + 16 * NT)
                for idx, (i, j) in enumerate(steps):
                    if j == 0:
                        t.wait_ge(plq[i % 2].h, plq[i % 2].b + prec[i])
                        if i >= 2:
                            t.wait_ge(vc.h, vcb + 4 * (i - 1))
                    for hc in range(4):
                        ins = t.matmul(pj[(i % 2) * 4 + hc][:],
                                       pb[i % 2][:, 128 * j:128 * (j + 1)],
                                       vsb[:, 2048 * j + 512 * hc:2048 * j + 512 * (hc + 1)],
                                       start=(j == 0), stop=(j == i))
                    ins.then_inc(mm.h, 1)

        nsteps = len(steps)
        vl.b += 16 * NT
        plq[0].b += ptot[0]
        plq[1].b += ptot[1]
        rl[0].b += 16 * (NT // 2)
        rl[1].b += 16 * (NT // 2)
        st[0].b += 16 * (NT // 2)
        st[1].b += 16 * (NT // 2)
        mm.b += nsteps
        assert vc.b == vcb + 4 * NT
        pc.close()

    def ph_projT(self, w_name, out_name, ne_tiles, xpos=None, scale=1.0, col_off=0):
        nc = self.nc
        d = self.dram
        w, outd, xnT = d[w_name], d[out_name], d["xnT"]
        E = ne_tiles * 128
        NST = S // 512
        pc = ExitStack()
        GS = 4 if NST % 4 == 0 else (2 if NST % 2 == 0 else 1)
        NXB = 2 if xpos is not None else GS
        wsb = self.sbuf([128, NH * E], BF16, ctx=pc)
        wtmp = [self.sbuf([128, E], F32, ctx=pc) for _ in range(2)]
        xts = [self.sbuf([128, 512 * NH], BF16, ctx=pc) for _ in range(NXB)]
        ob = [self.sbuf([128, 512], BF16, ctx=pc) for _ in range(4)]
        pj = [self.psum([128, 512], F32, ctx=pc) for _ in range(8)]
        ld = [self.ctr("ld0"), self.ctr("ld1")]
        xl = [self.ctr("ld2"), self.ctr("ld3"), self.ctr("ld4"), self.ctr("ld5")][:NXB]
        st = [self.ctr("st0"), self.ctr("st1"), self.ctr("st2"), self.ctr("st3")]
        mm = self.ctr("mm")
        vc, sc = self.ctr("vch"), self.ctr("sch")
        vcb, scb, mmb = vc.b, sc.b, mm.b
        if xpos is not None:
            qraw = self.sbuf([128, 512 * 16], F32, ctx=pc)
            tcos = [self.sbuf([128, 512], F32, ctx=pc) for _ in range(2)]
            tsin = [self.sbuf([128, 512], F32, ctx=pc) for _ in range(2)]
            tmp1 = self.sbuf([128, 512], F32, ctx=pc)
            tmp2 = self.sbuf([128, 512], F32, ctx=pc)
            tl = [self.ctr("tld0"), self.ctr("tld1")]
        npairs = 8 * NST
        nk = NST * ne_tiles
        xtot = [0] * NXB
        xrec = {}

        with nc.Block() as blk:
            @blk.sync
            def _(sync):
                for c in range(NH):
                    if c >= 2:
                        sync.wait_ge(vc.h, vcb + c - 1)
                    sync.dma_start(wtmp[c % 2][:],
                                   w[128 * c:128 * (c + 1), col_off:col_off + E]).then_inc(ld[c % 2].h, 16)
                xnT3 = xnT.rearrange("(c p) n -> p c n", p=128)
                for s_ in range(NST):
                    sl_ = s_ % NXB
                    if xpos is not None and s_ >= 2:
                        sync.wait_ge(mm.h, mmb + (s_ - 1) * ne_tiles)
                    if xpos is None and s_ >= GS:
                        sync.wait_ge(mm.h, mmb + (s_ // GS) * ne_tiles * GS)
                    xtot[sl_] += self.dmaN(sync, xl[sl_].h,
                                           xts[sl_][:].rearrange("p (c n) -> p c n", n=512),
                                           xnT3[:, :, 512 * s_:512 * (s_ + 1)])
                    xrec[s_] = xtot[sl_]
                    if xpos is not None:
                        qct, qst = d[xpos[0]], d[xpos[1]]
                        for p in range(8):
                            j = s_ * 8 + p
                            if j >= 2:
                                sync.wait_ge(vc.h, vcb + NH + 6 * (j - 2) + 5)
                            sync.dma_start(tcos[j % 2][:], qct[128 * p:128 * (p + 1), 512 * s_:512 * (s_ + 1)]).then_inc(tl[j % 2].h, 16)
                            sync.dma_start(tsin[j % 2][:], qst[128 * p:128 * (p + 1), 512 * s_:512 * (s_ + 1)]).then_inc(tl[j % 2].h, 16)
                for q in range(NXB):
                    sync.wait_ge(xl[q].h, xl[q].b + xtot[q])
                sync.wait_ge(ld[0].h, ld[0].b + 16 * ((NH + 1) // 2))
                sync.wait_ge(ld[1].h, ld[1].b + 16 * (NH // 2))

            @blk.vector
            def _(v):
                v = self.chained(v, "vch")
                for c in range(NH):
                    v.wait_ge(ld[c % 2].h, ld[c % 2].b + 16 * (c // 2 + 1))
                    v.tensor_copy(wsb[:, E * c:E * (c + 1)], wtmp[c % 2][:])
                if xpos is None:
                    for sg in range(NST // GS):
                        for et in range(ne_tiles):
                            for s4 in range(GS):
                                k = (sg * ne_tiles + et) * GS + s4
                                v.wait_ge(mm.h, mmb + k + 1)
                                if k >= 4:
                                    v.wait_ge(st[k % 4].h, st[k % 4].b + 16 * (k // 4))
                                v.tensor_scalar(ob[k % 4][:], pj[(et % 2) * GS + s4][:],
                                                scale, None, ALU.mult)
                else:
                    for s_ in range(NST):
                        for p in range(8):
                            j = s_ * 8 + p
                            v.wait_ge(sc.h, scb + 16 * s_ + 8 + p + 1)
                            v.wait_ge(tl[j % 2].h, tl[j % 2].b + 32 * (j // 2 + 1))
                            qe = qraw[:, 512 * p:512 * (p + 1)]
                            qo = qraw[:, 512 * (8 + p):512 * (9 + p)]
                            co, si = tcos[j % 2][:], tsin[j % 2][:]
                            if j >= 2:
                                v.wait_ge(st[j % 2].h, st[j % 2].b + 32 * (j // 2))
                            v.tensor_tensor(tmp1[:], qe, co, ALU.mult)
                            v.tensor_tensor(tmp2[:], qo, si, ALU.mult)
                            v.tensor_tensor(ob[(2 * j) % 4][:], tmp1[:], tmp2[:], ALU.subtract)
                            v.tensor_tensor(tmp1[:], qo, co, ALU.mult)
                            v.tensor_tensor(tmp2[:], qe, si, ALU.mult)
                            v.tensor_tensor(ob[(2 * j + 1) % 4][:], tmp1[:], tmp2[:], ALU.add)

            @blk.scalar
            def _(s):
                if xpos is None:
                    return
                s = self.chained(s, "sch")
                for s_ in range(NST):
                    if s_ >= 1:
                        s.wait_ge(vc.h, vcb + NH + 6 * 8 * s_)
                    for et in range(ne_tiles):
                        k = s_ * ne_tiles + et
                        s.wait_ge(mm.h, mmb + k + 1)
                        s.copy(qraw[:, 512 * et:512 * (et + 1)], pj[k % 4][:])

            @blk.gpsimd
            def _(g):
                if xpos is None:
                    for sg in range(NST // GS):
                        for et in range(ne_tiles):
                            for s4 in range(GS):
                                s_ = GS * sg + s4
                                k = (sg * ne_tiles + et) * GS + s4
                                g.wait_ge(vc.h, vcb + NH + k + 1)
                                if k >= 4:
                                    g.wait_ge(st[k % 4].h, st[k % 4].b + 16 * (k // 4))
                                g.dma_start(outd[128 * et:128 * (et + 1), 512 * s_:512 * (s_ + 1)],
                                            ob[k % 4][:]).then_inc(st[k % 4].h, 16)
                    nkk = nk * GS // GS
                    for q in range(4):
                        g.wait_ge(st[q].h, st[q].b + 16 * ((nkk - q + 3) // 4))
                else:
                    for s_ in range(NST):
                        for p in range(8):
                            j = s_ * 8 + p
                            g.wait_ge(vc.h, vcb + NH + 6 * (j + 1))
                            if j >= 2:
                                g.wait_ge(st[j % 2].h, st[j % 2].b + 32 * (j // 2))
                            g.dma_start(outd[128 * p:128 * (p + 1), 512 * s_:512 * (s_ + 1)],
                                        ob[(2 * j) % 4][:]).then_inc(st[j % 2].h, 16)
                            g.dma_start(outd[1024 + 128 * p:1024 + 128 * (p + 1), 512 * s_:512 * (s_ + 1)],
                                        ob[(2 * j + 1) % 4][:]).then_inc(st[j % 2].h, 16)
                    g.wait_ge(st[0].h, st[0].b + 32 * ((npairs + 1) // 2))
                    g.wait_ge(st[1].h, st[1].b + 32 * (npairs // 2))

            @blk.tensor
            def _(t):
                t.wait_ge(vc.h, vcb + NH)
                if xpos is not None:
                    for s_ in range(NST):
                        t.wait_ge(xl[s_ % NXB].h, xl[s_ % NXB].b + xrec[s_])
                        for et in range(ne_tiles):
                            k = s_ * ne_tiles + et
                            if k >= 4:
                                t.wait_ge(sc.h, scb + k - 3)
                            for c in range(NH):
                                ins = t.matmul(pj[k % 4][:],
                                               wsb[:, E * c + 128 * et:E * c + 128 * (et + 1)],
                                               xts[s_ % NXB][:, 512 * c:512 * (c + 1)],
                                               start=(c == 0), stop=(c == NH - 1))
                            ins.then_inc(mm.h, 1)
                else:
                    for sg in range(NST // GS):
                        for s4 in range(GS):
                            s_ = GS * sg + s4
                            t.wait_ge(xl[s_ % NXB].h, xl[s_ % NXB].b + xrec[s_])
                        for et in range(ne_tiles):
                            ke = sg * ne_tiles + et
                            if ke >= 2:
                                t.wait_ge(vc.h, vcb + NH + (ke - 2) * GS + GS)
                            for c in range(NH):
                                for s4 in range(GS):
                                    s_ = GS * sg + s4
                                    ins = t.matmul(pj[(et % 2) * GS + s4][:],
                                                   wsb[:, E * c + 128 * et:E * c + 128 * (et + 1)],
                                                   xts[s_ % NXB][:, 512 * c:512 * (c + 1)],
                                                   start=(c == 0), stop=(c == NH - 1))
                                    if c == NH - 1:
                                        ins.then_inc(mm.h, 1)

        ld[0].b += 16 * ((NH + 1) // 2)
        ld[1].b += 16 * (NH // 2)
        for q in range(NXB):
            xl[q].b += xtot[q]
        mm.b += nk
        if xpos is None:
            assert vc.b == vcb + NH + nk
            for q in range(4):
                st[q].b += 16 * ((nk - q + 3) // 4)
        else:
            assert vc.b == vcb + NH + 6 * npairs, (vc.b, vcb)
            assert sc.b == scb + nk
            st[0].b += 32 * ((npairs + 1) // 2)
            st[1].b += 32 * (npairs // 2)
            tl[0].b += 32 * ((npairs + 1) // 2)
            tl[1].b += 32 * (npairs // 2)
        pc.close()


    def ph_wprep(self, src_name, out_name, nbeta, ncols, hcross, natile,
                 su_name, mode, colrow_name=None, bounce_name="wbounce"):
        nc = self.nc
        d = self.dram
        srcd, outd, bounce = d[src_name], d[out_name], d[bounce_name]
        SBW = 2048
        nsbc = ncols // SBW
        scale = 1.0 / math.sqrt(128 * natile)
        pc = ExitStack()
        wtmp = [self.sbuf([128, SBW], F32, ctx=pc) for _ in range(4)]
        wtmpb = [self.sbuf([128, SBW], BF16, ctx=pc) for _ in range(4)]
        wbf = [self.sbuf([128, SBW], BF16, ctx=pc) for _ in range(2)]
        ob = [self.sbuf([128, SBW], BF16, ctx=pc) for _ in range(2)]
        pj = [self.psum([128, 512], F32, ctx=pc) for _ in range(8)]
        sucol = self.sbuf([128, natile], F32, ctx=pc)
        ld = [self.ctr("ld0"), self.ctr("ld1"), self.ctr("ld2"), self.ctr("ld3")]
        st = [self.ctr("st0"), self.ctr("st1")]
        sl = self.ctr("ld4")
        mm = self.ctr("mm")
        vc, sc = self.ctr("vch"), self.ctr("sch")
        if mode == "rowcol":
            colb = self.sbuf([128, ncols], F32, ctx=pc)
            crow = self.sbuf([1, ncols], F32, ctx=pc)

        def stage(srcap, dstap, lhsT, do_scale):
            bufs = wtmp if not do_scale else wtmpb
            nsb = nbeta * nsbc if not do_scale else natile * nsbc
            vcb, scb, mmb = vc.b, sc.b, mm.b
            moff = ncols // 512 if (do_scale and mode == "rowcol") else 0
            is_a = not do_scale
            vper = (1 + 4) if is_a else 4

            def vpos(k):
                return vcb + (moff if not is_a else 0) + vper * (k + 1)

            with nc.Block() as blk:
                @blk.sync
                def _(sync):
                    if do_scale and mode == "rowcol":
                        sync.dma_start(crow[:], d[colrow_name][:]).then_inc(sl.h, 16)
                    if is_a:
                        sync.dma_start(sucol[:], d[su_name][:]).then_inc(sl.h, 16)
                    for k in range(0, nsb, 2):
                        if k >= 4:
                            if is_a:
                                sync.wait_ge(vc.h, vpos(k - 4) - vper + 1)
                            else:
                                sync.wait_ge(mm.h, mmb + moff + 4 * (k - 4) + 4)
                        sync.dma_start(bufs[k % 4][:], srcap(k)).then_inc(ld[k % 4].h, 16)
                    sync.wait_ge(ld[0].h, ld[0].b + 16 * ((nsb + 3) // 4))
                    sync.wait_ge(ld[2].h, ld[2].b + 16 * ((nsb + 1) // 4))

                @blk.scalar
                def _(s):
                    for k in range(1, nsb, 2):
                        if k >= 4:
                            if is_a:
                                s.wait_ge(vc.h, vpos(k - 4) - vper + 1)
                            else:
                                s.wait_ge(mm.h, mmb + moff + 4 * (k - 4) + 4)
                        s.dma_start(bufs[k % 4][:], srcap(k)).then_inc(ld[k % 4].h, 16)
                    s.wait_ge(ld[1].h, ld[1].b + 16 * ((nsb + 2) // 4))
                    s.wait_ge(ld[3].h, ld[3].b + 16 * (nsb // 4))

                @blk.vector
                def _(v):
                    v = self.chained(v, "vch")
                    if do_scale and mode == "rowcol":
                        v.wait_ge(sl.h, sl.b + 16)
                        for ch in range(moff):
                            v.wait_ge(mm.h, mmb + ch + 1)
                            v.tensor_scalar(colb[:, 512 * ch:512 * (ch + 1)], pj[ch % 2][:],
                                            scale, None, ALU.mult)
                    for k in range(nsb):
                        a_ = k // nsbc
                        v.wait_ge(ld[k % 4].h, ld[k % 4].b + 16 * (k // 4 + 1))
                        if is_a:
                            v.tensor_copy(wbf[k % 2][:], bufs[k % 4][:])
                        if k >= 2:
                            v.wait_ge(st[k % 2].h, st[k % 2].b + 16 * (k // 2))
                        for q in range(4):
                            v.wait_ge(mm.h, mmb + moff + 4 * k + q + 1)
                            dst = ob[k % 2][:, 512 * q:512 * (q + 1)]
                            ps = pj[(k % 2) * 4 + q][:]
                            if not do_scale:
                                v.tensor_copy(dst, ps)
                            elif mode == "row":
                                v.tensor_scalar(dst, ps, sucol[:, a_:a_ + 1], scale,
                                                ALU.mult, ALU.mult)
                            else:
                                cs = colb[:, 512 * ((k % nsbc) * 4 + q):512 * ((k % nsbc) * 4 + q + 1)]
                                v.scalar_tensor_tensor(dst, ps, sucol[:, a_:a_ + 1], cs,
                                                       ALU.mult, ALU.mult)

                @blk.gpsimd
                def _(g):
                    for k in range(nsb):
                        g.wait_ge(vc.h, vpos(k))
                        if k >= 2:
                            g.wait_ge(st[k % 2].h, st[k % 2].b + 16 * (k // 2))
                        g.dma_start(dstap(k), ob[k % 2][:]).then_inc(st[k % 2].h, 16)
                    g.wait_ge(st[0].h, st[0].b + 16 * ((nsb + 1) // 2))
                    g.wait_ge(st[1].h, st[1].b + 16 * (nsb // 2))

                @blk.tensor
                def _(t):
                    if do_scale and mode == "rowcol":
                        t.wait_ge(sl.h, sl.b + 16 + (16 if is_a else 0))
                        for ch in range(moff):
                            if ch >= 2:
                                t.wait_ge(vc.h, vcb + ch - 1)
                            t.matmul(pj[ch % 2][:], self.onesrow[:],
                                     crow[:, 512 * ch:512 * (ch + 1)],
                                     start=True, stop=True).then_inc(mm.h, 1)
                        t.wait_ge(vc.h, vcb + moff)
                    if is_a:
                        t.wait_ge(sl.h, sl.b + 16)
                    for k in range(nsb):
                        if is_a:
                            t.wait_ge(vc.h, vcb + vper * k + 1)
                        else:
                            t.wait_ge(ld[k % 4].h, ld[k % 4].b + 16 * (k // 4 + 1))
                        if k >= 2:
                            t.wait_ge(vc.h, vpos(k - 2))
                        rhs = wbf[k % 2] if is_a else bufs[k % 4]
                        for q in range(4):
                            t.matmul(pj[(k % 2) * 4 + q][:], lhsT[:],
                                     rhs[:, 512 * q:512 * (q + 1)],
                                     start=True, stop=True).then_inc(mm.h, 1)

            for qq in range(4):
                ld[qq].b += 16 * ((nsb - qq + 3) // 4)
            st[0].b += 16 * ((nsb + 1) // 2)
            st[1].b += 16 * (nsb // 2)
            mm.b += moff + 4 * nsb
            sl.b += (16 if (do_scale and mode == "rowcol") else 0) + (16 if is_a else 0)
            assert vc.b == vpos(nsb - 1), (vc.b, vpos(nsb - 1))

        hx = self.h16x8b if hcross == 16 else self.h64x2b
        stage(lambda k: srcd[128 * (k // nsbc):128 * (k // nsbc + 1),
                             SBW * (k % nsbc):SBW * (k % nsbc + 1)],
              lambda k: bounce[128 * (k // nsbc):128 * (k // nsbc + 1),
                               SBW * (k % nsbc):SBW * (k % nsbc + 1)],
              hx, False)
        brd = bounce.rearrange("(p g) n -> g p n", g=natile)
        stage(lambda k: brd[k // nsbc, :, SBW * (k % nsbc):SBW * (k % nsbc + 1)],
              lambda k: outd[128 * (k // nsbc):128 * (k // nsbc + 1),
                             SBW * (k % nsbc):SBW * (k % nsbc + 1)],
              self.h128b, True)
        pc.close()

    def ph_wcast(self, src_name, out_name, nrows, ncols):
        nc = self.nc
        d = self.dram
        srcd, outd = d[src_name], d[out_name]
        CW = 2048
        nrt, nct = nrows // 128, ncols // CW
        pc = ExitStack()
        wtmp = [self.sbuf([128, CW], U8, ctx=pc) for _ in range(4)]
        ob = [self.sbuf([128, CW], BF16, ctx=pc) for _ in range(4)]
        ld = [self.ctr("ld0"), self.ctr("ld1"), self.ctr("ld2"), self.ctr("ld3")]
        st = [self.ctr("st0"), self.ctr("st1"), self.ctr("st2"), self.ctr("st3")]
        vc, sc = self.ctr("vch"), self.ctr("sch")
        vcb, scb = vc.b, sc.b
        nblk = nrt * nct

        def cpos(k):
            return (vcb if k % 2 == 0 else scb) + k // 2 + 1

        with nc.Block() as blk:
            @blk.sync
            def _(sync):
                for k in range(0, nblk, 2):
                    r, ch = k // nct, k % nct
                    if k >= 4:
                        sync.wait_ge(vc.h, vcb + (k - 4) // 2 + 1)
                    sync.dma_start(wtmp[k % 4][:],
                                   srcd[128 * r:128 * (r + 1), CW * ch:CW * (ch + 1)]).then_inc(ld[k % 4].h, 16)
                sync.wait_ge(ld[0].h, ld[0].b + 16 * ((nblk + 3) // 4))
                sync.wait_ge(ld[2].h, ld[2].b + 16 * ((nblk + 1) // 4))

            @blk.scalar
            def _(s):
                sch = self.chained(s, "sch")
                odds = list(range(1, nblk, 2))

                def dma_odd(k):
                    r, ch = k // nct, k % nct
                    s.dma_start(wtmp[k % 4][:],
                                srcd[128 * r:128 * (r + 1), CW * ch:CW * (ch + 1)]).then_inc(ld[k % 4].h, 16)
                for k in odds[:2]:
                    dma_odd(k)
                for i, k in enumerate(odds):
                    sch.wait_ge(ld[k % 4].h, ld[k % 4].b + 16 * (k // 4 + 1))
                    if k >= 4:
                        sch.wait_ge(st[k % 4].h, st[k % 4].b + 16 * (k // 4))
                    sch.activation(ob[k % 4][:], wtmp[k % 4][:], AF.Copy,
                                   bias=-128.0)
                    if i + 2 < len(odds):
                        s.wait_ge(sc.h, scb + i + 1)
                        dma_odd(odds[i + 2])
                s.wait_ge(ld[1].h, ld[1].b + 16 * ((nblk + 2) // 4))
                s.wait_ge(ld[3].h, ld[3].b + 16 * (nblk // 4))

            @blk.vector
            def _(v):
                v = self.chained(v, "vch")
                for k in range(0, nblk, 2):
                    v.wait_ge(ld[k % 4].h, ld[k % 4].b + 16 * (k // 4 + 1))
                    if k >= 4:
                        v.wait_ge(st[k % 4].h, st[k % 4].b + 16 * (k // 4))
                    v.tensor_scalar(ob[k % 4][:], wtmp[k % 4][:], 128.0, None, ALU.subtract)

            @blk.gpsimd
            def _(g):
                for k in range(nblk):
                    r, ch = k // nct, k % nct
                    g.wait_ge(vc.h if k % 2 == 0 else sc.h, cpos(k))
                    if k >= 4:
                        g.wait_ge(st[k % 4].h, st[k % 4].b + 16 * (k // 4))
                    g.dma_start(outd[128 * r:128 * (r + 1), CW * ch:CW * (ch + 1)],
                                ob[k % 4][:]).then_inc(st[k % 4].h, 16)
                for q in range(4):
                    g.wait_ge(st[q].h, st[q].b + 16 * ((nblk - q + 3) // 4))
        for q in range(4):
            ld[q].b += 16 * ((nblk - q + 3) // 4)
            st[q].b += 16 * ((nblk - q + 3) // 4)
        assert vc.b == vcb + (nblk + 1) // 2 and sc.b == scb + nblk // 2
        pc.close()

    def ph_ffn_up(self, hup_name, hgate_name, scu_name, scg_name):
        nc = self.nc
        d = self.dram
        hup, hgate, t0, xnT = d[hup_name], d[hgate_name], d["t0"], d["xnT"]
        pc = ExitStack()
        xth = self.sbuf([128, NH * (S // 2)], BF16, ctx=pc)
        wub = [self.sbuf([128, NH * 512], BF16, ctx=pc) for _ in range(2)]
        wgb = [self.sbuf([128, NH * 512], BF16, ctx=pc) for _ in range(2)]
        scu = self.sbuf([128, NI], F32, ctx=pc)
        scg = self.sbuf([128, NI], F32, ctx=pc)
        gs = [self.sbuf([128, 512], F32, ctx=pc) for _ in range(2)]
        ob = [self.sbuf([128, 512], BF16, ctx=pc) for _ in range(2)]
        pu = [self.psum([128, 512], F32, ctx=pc) for _ in range(4)]
        pg = [self.psum([128, 512], F32, ctx=pc) for _ in range(4)]
        xl = self.ctr("ld0")
        sl = self.ctr("ld1")
        wl = [self.ctr("ld2"), self.ctr("ld3")]
        st = [self.ctr("st0"), self.ctr("st1")]
        mm = self.ctr("mm")
        vc, sc = self.ctr("vch"), self.ctr("sch")
        vcb, scb, mmb = vc.b, sc.b, mm.b
        NSUP = 16
        TPH = S // 2
        NSTH = TPH // 512
        KPH = NSUP * NSTH * 4
        wtot, wrec = [0, 0], {}
        with nc.Block() as blk:
            @blk.sync
            def _(sync):
                hup3 = hup.rearrange("(c p) n -> p c n", p=128)
                hgate3 = hgate.rearrange("(c p) n -> p c n", p=128)
                sync.dma_start(scu[:], d[scu_name][:]).then_inc(sl.h, 16)
                sync.dma_start(scg[:], d[scg_name][:]).then_inc(sl.h, 16)
                for half in range(2):
                    if half == 1:
                        sync.wait_ge(mm.h, mmb + KPH)
                    for c in range(NH):
                        sync.dma_start(xth[:, TPH * c:TPH * (c + 1)],
                                       xnT[128 * c:128 * (c + 1), TPH * half:TPH * (half + 1)]).then_inc(xl.h, 16)
                    for isup in range(NSUP):
                        g_ = half * NSUP + isup
                        if g_ >= 2:
                            sync.wait_ge(mm.h, mmb + 4 * NSTH * (g_ - 1))
                        wtot[isup % 2] += self.dmaN(sync, wl[isup % 2].h,
                                                    wub[isup % 2][:].rearrange("p (c n) -> p c n", n=512),
                                                    hup3[:, :, 512 * isup:512 * (isup + 1)])
                        wtot[isup % 2] += self.dmaN(sync, wl[isup % 2].h,
                                                    wgb[isup % 2][:].rearrange("p (c n) -> p c n", n=512),
                                                    hgate3[:, :, 512 * isup:512 * (isup + 1)])
                        wrec[g_] = wtot[isup % 2]
                sync.wait_ge(xl.h, xl.b + 16 * 2 * NH)
                sync.wait_ge(sl.h, sl.b + 32)
                sync.wait_ge(wl[0].h, wl[0].b + wtot[0])
                sync.wait_ge(wl[1].h, wl[1].b + wtot[1])

            @blk.tensor
            def _(t):
                for half in range(2):
                    t.wait_ge(xl.h, xl.b + 16 * NH * (half + 1))
                    for isup in range(NSUP):
                        g_ = half * NSUP + isup
                        t.wait_ge(wl[isup % 2].h, wl[isup % 2].b + wrec[g_])
                        for st_ in range(NSTH):
                            for it in range(4):
                                k = g_ * 4 * NSTH + st_ * 4 + it
                                if k >= 4:
                                    t.wait_ge(vc.h, vcb + k - 3)
                                    t.wait_ge(sc.h, scb + k - 3)
                                for c in range(NH):
                                    t.matmul(pu[k % 4][:],
                                             wub[isup % 2][:, 512 * c + 128 * it:512 * c + 128 * (it + 1)],
                                             xth[:, TPH * c + 512 * st_:TPH * c + 512 * (st_ + 1)],
                                             start=(c == 0), stop=(c == NH - 1))
                                for c in range(NH):
                                    ins = t.matmul(pg[k % 4][:],
                                                   wgb[isup % 2][:, 512 * c + 128 * it:512 * c + 128 * (it + 1)],
                                                   xth[:, TPH * c + 512 * st_:TPH * c + 512 * (st_ + 1)],
                                                   start=(c == 0), stop=(c == NH - 1))
                                ins.then_inc(mm.h, 1)

            @blk.scalar
            def _(s):
                s = self.chained(s, "sch")
                s.wait_ge(sl.h, sl.b + 32)
                for half in range(2):
                    for isup in range(NSUP):
                        for st_ in range(NSTH):
                            for it in range(4):
                                k = (half * NSUP + isup) * 4 * NSTH + st_ * 4 + it
                                i_abs = isup * 4 + it
                                s.wait_ge(mm.h, mmb + k + 1)
                                if k >= 2:
                                    s.wait_ge(vc.h, vcb + k - 1)
                                s.activation(gs[k % 2][:], pg[k % 4][:], AF.Silu,
                                             scale=scg[:, i_abs:i_abs + 1])

            @blk.vector
            def _(v):
                v = self.chained(v, "vch")
                v.wait_ge(sl.h, sl.b + 32)
                for half in range(2):
                    for isup in range(NSUP):
                        for st_ in range(NSTH):
                            for it in range(4):
                                k = (half * NSUP + isup) * 4 * NSTH + st_ * 4 + it
                                i_abs = isup * 4 + it
                                v.wait_ge(sc.h, scb + k + 1)
                                if k >= 2:
                                    v.wait_ge(st[k % 2].h, st[k % 2].b + 16 * (k // 2))
                                v.scalar_tensor_tensor(ob[k % 2][:], pu[k % 4][:],
                                                       scu[:, i_abs:i_abs + 1], gs[k % 2][:],
                                                       ALU.mult, ALU.mult)

            @blk.gpsimd
            def _(g):
                for half in range(2):
                    for isup in range(NSUP):
                        for st_ in range(NSTH):
                            for it in range(4):
                                k = (half * NSUP + isup) * 4 * NSTH + st_ * 4 + it
                                i_abs = isup * 4 + it
                                st_abs = half * NSTH + st_
                                g.wait_ge(vc.h, vcb + k + 1)
                                if k >= 2:
                                    g.wait_ge(st[k % 2].h, st[k % 2].b + 16 * (k // 2))
                                g.dma_start(t0[128 * i_abs:128 * (i_abs + 1), 512 * st_abs:512 * (st_abs + 1)],
                                            ob[k % 2][:]).then_inc(st[k % 2].h, 16)
                g.wait_ge(st[0].h, st[0].b + 16 * KPH)
                g.wait_ge(st[1].h, st[1].b + 16 * KPH)

        xl.b += 32 * NH
        sl.b += 32
        wl[0].b += wtot[0]
        wl[1].b += wtot[1]
        st[0].b += 16 * KPH
        st[1].b += 16 * KPH
        mm.b += 2 * KPH
        assert vc.b == vcb + 2 * KPH and sc.b == scb + 2 * KPH
        pc.close()

    def ph_ffn_down(self, hdn_name, res_name, out_name, coln_name=None):
        nc = self.nc
        d = self.dram
        hdn, res, outd, t0 = d[hdn_name], d[res_name], d[out_name], d["t0"]
        pc = ExitStack()
        wdb = self.sbuf([128, NI * 1024], BF16, ctx=pc)
        tb = [self.sbuf([128, NI * 128], BF16, ctx=pc) for _ in range(2)]
        rb = [self.sbuf([128, 1024], F32, ctx=pc) for _ in range(2)]
        ob = [self.sbuf([128, 1024], F32, ctx=pc) for _ in range(2)]
        pj = [self.psum([128, 512], F32, ctx=pc) for _ in range(4)]
        scaled = coln_name is not None
        if scaled:
            pjr = [self.psum([128, 512], F32, ctx=pc) for _ in range(2)]
            colb = self.sbuf([128, 1024], F32, ctx=pc)
            crow = self.sbuf([1, H], F32, ctx=pc)
        wl = self.ctr("ld0")
        tl = [self.ctr("ld2"), self.ctr("ld3")]
        rl = [self.ctr("ld4"), self.ctr("ld5")]
        st = [self.ctr("st0"), self.ctr("st1")]
        mm = self.ctr("mm")
        vc = self.ctr("vch")
        vcb, mmb = vc.b, mm.b
        nevt = 4 if scaled else 2
        nrep = 2 if scaled else 0

        ttot, trec = [0, 0], {}

        def posv(k):
            return vcb + nrep * (k // NT + 1) + nevt * (k + 1)

        def posm(k):
            return mmb + nrep * (k // NT + 1) + (k + 1)

        with nc.Block() as blk:
            @blk.sync
            def _(sync):
                t03 = t0.rearrange("(c p) n -> p c n", p=128)
                if scaled:
                    sync.dma_start(crow[:], d[coln_name][:]).then_inc(wl.h, 16)
                for half in range(2):
                    if half == 1:
                        sync.wait_ge(mm.h, posm(NT - 1))
                    for c in range(NI):
                        sync.dma_start(wdb[:, 1024 * c:1024 * (c + 1)],
                                       hdn[128 * c:128 * (c + 1), 1024 * half:1024 * (half + 1)]).then_inc(wl.h, 16)
                    for i in range(NT):
                        k = half * NT + i
                        if k >= 2:
                            sync.wait_ge(mm.h, posm(k - 2))
                        ttot[k % 2] += self.dmaN(sync, tl[k % 2].h,
                                                 tb[k % 2][:].rearrange("p (c n) -> p c n", n=128),
                                                 t03[:, :, 128 * i:128 * (i + 1)])
                        trec[k] = ttot[k % 2]
                sync.wait_ge(wl.h, wl.b + (16 if scaled else 0) + 16 * 2 * NI)
                sync.wait_ge(tl[0].h, tl[0].b + ttot[0])
                sync.wait_ge(tl[1].h, tl[1].b + ttot[1])

            @blk.scalar
            def _(s):
                for half in range(2):
                    for i in range(NT):
                        k = half * NT + i
                        if k >= 2:
                            s.wait_ge(vc.h, posv(k - 2))
                        s.dma_start(rb[k % 2][:],
                                    res[128 * i:128 * (i + 1), 1024 * half:1024 * (half + 1)]).then_inc(rl[k % 2].h, 16)
                s.wait_ge(rl[0].h, rl[0].b + 16 * NT)
                s.wait_ge(rl[1].h, rl[1].b + 16 * NT)

            @blk.tensor
            def _(t):
                for half in range(2):
                    t.wait_ge(wl.h, wl.b + (16 if scaled else 0) + 16 * NI * (half + 1))
                    if scaled:
                        for ch in range(2):
                            t.matmul(pjr[ch][:], self.onesrow[:],
                                     crow[:, 1024 * half + 512 * ch:1024 * half + 512 * (ch + 1)],
                                     start=True, stop=True).then_inc(mm.h, 1)
                    for i in range(NT):
                        k = half * NT + i
                        t.wait_ge(tl[k % 2].h, tl[k % 2].b + trec[k])
                        if k >= 2:
                            t.wait_ge(vc.h, posv(k - 2))
                        for c in range(NI):
                            for hc in range(2):
                                ins = t.matmul(pj[(k % 2) * 2 + hc][:],
                                               tb[k % 2][:, 128 * c:128 * (c + 1)],
                                               wdb[:, 1024 * c + 512 * hc:1024 * c + 512 * (hc + 1)],
                                               start=(c == 0), stop=(c == NI - 1))
                        ins.then_inc(mm.h, 1)

            @blk.vector
            def _(v):
                v = self.chained(v, "vch")
                for half in range(2):
                    if scaled:
                        for ch in range(2):
                            v.wait_ge(mm.h, mmb + nrep * half + (NT * half) + ch + 1)
                            v.tensor_copy(colb[:, 512 * ch:512 * (ch + 1)], pjr[ch][:])
                    for i in range(NT):
                        k = half * NT + i
                        v.wait_ge(mm.h, posm(k))
                        v.wait_ge(rl[k % 2].h, rl[k % 2].b + 16 * (k // 2 + 1))
                        if k >= 2:
                            v.wait_ge(st[k % 2].h, st[k % 2].b + 16 * (k // 2))
                        if not scaled:
                            for hc in range(2):
                                v.tensor_add(ob[k % 2][:, 512 * hc:512 * (hc + 1)],
                                             pj[(k % 2) * 2 + hc][:],
                                             rb[k % 2][:, 512 * hc:512 * (hc + 1)])
                        else:
                            for hc in range(2):
                                v.tensor_tensor(ob[k % 2][:, 512 * hc:512 * (hc + 1)],
                                                pj[(k % 2) * 2 + hc][:],
                                                colb[:, 512 * hc:512 * (hc + 1)], ALU.mult)
                            for hc in range(2):
                                v.tensor_add(ob[k % 2][:, 512 * hc:512 * (hc + 1)],
                                             ob[k % 2][:, 512 * hc:512 * (hc + 1)],
                                             rb[k % 2][:, 512 * hc:512 * (hc + 1)])

            @blk.gpsimd
            def _(g):
                for half in range(2):
                    for i in range(NT):
                        k = half * NT + i
                        g.wait_ge(vc.h, posv(k))
                        if k >= 2:
                            g.wait_ge(st[k % 2].h, st[k % 2].b + 16 * (k // 2))
                        g.dma_start(outd[128 * i:128 * (i + 1), 1024 * half:1024 * (half + 1)],
                                    ob[k % 2][:]).then_inc(st[k % 2].h, 16)
                g.wait_ge(st[0].h, st[0].b + 16 * NT)
                g.wait_ge(st[1].h, st[1].b + 16 * NT)

        wl.b += (16 if scaled else 0) + 32 * NI
        tl[0].b += ttot[0]
        tl[1].b += ttot[1]
        rl[0].b += 16 * NT
        rl[1].b += 16 * NT
        st[0].b += 16 * NT
        st[1].b += 16 * NT
        mm.b += 2 * nrep + 2 * NT
        assert vc.b == posv(2 * NT - 1), (vc.b, posv(2 * NT - 1))
        pc.close()


    def ph_kv1(self):
        nc = self.nc
        d = self.dram
        w, kTd, v1d, xnT = d["wkv1"], d["kT1"], d["v1"], d["xnT"]
        NST = S // 512
        GS = 4 if NST % 4 == 0 else (2 if NST % 2 == 0 else 1)
        NSG = NST // GS
        pc = ExitStack()
        wsb = self.sbuf([128, NH * 1024], BF16, ctx=pc)
        wtmp = [self.sbuf([128, 1024], F32, ctx=pc) for _ in range(2)]
        xts = [self.sbuf([128, 512 * NH], BF16, ctx=pc) for _ in range(GS)]
        ob = [self.sbuf([128, 512], BF16, ctx=pc) for _ in range(4)]
        pj = [self.psum([128, 512], F32, ctx=pc) for _ in range(8)]
        ld = [self.ctr("ld0"), self.ctr("ld1")]
        xl = [self.ctr("ld2"), self.ctr("ld3"), self.ctr("ld4"), self.ctr("ld5")][:GS]
        st = [self.ctr("st0"), self.ctr("st1"), self.ctr("st2"), self.ctr("st3")]
        mm = self.ctr("mm")
        vc = self.ctr("vch")
        vcb, mmb = vc.b, mm.b
        xtot = [0] * GS
        xrec = {}
        NKE = 4 * GS
        NVU = 4 * GS
        PSG = NKE + NVU

        def evpos(sg, i):
            return vcb + NH + PSG * sg + i + 1

        with nc.Block() as blk:
            @blk.sync
            def _(sync):
                for c in range(NH):
                    if c >= 2:
                        sync.wait_ge(vc.h, vcb + c - 1)
                    sync.dma_start(wtmp[c % 2][:],
                                   w[128 * c:128 * (c + 1), :]).then_inc(ld[c % 2].h, 16)
                xnT3 = xnT.rearrange("(c p) n -> p c n", p=128)
                for s_ in range(NST):
                    sl_ = s_ % GS
                    if s_ >= GS:
                        sync.wait_ge(vc.h, evpos(s_ // GS - 1, PSG - 1))
                    xtot[sl_] += self.dmaN(sync, xl[sl_].h,
                                           xts[sl_][:].rearrange("p (c n) -> p c n", n=512),
                                           xnT3[:, :, 512 * s_:512 * (s_ + 1)])
                    xrec[s_] = xtot[sl_]
                for q in range(GS):
                    sync.wait_ge(xl[q].h, xl[q].b + xtot[q])
                sync.wait_ge(ld[0].h, ld[0].b + 16 * ((NH + 1) // 2))
                sync.wait_ge(ld[1].h, ld[1].b + 16 * (NH // 2))

            @blk.vector
            def _(v):
                v = self.chained(v, "vch")
                for c in range(NH):
                    v.wait_ge(ld[c % 2].h, ld[c % 2].b + 16 * (c // 2 + 1))
                    v.tensor_copy(wsb[:, 1024 * c:1024 * (c + 1)], wtmp[c % 2][:])
                for sg in range(NSG):
                    for i in range(PSG):
                        k = PSG * sg + i
                        v.wait_ge(mm.h, mmb + k + 1)
                        if k >= 4:
                            v.wait_ge(st[k % 4].h, st[k % 4].b + 16 * (k // 4))
                        if i < NKE:
                            et, s4 = i // GS, i % GS
                            v.tensor_copy(ob[k % 4][:], pj[(et % 2) * GS + s4][:])
                        else:
                            tt = i - NKE
                            v.tensor_copy(ob[k % 4][:], pj[tt % 2][:])

            @blk.gpsimd
            def _(g):
                for sg in range(NSG):
                    for i in range(PSG):
                        k = PSG * sg + i
                        g.wait_ge(vc.h, evpos(sg, i))
                        if k >= 4:
                            g.wait_ge(st[k % 4].h, st[k % 4].b + 16 * (k // 4))
                        if i < NKE:
                            et, s4 = i // GS, i % GS
                            s_ = GS * sg + s4
                            g.dma_start(kTd[128 * et:128 * (et + 1), 512 * s_:512 * (s_ + 1)],
                                        ob[k % 4][:]).then_inc(st[k % 4].h, 16)
                        else:
                            tt = i - NKE
                            ta = (GS * sg * 4) + tt
                            g.dma_start(v1d[128 * ta:128 * (ta + 1), :],
                                        ob[k % 4][:]).then_inc(st[k % 4].h, 16)
                nk = PSG * NSG
                for q in range(4):
                    g.wait_ge(st[q].h, st[q].b + 16 * ((nk - q + 3) // 4))

            @blk.tensor
            def _(t):
                t.wait_ge(vc.h, vcb + NH)
                for sg in range(NSG):
                    for s4 in range(GS):
                        s_ = GS * sg + s4
                        t.wait_ge(xl[s_ % GS].h, xl[s_ % GS].b + xrec[s_])
                    if sg >= 1:
                        t.wait_ge(vc.h, evpos(sg - 1, PSG - 1))
                    for et in range(4):
                        if et >= 2:
                            t.wait_ge(vc.h, evpos(sg, (et - 2) * GS + GS - 1))
                        for c in range(NH):
                            for s4 in range(GS):
                                ins = t.matmul(pj[(et % 2) * GS + s4][:],
                                               wsb[:, 1024 * c + 128 * et:1024 * c + 128 * (et + 1)],
                                               xts[s4][:, 512 * c:512 * (c + 1)],
                                               start=(c == 0), stop=(c == NH - 1))
                                if c == NH - 1:
                                    ins.then_inc(mm.h, 1)
                    t.wait_ge(vc.h, evpos(sg, NKE - 1))
                    for tt in range(4 * GS):
                        s4 = tt // 4
                        if tt >= 2:
                            t.wait_ge(vc.h, evpos(sg, NKE + tt - 2))
                        for c in range(NH):
                            ins = t.matmul(pj[tt % 2][:],
                                           xts[s4][:, 512 * c + 128 * (tt % 4):512 * c + 128 * (tt % 4 + 1)],
                                           wsb[:, 1024 * c + 512:1024 * (c + 1)],
                                           start=(c == 0), stop=(c == NH - 1))
                        ins.then_inc(mm.h, 1)

        nk = PSG * NSG
        ld[0].b += 16 * ((NH + 1) // 2)
        ld[1].b += 16 * (NH // 2)
        for q in range(GS):
            xl[q].b += xtot[q]
        for q in range(4):
            st[q].b += 16 * ((nk - q + 3) // 4)
        mm.b += nk
        assert vc.b == vcb + NH + nk, (vc.b,)
        pc.close()

    def ph_attn1(self):
        nc = self.nc
        d = self.dram
        qT, kT1, v1, ctx = d["qT"], d["kT1"], d["v1"], d["ctx"]
        NST = S // 512
        pc = ExitStack()
        ksb = self.sbuf([128, 4 * S], BF16, ctx=pc)
        vsb = self.sbuf([128, NT * 512], BF16, ctx=pc)
        qts = [self.sbuf([128, 16 * 512], BF16, ctx=pc) for _ in range(2)]
        pbr = self.sbuf([128, 4 * 1024], BF16, ctx=pc)
        sqb = self.sbuf([128, 16 * 512], BF16, ctx=pc)
        qn2 = self.sbuf([1, 16 * 512], F32, ctx=pc)
        knrow = self.sbuf([1, S], F32, ctx=pc)
        mt = self.sbuf([1, 64], F32, ctx=pc)
        cbias = self.sbuf([128, 8], F32, ctx=pc)
        rrow = self.sbuf([1, 1024], F32, ctx=pc)
        rsb = self.sbuf([128, 1024], F32, ctx=pc)
        ob = [self.sbuf([128, 1024], BF16, ctx=pc) for _ in range(2)]
        psc = [self.psum([128, 1024], F32, ctx=pc) for _ in range(2)]
        pcs = [self.psum([128, 512], F32, ctx=pc) for _ in range(2)]
        patt = self.psum([128, 1024], F32, ctx=pc)
        kl, vl = self.ctr("ld0"), self.ctr("ld1")
        ql = [self.ctr("ld2"), self.ctr("ld3")]
        st = [self.ctr("st0"), self.ctr("st1")]
        mm = self.ctr("mm")
        vc, sc = self.ctr("vch"), self.ctr("sch")
        V0, S0, M0 = vc.b, sc.b, mm.b
        nj = lambda s_: 4 * s_ + 4
        NKCH = S // 512
        qtot, qrec, vstot = [0, 0], {}, [0]

        POS = {}
        vp, sp, mp = V0, S0, M0
        for g in range(4):
            vp += 1
            for ch in range(NKCH):
                mp += 1; POS[("kcs", g, ch)] = mp
                vp += 1; POS[("kev", g, ch)] = vp
            vp += 1; POS[("kmax", g)] = vp
            sp += 1; POS[("ksqrt", g)] = sp
        for s_ in range(NST):
            for h in range(16):
                vp += 1
            POS[("sqdone", s_)] = vp
            for h in range(16):
                mp += 1; POS[("qcs", s_, h)] = mp
                vp += 1; POS[("qev", s_, h)] = vp
                vp += 1; POS[("qmax", s_, h)] = vp
            for h in range(16):
                sp += 1; POS[("qsqrt", s_, h)] = sp
            for P in range(8):
                vp += 1
                vp += 1; POS[("negc", s_, P)] = vp
            for P in range(8):
                mp += 1; POS[("crepl", s_, P)] = mp
                vp += 1; POS[("cbias", s_, P)] = vp
            for P in range(8):
                n = nj(s_)
                for j in range(n):
                    mp += 1; POS[("sco", s_, P, j)] = mp
                    if j >= 1:
                        mp += 1; POS[("pv", s_, P, j - 1)] = mp
                mp += 1; POS[("cslast", s_, P)] = mp
                mp += 1; POS[("pv", s_, P, n - 1)] = mp
                mp += 2; POS[("repl", s_, P)] = mp
                for j in range(n):
                    sp += 1; POS[("exp", s_, P, j)] = sp
                for qc in range(4):
                    vp += 2 * qc + 2
                POS[("maskd", s_, P)] = vp
                vp += 2; POS[("recip", s_, P)] = vp
                vp += 1; POS[("rsb", s_, P)] = vp
                vp += 1; POS[("obtt", s_, P)] = vp
            POS[("stend", s_)] = mp
        VEND, SEND, MEND = vp, sp, mp

        def maskpos(s_, P, j):
            qc = j - 4 * s_
            base = POS[("cbias", s_, 7)] if False else None
            pref = sum(2 * q + 2 for q in range(qc + 1))
            return POS[("maskd", s_, P)] - sum(2 * q + 2 for q in range(4)) + pref

        with nc.Block() as blk:
            @blk.sync
            def _(sync):
                for g in range(4):
                    sync.dma_start(ksb[:, S * g:S * (g + 1)],
                                   kT1[128 * g:128 * (g + 1), :]).then_inc(kl.h, 16)
                vstot[0] = self.dmaN(sync, vl.h,
                                     vsb[:].rearrange("p (t n) -> p t n", n=512),
                                     v1.rearrange("(t p) n -> p t n", p=128))
                qT3 = qT.rearrange("(c p) n -> p c n", p=128)
                for s_ in range(NST):
                    if s_ >= 2:
                        sync.wait_ge(mm.h, POS[("stend", s_ - 1)])
                    qtot[s_ % 2] += self.dmaN(sync, ql[s_ % 2].h,
                                              qts[s_ % 2][:].rearrange("p (c n) -> p c n", n=512),
                                              qT3[:, :, 512 * s_:512 * (s_ + 1)])
                    qrec[s_] = qtot[s_ % 2]
                sync.wait_ge(ql[0].h, ql[0].b + qtot[0])
                sync.wait_ge(ql[1].h, ql[1].b + qtot[1])

            @blk.vector
            def _(v):
                v = self.chained(v, "vch")
                v.wait_ge(kl.h, kl.b + 64)
                for g in range(4):
                    if g >= 1:
                        v.wait_ge(mm.h, POS[("kcs", g - 1, NKCH - 1)])
                    v.tensor_tensor(sqb[:, 0:S], ksb[:, S * g:S * (g + 1)],
                                    ksb[:, S * g:S * (g + 1)], ALU.mult)
                    for ch in range(NKCH):
                        v.wait_ge(mm.h, POS[("kcs", g, ch)])
                        v.tensor_copy(knrow[:, 512 * ch:512 * (ch + 1)],
                                      pcs[(NKCH * g + ch) % 2][0:1, :])
                    v.reduce_max(mt[:, g:g + 1], knrow[:, 0:S], axis=AX.X)
                for s_ in range(NST):
                    v.wait_ge(ql[s_ % 2].h, ql[s_ % 2].b + qrec[s_])
                    if s_ >= 1:
                        v.wait_ge(mm.h, POS[("qcs", s_ - 1, 15)])
                    else:
                        v.wait_ge(mm.h, POS[("kcs", 3, NKCH - 1)])
                    for h in range(16):
                        v.tensor_tensor(sqb[:, 512 * h:512 * (h + 1)],
                                        qts[s_ % 2][:, 512 * h:512 * (h + 1)],
                                        qts[s_ % 2][:, 512 * h:512 * (h + 1)], ALU.mult)
                    for h in range(16):
                        v.wait_ge(mm.h, POS[("qcs", s_, h)])
                        v.tensor_copy(qn2[:, 512 * h:512 * (h + 1)], pcs[h % 2][0:1, :])
                        v.reduce_max(mt[:, 4 + h:5 + h], qn2[:, 512 * h:512 * (h + 1)], axis=AX.X)
                    for P in range(8):
                        v.wait_ge(sc.h, POS[("qsqrt", s_, 2 * P + 1)])
                        v.tensor_max(mt[:, 36 + P:37 + P], mt[:, 20 + 2 * P:21 + 2 * P],
                                     mt[:, 21 + 2 * P:22 + 2 * P])
                        v.tensor_scalar(mt[:, 36 + P:37 + P], mt[:, 36 + P:37 + P],
                                        mt[:, P // 2:P // 2 + 1], -1.0, ALU.mult, ALU.mult)
                    for P in range(8):
                        v.wait_ge(mm.h, POS[("crepl", s_, P)])
                        v.tensor_copy(cbias[:, P:P + 1], psc[0][:, P:P + 1])
                    for P in range(8):
                        n = nj(s_)
                        for qc in range(4):
                            j = 4 * s_ + qc
                            v.wait_ge(sc.h, POS[("exp", s_, P, j)])
                            pb = pbr[:, 1024 * (j % 4):1024 * (j % 4 + 1)]
                            for qc2 in range(qc):
                                v.memset(pb[:, 128 * qc2:128 * (qc2 + 1)], 0.0)
                                v.memset(pb[:, 512 + 128 * qc2:512 + 128 * (qc2 + 1)], 0.0)
                            v.tensor_tensor(pb[:, 128 * qc:128 * (qc + 1)],
                                            pb[:, 128 * qc:128 * (qc + 1)],
                                            self.utmaskb[:], ALU.mult)
                            v.tensor_tensor(pb[:, 512 + 128 * qc:512 + 128 * (qc + 1)],
                                            pb[:, 512 + 128 * qc:512 + 128 * (qc + 1)],
                                            self.utmaskb[:], ALU.mult)
                        v.wait_ge(mm.h, POS[("cslast", s_, P)])
                        v.reciprocal(rrow[:, 0:512], pcs[0][0:1, :])
                        v.reciprocal(rrow[:, 512:1024], pcs[1][0:1, :])
                        v.wait_ge(mm.h, POS[("repl", s_, P)])
                        v.tensor_copy(rsb[:], psc[0][:])
                        v.wait_ge(mm.h, POS[("pv", s_, P, n - 1)])
                        gi = 8 * s_ + P
                        if gi >= 2:
                            v.wait_ge(st[gi % 2].h, st[gi % 2].b + 32 * (gi // 2))
                        v.tensor_tensor(ob[gi % 2][:], patt[:], rsb[:], ALU.mult)

            @blk.scalar
            def _(s):
                s = self.chained(s, "sch")
                for g in range(4):
                    s.wait_ge(vc.h, POS[("kmax", g)])
                    s.activation(mt[:, g:g + 1], mt[:, g:g + 1], AF.Sqrt)
                for s_ in range(NST):
                    for h in range(16):
                        s.wait_ge(vc.h, POS[("qmax", s_, h)])
                        s.activation(mt[:, 20 + h:21 + h], mt[:, 4 + h:5 + h], AF.Sqrt)
                    for P in range(8):
                        n = nj(s_)
                        s.wait_ge(vc.h, POS[("cbias", s_, P)])
                        for j in range(n):
                            s.wait_ge(mm.h, POS[("sco", s_, P, j)])
                            if j >= 4:
                                s.wait_ge(mm.h, POS[("pv", s_, P, j - 4)])
                            s.activation(pbr[:, 1024 * (j % 4):1024 * (j % 4 + 1)],
                                         psc[j % 2][:], AF.Exp, bias=cbias[:, P:P + 1])

            @blk.gpsimd
            def _(g_):
                for s_ in range(NST):
                    for P in range(8):
                        gi = 8 * s_ + P
                        h0 = 2 * P
                        g_.wait_ge(vc.h, POS[("obtt", s_, P)])
                        if gi >= 2:
                            g_.wait_ge(st[gi % 2].h, st[gi % 2].b + 32 * (gi // 2))
                        g_.dma_start(ctx[128 * h0:128 * (h0 + 1), 512 * s_:512 * (s_ + 1)],
                                     ob[gi % 2][:, 0:512]).then_inc(st[gi % 2].h, 16)
                        g_.dma_start(ctx[128 * (h0 + 1):128 * (h0 + 2), 512 * s_:512 * (s_ + 1)],
                                     ob[gi % 2][:, 512:1024]).then_inc(st[gi % 2].h, 16)
                ngi = 8 * NST
                g_.wait_ge(st[0].h, st[0].b + 32 * (ngi // 2))
                g_.wait_ge(st[1].h, st[1].b + 32 * (ngi // 2))

            @blk.tensor
            def _(t):
                t.wait_ge(kl.h, kl.b + 64)
                for g in range(4):
                    t.wait_ge(vc.h, V0 + (NKCH + 2) * g + 1)
                    for ch in range(NKCH):
                        flat = NKCH * g + ch
                        if flat >= 2:
                            t.wait_ge(vc.h, POS[("kev", (flat - 2) // NKCH, (flat - 2) % NKCH)])
                        t.matmul(pcs[flat % 2][0:1, :], self.onescol[:],
                                 sqb[:, 512 * ch:512 * (ch + 1)],
                                 start=True, stop=True).then_inc(mm.h, 1)
                t.wait_ge(vl.h, vl.b + vstot[0])
                for s_ in range(NST):
                    n = nj(s_)
                    t.wait_ge(vc.h, POS[("sqdone", s_)])
                    if s_ >= 1:
                        t.wait_ge(vc.h, POS[("recip", s_ - 1, 7)])
                    for h in range(16):
                        if h >= 2:
                            t.wait_ge(vc.h, POS[("qev", s_, h - 2)])
                        t.matmul(pcs[h % 2][0:1, :], self.onescol[:],
                                 sqb[:, 512 * h:512 * (h + 1)],
                                 start=True, stop=True).then_inc(mm.h, 1)
                    if s_ >= 1:
                        t.wait_ge(vc.h, POS[("rsb", s_ - 1, 7)])
                    for P in range(8):
                        t.wait_ge(vc.h, POS[("negc", s_, P)])
                        t.matmul(psc[0][:, P:P + 1], self.onesrow[:],
                                 mt[:, 36 + P:37 + P],
                                 start=True, stop=True).then_inc(mm.h, 1)
                    for P in range(8):
                        gr = P // 2
                        h0 = 2 * P
                        if P == 0:
                            t.wait_ge(vc.h, POS[("cbias", s_, 7)])
                        else:
                            t.wait_ge(vc.h, POS[("rsb", s_, P - 1)])
                        for j in range(n):
                            if j >= 2:
                                t.wait_ge(sc.h, POS[("exp", s_, P, j - 2)])
                            lk = ksb[:, S * gr + 128 * j:S * gr + 128 * (j + 1)]
                            t.matmul(psc[j % 2][:, 0:512], lk,
                                     qts[s_ % 2][:, 512 * h0:512 * (h0 + 1)],
                                     start=True, stop=True)
                            t.matmul(psc[j % 2][:, 512:1024], lk,
                                     qts[s_ % 2][:, 512 * (h0 + 1):512 * (h0 + 2)],
                                     start=True, stop=True).then_inc(mm.h, 1)
                            if j >= 1:
                                jj = j - 1
                                pb = pbr[:, 1024 * (jj % 4):1024 * (jj % 4 + 1)]
                                if jj >= 4 * s_:
                                    t.wait_ge(vc.h, maskpos(s_, P, jj))
                                else:
                                    t.wait_ge(sc.h, POS[("exp", s_, P, jj)])
                                if jj == 0:
                                    t.wait_ge(vc.h, POS[("recip", s_, P - 1)] if P >= 1
                                              else (POS[("recip", s_ - 1, 7)] if s_ >= 1 else 0))
                                t.matmul(pcs[0][0:1, :], self.onescol[:], pb[:, 0:512],
                                         start=(jj == 0), stop=(jj == n - 1))
                                t.matmul(pcs[1][0:1, :], self.onescol[:], pb[:, 512:1024],
                                         start=(jj == 0), stop=(jj == n - 1))
                                lv = vsb[:, 512 * jj + 128 * gr:512 * jj + 128 * (gr + 1)]
                                if jj == 0:
                                    t.wait_ge(vc.h, POS[("obtt", s_, P - 1)] if P >= 1
                                              else (POS[("obtt", s_ - 1, 7)] if s_ >= 1 else 0))
                                t.matmul(patt[:, 0:512], lv, pb[:, 0:512],
                                         start=(jj == 0), stop=(jj == n - 1))
                                t.matmul(patt[:, 512:1024], lv, pb[:, 512:1024],
                                         start=(jj == 0), stop=(jj == n - 1)).then_inc(mm.h, 1)
                        jj = n - 1
                        pb = pbr[:, 1024 * (jj % 4):1024 * (jj % 4 + 1)]
                        t.wait_ge(vc.h, maskpos(s_, P, jj))
                        t.matmul(pcs[0][0:1, :], self.onescol[:], pb[:, 0:512],
                                 start=(jj == 0), stop=True)
                        t.matmul(pcs[1][0:1, :], self.onescol[:], pb[:, 512:1024],
                                 start=(jj == 0), stop=True).then_inc(mm.h, 1)
                        lv = vsb[:, 512 * jj + 128 * gr:512 * jj + 128 * (gr + 1)]
                        if jj == 0:
                            t.wait_ge(vc.h, POS[("obtt", s_, P - 1)] if P >= 1
                                      else (POS[("obtt", s_ - 1, 7)] if s_ >= 1 else 0))
                        t.matmul(patt[:, 0:512], lv, pb[:, 0:512],
                                 start=(jj == 0), stop=True)
                        t.matmul(patt[:, 512:1024], lv, pb[:, 512:1024],
                                 start=(jj == 0), stop=True).then_inc(mm.h, 1)
                        t.wait_ge(vc.h, POS[("recip", s_, P)])
                        t.matmul(psc[0][:, 0:512], self.onesrow[:], rrow[:, 0:512],
                                 start=True, stop=True).then_inc(mm.h, 1)
                        t.matmul(psc[0][:, 512:1024], self.onesrow[:], rrow[:, 512:1024],
                                 start=True, stop=True).then_inc(mm.h, 1)

        ngi = 8 * NST
        kl.b += 64
        vl.b += vstot[0]
        ql[0].b += qtot[0]
        ql[1].b += qtot[1]
        st[0].b += 32 * (ngi // 2)
        st[1].b += 32 * (ngi // 2)
        mm.b = MEND
        assert vc.b == VEND, (vc.b, VEND)
        assert sc.b == SEND, (sc.b, SEND)
        pc.close()

def build_program():
    nc = bass.Bass("TRN2", target_bir_lowering=False)
    p = Prog(nc)
    f32in = [("x0", [S, H]), ("wqp", [H, H]), ("wkp", [H, H]), ("wv", [H, H]),
             ("wupts", [H, I]), ("wgatets", [H, I]), ("wdnts", [I, H]),
             ("su_ug", [128, 16]), ("su_dp", [128, 64]),
             ("wsc_up", [128, 64]), ("wsc_gate", [128, 64]), ("wsc_dp", [1, H]),
             ("wq1", [H, H]), ("wkv1", [H, 1024]), ("wdense1", [H, H]),
             ("sc_up1", [128, 64]), ("sc_gate1", [128, 64]), ("sc_dn1", [1, H]),
             ("qcos", [1024, S]), ("qsin", [1024, S]),
             ("kcos", [1024, S]), ("ksin", [1024, S]),
             ("ident", [128, 128]), ("h128", [128, 128]), ("h16x8", [128, 128]),
             ("h64x2", [128, 128]), ("utmask", [128, 128])]
    for n, shp in f32in:
        p.dram_in(n, shp, F32)
    for n, shp in [("upt1", [H, I]), ("gatet1", [H, I]), ("dnt1", [I, H])]:
        p.dram_in(n, shp, U8)
    for n, shp, dt in [("xnT", [H, S], BF16), ("qT", [H, S], BF16),
                       ("kT", [H, S], BF16), ("v0", [S, H], BF16),
                       ("psp", [S, S], BF16), ("r1", [S, H], F32),
                       ("x1", [S, H], F32), ("wbounce", [H, I], BF16),
                       ("wbounce2", [I, H], BF16), ("hwup", [H, I], BF16),
                       ("hwgate", [H, I], BF16), ("hwdn", [I, H], BF16),
                       ("t0", [I, S], BF16), ("kT1", [512, S], BF16),
                       ("v1", [S, 512], BF16), ("ctx", [H, S], BF16)]:
        p.dram_tmp(n, shp, dt)
    p.dram_out("out", [S, H], F32)

    p.setup_consts()
    p.ph_wprep("wupts", "hwup", 16, I, 16, 16, "su_ug", "row")
    p.ph_wprep("wgatets", "hwgate", 16, I, 16, 16, "su_ug", "row")
    p.ph_wprep("wdnts", "hwdn", 64, H, 64, 64, "su_dp", "rowcol",
               colrow_name="wsc_dp", bounce_name="wbounce2")
    p.ph_ln("x0")
    p.ph_projT("wqp", "qT", 16, xpos=("qcos", "qsin"))
    p.ph_projT("wkp", "kT", 16, xpos=("kcos", "ksin"))
    p.ph_projN("wv", "v0", H)
    p.ph_scores0(build_consts()["cdec"])
    p.ph_pv0()
    p.ph_ffn_up("hwup", "hwgate", "wsc_up", "wsc_gate")
    p.ph_ffn_down("hwdn", "r1", "x1")
    p.ph_wcast("upt1", "hwup", H, I)
    p.ph_wcast("gatet1", "hwgate", H, I)
    p.ph_wcast("dnt1", "hwdn", I, H)
    p.ph_ln("x1")
    p.ph_projT("wq1", "qT", 16, scale=float(KVC ** -0.5))
    p.ph_kv1()
    p.ph_attn1()
    p.ph_projN("wdense1", "r1", H, src_name="ctx", res_name="x1")
    p.ph_ffn_up("hwup", "hwgate", "sc_up1", "sc_gate1")
    p.ph_ffn_down("hwdn", "r1", "out", coln_name="sc_dn1")
    p.ctx.close()
    nc.finalize()
    return nc


def kernel(**inputs):
    X = np.ascontiguousarray(inputs["L0_attn_X"].astype(np.float32, copy=False))
    prep = host_prep(inputs)
    c = build_consts()
    nc = build_program()
    base = dict(prep)
    base.update(qcos=c["qcos"], qsin=c["qsin"], kcos=c["kcos"], ksin=c["ksin"],
                ident=c["ident"], h128=c["h128"], h16x8=c["h16x8"],
                h64x2=c["h64x2"], utmask=c["utmask"])
    in_maps = []
    for core in range(8):
        m = dict(base)
        m["x0"] = np.ascontiguousarray(X[core])
        in_maps.append(m)
    res = run_bass_kernel_spmd(nc, in_maps, core_ids=list(range(8)))
    return np.stack([r["out"] for r in res.results], axis=0)

if __name__ == "__main__":
    def _xpos_np(x, downscale=False):
        L, d = x.shape
        half = d // 2
        scale_vec = (np.arange(0, d, 2, dtype=np.float64) + 0.4 * d) / (1.4 * d)
        power = np.arange(L, dtype=np.float64) / SCALE_BASE
        scale = scale_vec[None, :] ** power[:, None]
        inv_freq = 1.0 / 10000.0 ** (np.arange(half, dtype=np.float64) / half)
        sinusoid = np.arange(L, dtype=np.float64)[:, None] * inv_freq[None, :]
        sin, cos = np.sin(sinusoid), np.cos(sinusoid)
        if downscale:
            scale = 1.0 / scale
        sin_s = np.repeat(sin * scale, 2, axis=-1)
        cos_s = np.repeat(cos * scale, 2, axis=-1)
        x1, x2 = x[:, 0::2], x[:, 1::2]
        rot = np.stack((-x2, x1), axis=-1).reshape(x.shape)
        return x * cos_s + rot * sin_s

    rng = np.random.default_rng(0)
    Ltest = 256
    X = rng.standard_normal((Ltest, H)).astype(np.float64)
    c = build_consts()
    dq = (GAMMA ** (np.arange(S) % 128)).astype(np.float64)
    for name, ds, tc, ts_, dd in (
        ("q", False, c["qcos"], c["qsin"], dq),
        ("k", True, c["kcos"], c["ksin"], 1.0 / dq),
    ):
        ref = _xpos_np(X, ds)
        Qe, Qo = X[:, 0::2], X[:, 1::2]
        cosv = tc[:, :Ltest].T.astype(np.float64) / dd[:Ltest, None]
        sinv = ts_[:, :Ltest].T.astype(np.float64) / dd[:Ltest, None]
        oute = Qe * cosv - Qo * sinv
        outo = Qo * cosv + Qe * sinv
        got = np.empty_like(X)
        got[:, 0::2], got[:, 1::2] = oute, outo
        err = np.abs(got - ref).max() / (np.abs(ref).max() + 1e-9)
        print(f"xpos {name}: relmax {err:.2e}")

    def fwht(x):
        n = x.shape[-1]
        h = 1
        x = x.copy()
        while h < n:
            x = x.reshape(-1, n // (2 * h), 2, h)
            a, b = x[:, :, 0, :].copy(), x[:, :, 1, :].copy()
            x = np.stack([a + b, a - b], axis=2).reshape(-1, n)
            h *= 2
        return x / math.sqrt(n)

    for (n_feat, ntile, hcross) in ((H, 16, c["h16x8"]), ):
        Wup = rng.standard_normal((I, H)) * 0.02
        su = np.where(rng.random(H) > 0.5, 1.0, -1.0)
        ws = rng.uniform(0.5, 1.5, I)
        x = rng.standard_normal((4, H))
        ref = (fwht(x * su) / 64.0) @ Wup.T * (ws * 64.0)
        wupts = _swiz(np.ascontiguousarray(Wup.T), 16)
        step1 = np.zeros_like(wupts)
        for beta in range(16):
            step1[128*beta:128*(beta+1)] = c["h16x8"].astype(np.float64) @ wupts[128*beta:128*(beta+1)]
        aord = np.zeros_like(step1)
        for alpha in range(16):
            aord[128*alpha:128*(alpha+1)] = step1[alpha::16]
        what = np.zeros_like(aord)
        h128 = c["h128"].astype(np.float64)
        for alpha in range(16):
            blk = h128 @ aord[128*alpha:128*(alpha+1)]
            what[128*alpha:128*(alpha+1)] = blk * su[128*alpha:128*(alpha+1), None] / math.sqrt(H)
        got = (x @ what) * ws
        print("fwht fold up: relmax", np.abs(got - ref).max() / np.abs(ref).max())

    Wdn = rng.standard_normal((H, I)) * 0.02
    su = np.where(rng.random(I) > 0.5, 1.0, -1.0)
    wsd = rng.uniform(0.5, 1.5, H)
    t = rng.standard_normal((4, I))
    ref = (fwht(t * su) / 64.0) @ Wdn.T * (wsd * 64.0)
    wdnts = _swiz(np.ascontiguousarray(Wdn.T), 64)
    step1 = np.zeros_like(wdnts)
    for beta in range(64):
        step1[128*beta:128*(beta+1)] = c["h64x2"].astype(np.float64) @ wdnts[128*beta:128*(beta+1)]
    aord = np.zeros_like(step1)
    for alpha in range(64):
        aord[128*alpha:128*(alpha+1)] = step1[alpha::64]
    what = np.zeros_like(aord)
    h128 = c["h128"].astype(np.float64)
    sdpb = wsd / math.sqrt(I)
    for alpha in range(64):
        blk = h128 @ aord[128*alpha:128*(alpha+1)]
        what[128*alpha:128*(alpha+1)] = blk * su[128*alpha:128*(alpha+1), None] * sdpb[None, :]
    got = t @ what
    print("fwht fold dn: relmax", np.abs(got - ref).max() / np.abs(ref).max())
